# version 02
# baseline (speedup 1.0000x reference)
import numpy as np
import ml_dtypes
from contextlib import ExitStack

import concourse.bass as bass
import concourse.mybir as mybir
from concourse.bass_utils import run_bass_kernel_spmd

F32 = mybir.dt.float32
BF16 = mybir.dt.bfloat16
U32 = mybir.dt.uint32
AF = mybir.ActivationFunctionType
ALU = mybir.AluOpType
AX = mybir.AxisListType

NCORES = 8
BPC = 2
T = 4096
C = 1024
TBS = 512
NTB = T // TBS
CCN = C // 128
CAP = 512
H = 16
DFF = 2730
DFFP = 2816
MCF = DFFP // 128
NBIS = 60
NEG = -1.0e30
ISCALE = float(C) ** -0.5



class Q:
    def __init__(self, prog, name):
        self.prog = prog
        self.name = name
        self.sem = prog.ctx.enter_context(prog.nc.semaphore(f"s_{name}"))
        self.bsem = prog.ctx.enter_context(prog.nc.semaphore(f"sb_{name}"))
        self.ops = []
        self.pad = False

    def emit(self, fn, waits=(), inc=False, sem=None, val=1, w=None, r=()):
        waits = tuple(t for t in waits if t is not None)
        isem = None
        tick = None
        if inc:
            isem = sem if sem is not None else self.sem
            cum = self.prog.sem_cnt.get(isem.name, 0) + val
            self.prog.sem_cnt[isem.name] = cum
            tick = (isem, cum)
        self.ops.append((fn, waits, isem, val, w, tuple(r), self.pad))
        return tick

    def tick(self):
        return (self.sem, self.prog.sem_cnt.get(self.sem.name, 0))

    def play(self, eng):
        satisfied = {}
        prev_w = None
        first = True
        for fn, waits, isem, ival, wname, rnames, pad in self.ops:
            byname = {}
            for (sem, v) in waits:
                if sem is self.sem:
                    continue
                if v > byname.get(sem.name, (None, 0))[1]:
                    byname[sem.name] = (sem, v)
            for sem, v in byname.values():
                if v > satisfied.get(sem.name, 0):
                    eng.wait_ge(sem, v)
                    satisfied[sem.name] = v
            if self.name == "vec" and not first and (
                    pad or (prev_w is not None and
                            (prev_w == wname or prev_w in rnames))):
                padt = self.prog.padt
                eng.tensor_copy(padt[:, 0:1], padt[:, 2:3])
                eng.tensor_copy(padt[:, 1:2], padt[:, 2:3])
            inst = fn(eng)
            if isem is not None:
                inst.then_inc(isem, ival)
            prev_w = wname if self.name == "vec" else None
            first = False


class Ring:

    def __init__(self, prog, name, nslots):
        self.nslots = nslots
        self.sems = [prog.ctx.enter_context(prog.nc.semaphore(f"r_{name}{i}"))
                     for i in range(nslots)]
        self.cnt = [0] * nslots
        self.free_t = [None] * nslots

    def load(self, q, i, fn, waits=()):
        s = i % self.nslots
        self.cnt[s] += 16
        w = list(waits)
        if self.free_t[s] is not None:
            w.append(self.free_t[s])
        q.emit(fn, waits=w, inc=True, sem=self.sems[s], val=16)
        return (self.sems[s], self.cnt[s])

    def consumed(self, i, ticket):
        assert ticket is not None, "Ring.consumed needs an incrementing ticket"
        self.free_t[i % self.nslots] = ticket


class Prog:
    def __init__(self, nc, ctx):
        self.nc = nc
        self.ctx = ctx
        self.sem_cnt = {}
        self.extra_sems = []
        self.ten = Q(self, "ten")
        self.vec = Q(self, "vec")
        self.sca = Q(self, "sca")
        self.syn = Q(self, "syn")
        self.gps = Q(self, "gps")
        self.queues = [self.ten, self.vec, self.sca, self.syn, self.gps]

    def mksem(self, name):
        s = self.ctx.enter_context(self.nc.semaphore(name))
        self.extra_sems.append(s)
        return s

    def barrier(self):
        ticks = [q.emit(lambda e: e.nop(), inc=True, sem=q.bsem)
                 for q in self.queues]
        ticks += [(s, self.sem_cnt[s.name]) for s in self.extra_sems
                  if self.sem_cnt.get(s.name, 0) > 0]
        for q in self.queues:
            q.emit(lambda e: e.nop(), waits=ticks)

    def play(self, block):
        @block.tensor
        def _(eng):
            self.ten.play(eng)

        @block.vector
        def _(eng):
            self.vec.play(eng)

        @block.scalar
        def _(eng):
            self.sca.play(eng)

        @block.sync
        def _(eng):
            self.syn.play(eng)

        @block.gpsimd
        def _(eng):
            self.gps.play(eng)



def build_program(dbg=False):
    nc = bass.Bass("TRN2", target_bir_lowering=False, debug=False,
                   detect_race_conditions=False)
    ctx = ExitStack()
    dt = {}

    def dram(name, shape, dtype, kind="ExternalInput"):
        dt[name] = nc.dram_tensor(name, list(shape), dtype, kind=kind).ap()
        return dt[name]

    dram("x_nat", (BPC, T, C), F32)
    dram("x_tr", (BPC, C, T), F32)
    dram("wk", (128, CCN, C), F32)
    dram("wq", (128, CCN, C), F32)
    dram("wv", (128, CCN, C), F32)
    dram("wp", (128, CCN, C), F32)
    dram("bkc", (128, CCN), F32)
    dram("bqc", (128, CCN), F32)
    dram("bv_row", (1, C), F32)
    dram("bp_row", (1, C), F32)
    dram("wqkv_t", (16, 128, CCN, 128), F32)
    dram("wv2", (128, CCN, C), F32)
    dram("wout_t", (128, CCN, C), F32)
    dram("w1_t", (MCF, 128, CCN, 128), F32)
    dram("w2_t", (MCF, 128, CCN, 128), F32)
    dram("w3_t", (MCF, 128, C), F32)
    dram("g1_row", (1, C), F32)
    dram("g2_row", (1, C), F32)
    dram("wr_row", (1, C), F32)
    dram("cos_t", (NTB, 128, 4, TBS), BF16)
    dram("sin_t", (NTB, 128, 4, TBS), BF16)
    dram("masks", (128, 4, TBS), F32)
    dram("tri", (128, 4, TBS), F32)
    dram("slotiota", (128, 512), F32)
    dram("iotacols", (128, 4), F32)
    dram("identf", (128, 128), F32)
    dram("ones_r_b", (1, 128), BF16)
    dram("ones_r_f", (1, 128), F32)
    dram("ones_c_b", (128, 1), BF16)
    dram("ones_sq_f", (128, 128), F32)

    dram("out", (BPC, T, C), F32, kind="ExternalOutput")
    sk = "ExternalOutput" if dbg else "Internal"
    dram("kt_s", (BPC, NTB, 128, CCN, TBS), BF16, kind=sk)
    dram("qt_s", (BPC, 32, 128, C), BF16, kind=sk)
    dram("v_s", (BPC, T, C), BF16, kind=sk)
    dram("x1_s", (BPC, T, C), BF16, kind=sk)
    if dbg:
        dram("dbg_score", (BPC, 128, 32), F32, kind="ExternalOutput")
        dram("dbg_st", (128, 8 * TBS), F32, kind="ExternalOutput")
        dram("dbg_ot", (128, CCN * TBS), F32, kind="ExternalOutput")
        dram("dbg_selm", (BPC, 128, 32), F32, kind="ExternalOutput")
        dram("dbg_selt", (BPC, 32, 128), F32, kind="ExternalOutput")
        dram("dbg_pos", (BPC, 32, 128), F32, kind="ExternalOutput")
        dram("dbg_offc", (BPC, 32, 1), F32, kind="ExternalOutput")
        dram("dbg_slott", (BPC, 32, 128), F32, kind="ExternalOutput")
        dram("dbg_lo", (BPC, 128, 1), F32, kind="ExternalOutput")
        dram("dbg_bis", (BPC, 128, 4 * NBIS), F32, kind="ExternalOutput")
        dram("dbg_smat", (BPC, 128, 32), F32, kind="ExternalOutput")
        dram("dbg_slot", (BPC, 128, 32), F32, kind="ExternalOutput")
        dram("dbg_rank", (BPC, 128, 4), F32, kind="ExternalOutput")
        dram("dbg_xslot", (BPC, 512, C), F32, kind="ExternalOutput")
        dram("dbg_s2", (BPC, 512, C), F32, kind="ExternalOutput")
        dram("dbg_selout", (BPC, 512, C), F32, kind="ExternalOutput")

    P = Prog(nc, ctx)
    ten, vec, sca, syn, gps = P.ten, P.vec, P.sca, P.syn, P.gps

    def sb(name, shape, dtype):
        return ctx.enter_context(nc.sbuf_tensor("t_" + name, list(shape), dtype))

    ARENA_B = 164 * 1024
    arena = sb("arena", (128, ARENA_B), mybir.dt.uint8)

    class Bump:
        def __init__(self, base):
            self.off = base

        def maker(self):
            def mk(name, shape, dtyp):
                per = 1
                for s_ in shape[1:]:
                    per *= s_
                per *= mybir.dt.size(dtyp)
                off = (self.off + 63) & ~63
                self.off = off + per
                assert self.off <= ARENA_B, (name, self.off)
                v = arena[0:shape[0], off:off + per].bitcast(dtyp)
                if len(shape) == 3:
                    v = v.rearrange("p (a f) -> p a f", a=shape[1])
                return v
            return mk

    identf = sb("identf", (128, 128), F32)
    ones_r_b = sb("ones_r_b", (1, 128), BF16)
    ones_r_f = sb("ones_r_f", (1, 128), F32)
    ones_c_b = sb("ones_c_b", (128, 1), BF16)
    ones_sq_f = sb("ones_sq_f", (128, 128), F32)
    iotacols = sb("iotacols", (128, 4), F32)
    bkc = sb("bkc", (128, CCN), F32)
    bqc = sb("bqc", (128, CCN), F32)
    bvr = sb("bvr", (1, C), BF16)
    bpr = sb("bpr", (1, C), BF16)
    grow = sb("grow", (1, 3 * C), BF16)
    g1rep = sb("g1rep", (128, C), BF16)
    g2rep = sb("g2rep", (128, C), BF16)
    wrrep = sb("wrrep", (128, C), BF16)
    scoremat = sb("scoremat", (128, 32), F32)
    smat = sb("smat", (128, 32), F32)
    scb = sb("scb", (128, 32), BF16)
    ssb = sb("ssb", (128, 32), BF16)
    locol = sb("locol", (128, 1), F32)
    hicol = sb("hicol", (128, 1), F32)
    midA = sb("midA", (128, 1), F32)
    midB = sb("midB", (128, 1), F32)
    cmp32 = sb("cmp32", (128, 32), F32)
    cntcol = sb("cntcol", (128, 1), F32)
    cge = sb("cge", (128, 1), U32)
    clt = sb("clt", (128, 1), U32)
    selm = sb("selm", (128, 32), F32)
    selt = sb("selt", (32, 128), F32)
    scr32 = sb("scr32", (32, 128), F32)
    pret = [sb(f"pret{i}", (32, 128), F32) for i in range(2)]
    post_t = sb("post_t", (32, 128), F32)
    slott = sb("slott", (32, 128), F32)
    nselu = sb("nselu", (32, 128), U32)
    c512t = sb("c512t", (32, 128), F32)
    rt32 = sb("rt32", (32, 32), F32)
    rr32 = sb("rr32", (32, 32), F32)
    er32 = [sb(f"er32_{i}", (1, 32), F32) for i in range(2)]
    offc = sb("offc", (32, 1), F32)
    slotc = sb("slotc", (128, 32), F32)
    epsln = sb("epsln", (128, 1), F32)
    epsln4 = sb("epsln4", (128, 1), F32)
    epsrms = sb("epsrms", (128, 1), F32)
    b1col = sb("b1col", (128, 1), F32)
    sccol = sb("sccol", (128, 4), F32)
    sscol = sb("sscol", (128, 4), F32)
    nsscol = sb("nsscol", (128, 4), F32)
    rankc = sb("rankc", (128, 4), F32)
    rankrep = sb("rankrep", (128, 512), F32)
    minn = [sb(f"minn{i}", (128, 512), BF16) for i in range(4)]
    stats = sb("stats", (128, 4 * 12), F32)
    mvt = sb("mvt", (128, 4 * 2), F32)
    lncol = sb("lncol", (128, 16), F32)
    abscol = sb("abscol", (128, 1), F32)
    padt = sb("padt", (128, 4), F32)
    P.padt = padt
    rcall = sb("rcall", (128, 32), F32)
    rcflat = sb("rcflat", (1, 1024), F32)

    ps = [ctx.enter_context(nc.psum_tensor(f"ps{i}", (128, TBS), F32))
          for i in range(8)]
    ps_free = [None] * 8

    def mm_group(psum_ap, pairs, waits, bank, skip_check=False):
        n = len(pairs)
        tmm = None
        for i, (lhsT, rhs) in enumerate(pairs):
            w = ()
            if i == 0:
                w = list(waits)
                if bank is not None:
                    w.append(ps_free[bank])
            tmm = ten.emit(
                lambda e, lhsT=lhsT, rhs=rhs, i=i, n=n: e.matmul(
                    psum_ap, lhsT, rhs, start=(i == 0), stop=(i == n - 1),
                    skip_group_check=skip_check),
                waits=w, inc=(i == n - 1))
        return tmm

    def ld(q, dst, src, waits=()):
        return q.emit(lambda e, d=dst, s=src: e.dma_start(out=d, in_=s),
                      waits=waits, inc=True, val=16)

    t_c = []
    for dst, src in ((identf, "identf"), (ones_r_b, "ones_r_b"),
                     (ones_r_f, "ones_r_f"), (ones_c_b, "ones_c_b"),
                     (ones_sq_f, "ones_sq_f"),
                     (iotacols, "iotacols"), (bkc, "bkc"), (bqc, "bqc")):
        t_c.append(ld(syn, dst[:], dt[src]))
    for dst, src in ((bvr[:], "bv_row"), (bpr[:], "bp_row"),
                     (grow[:, 0:C], "g1_row"), (grow[:, C:2 * C], "g2_row"),
                     (grow[:, 2 * C:3 * C], "wr_row")):
        t_c.append(ld(gps, dst, dt[src]))
    t_eps = None
    vec.emit(lambda e: e.memset(padt[:], 0.0))
    for tile, v in ((epsln, 1e-5), (epsln4, 2.5e-6), (epsrms, 1e-6),
                    (b1col, 1.0), (c512t, 512.0)):
        t_eps = vec.emit(lambda e, t=tile, vv=v: e.memset(t[:], vv), inc=True)
    t_grep = None
    for gi, gdst in enumerate((g1rep, g2rep, wrrep)):
        tmm0 = mm_group(ps[0][:], [(ones_r_b[:],
                                    grow[:, gi * C:gi * C + 512])],
                        list(t_c), 0)
        tmm1 = mm_group(ps[1][:], [(ones_r_b[:],
                                    grow[:, gi * C + 512:(gi + 1) * C])],
                        list(t_c), 1)
        t_grep = vec.emit(lambda e, d=gdst: (
            e.tensor_copy(d[:, 0:512], ps[0][:]),
            e.tensor_copy(d[:, 512:1024], ps[1][:]))[-1],
            waits=[tmm0, tmm1], inc=True, w=f"grep{gi}")
        ps_free[0] = t_grep
        ps_free[1] = t_grep
    P.barrier()

    rxs = Ring(P, "xs", 2)
    rcs = Ring(P, "cs", 2)
    rki = Ring(P, "ki", 2)
    rqj = Ring(P, "qj", 4)
    rvj = Ring(P, "vj", 4)
    rxn = Ring(P, "xn", 4)
    rx1 = Ring(P, "x1r", 4)
    rwq = Ring(P, "wqk", 2)
    rwb = Ring(P, "wb", 2)
    rw3b = Ring(P, "w3b", 2)
    rwv = Ring(P, "wv2", 2)
    ktq_sems = [P.mksem(f"ktq{i}") for i in range(2)]
    v_sems = [P.mksem(f"vst{i}") for i in range(4)]
    x1_sems = [P.mksem(f"x1s{i}") for i in range(4)]
    u_sems = [P.mksem(f"ust{i}") for i in range(4)]
    scat_sems = [P.mksem(f"scat{i}") for i in range(2)]

    for b in range(BPC):
        P.barrier()
        if True:
            bump_a = Bump(0)
            asb = bump_a.maker()
            wp_sb = asb("wp", (128, CCN * C), BF16)
            t_wp = ld(gps, wp_sb[:], dt["wp"].rearrange("p a b -> p (a b)"))

            if True:
                psb = Bump(bump_a.off).maker()
                wk_sb = psb("wk", (128, CCN * C), BF16)
                wq_sb = psb("wq", (128, CCN * C), BF16)
                wv_sb = psb("wv", (128, CCN * C), BF16)
                xs = [psb(f"xs{i}", (128, CCN * TBS), BF16) for i in range(2)]
                cs = [psb(f"cs{i}", (128, 4 * TBS), BF16) for i in range(2)]
                sn = [psb(f"sn{i}", (128, 4 * TBS), BF16) for i in range(2)]
                kraw = psb("kraw", (128, CCN * TBS), BF16)
                qraw = psb("qraw", (128, CCN * TBS), BF16)
                mrope = [psb(f"mrope{i}", (128, TBS), BF16) for i in range(4)]
                ktf = psb("ktf", (128, CCN * TBS), BF16)
                qtf = psb("qtf", (128, CCN * TBS), BF16)
                vev = [psb(f"vev{i}", (128, TBS), BF16) for i in range(4)]


                t_wA = [t_wp]
                for dst, srcn in ((wk_sb, "wk"), (wq_sb, "wq"),
                                  (wv_sb, "wv")):
                    t_wA.append(ld(gps, dst[:],
                                   dt[srcn].rearrange("p a b -> p (a b)")))

                t_rope_prev = None
                t_store_prev = None
                t_vev_cons = [None] * 4

                for tb in range(NTB):
                    sl = tb % 2
                    t_x = rxs.load(gps, tb + b * NTB, lambda e, tb=tb, sl=sl, b=b:
                                   e.dma_start(
                        out=xs[sl][:].rearrange("p (a f) -> p a f", a=CCN),
                        in_=dt["x_tr"][b, :, tb * TBS:(tb + 1) * TBS]
                        .rearrange("(a p) f -> p a f", p=128)))
                    t_co = rcs.load(syn, 2 * tb + b, lambda e, tb=tb, sl=sl:
                                    e.dma_start(
                        out=cs[sl][:].rearrange("p (a f) -> p a f", a=4),
                        in_=dt["cos_t"][tb]))
                    t_si = rcs.load(syn, 2 * tb + 1 + b, lambda e, tb=tb, sl=sl:
                                    e.dma_start(
                        out=sn[sl][:].rearrange("p (a f) -> p a f", a=4),
                        in_=dt["sin_t"][tb]))

                    ev_kq = []
                    for which, wsb, braw, bias in ((0, wk_sb, kraw, bkc),
                                                   (1, wq_sb, qraw, bqc)):
                        for oc in range(CCN):
                            bank = (which * CCN + oc) % 2
                            pairs = [(wsb[:, cc * C + oc * 128:
                                          cc * C + (oc + 1) * 128],
                                      xs[sl][:, cc * TBS:(cc + 1) * TBS])
                                     for cc in range(CCN)]
                            tmm = mm_group(ps[bank][:], pairs,
                                           [t_x] + (t_wA if tb == 0 else []),
                                           bank)
                            tev = sca.emit(lambda e, braw=braw, oc=oc,
                                           bias=bias, bank=bank:
                                           e.activation(
                                out=braw[:, oc * TBS:(oc + 1) * TBS],
                                in_=ps[bank][:], func=AF.Identity,
                                bias=bias[:, oc:oc + 1]),
                                waits=[tmm, t_rope_prev, t_c[-1]],
                                inc=True)
                            ps_free[bank] = tev
                            ev_kq.append(tev)

                    t_rope = []
                    for src_raw, bout in ((kraw, ktf), (qraw, qtf)):
                        for p4 in range(4):
                            z0 = src_raw[:, p4 * TBS:(p4 + 1) * TBS]
                            z1 = src_raw[:, (p4 + 4) * TBS:(p4 + 5) * TBS]
                            cs_ = cs[sl][:, p4 * TBS:(p4 + 1) * TBS]
                            sn_ = sn[sl][:, p4 * TBS:(p4 + 1) * TBS]
                            w0 = [t_co, t_si, ev_kq[-1], t_store_prev]
                            vec.emit(lambda e, x=cs_, y=z0: e.tensor_mul(
                                mrope[0][:], x, y), waits=w0, w="m0")
                            vec.emit(lambda e, x=sn_, y=z1: e.tensor_mul(
                                mrope[1][:], x, y), w="m1")
                            vec.emit(lambda e, x=sn_, y=z0: e.tensor_mul(
                                mrope[2][:], x, y), w="m2")
                            vec.emit(lambda e, x=cs_, y=z1: e.tensor_mul(
                                mrope[3][:], x, y), w="m3")
                            vec.emit(lambda e,
                                     o=bout[:, p4 * TBS:(p4 + 1) * TBS]:
                                     e.tensor_add(o, mrope[0][:],
                                                  mrope[1][:]),
                                     w="r0", r=("m0", "m1"))
                            t_r = vec.emit(
                                lambda e,
                                o=bout[:, (p4 + 4) * TBS:(p4 + 5) * TBS]:
                                e.tensor_sub(o, mrope[3][:], mrope[2][:]),
                                inc=True, w="r1", r=("m2", "m3"))
                            t_rope.append(t_r)
                    t_rope_prev = t_rope[-1]
                    rcs.consumed(2 * tb + b, t_rope_prev)
                    rcs.consumed(2 * tb + 1 + b, t_rope_prev)

                    ksem = ktq_sems[tb % 2]
                    st_kq = []
                    for oc in range(CCN):
                        st_kq.append(syn.emit(
                            lambda e, oc=oc, tb=tb, b=b: e.dma_start(
                                out=dt["kt_s"][b, tb, :, oc, :],
                                in_=ktf[:, oc * TBS:(oc + 1) * TBS]),
                            waits=[t_rope[oc % 4]], inc=True, val=16,
                            sem=ksem))
                        st_kq.append(syn.emit(
                            lambda e, oc=oc, tb=tb, b=b: e.dma_start(
                                out=dt["qt_s"][b, 4 * tb:4 * tb + 4, :,
                                               oc * 128:(oc + 1) * 128]
                                .rearrange("j p f -> p j f"),
                                in_=qtf[:, oc * TBS:(oc + 1) * TBS]
                                .rearrange("p (j f) -> p j f", j=4)),
                            waits=[t_rope[4 + (oc % 4)]], inc=True, val=16,
                            sem=ksem))
                    t_store_prev = st_kq[-1]

                    for mh in range(2):
                        for tsub in range(4):
                            bank = 2 + (mh * 4 + tsub) % 2
                            vi = (mh * 4 + tsub) % 4
                            ten.emit(lambda e, mh=mh, bank=bank: e.matmul(
                                ps[bank][:], ones_r_b[:],
                                bvr[:, mh * 512:(mh + 1) * 512],
                                start=True, stop=False),
                                waits=[t_x, ps_free[bank]] +
                                ([t_wA[3], t_c[-1]] if tb == 0 else []))
                            for cc in range(CCN):
                                tmm = ten.emit(lambda e, cc=cc, mh=mh,
                                               tsub=tsub, bank=bank, sl=sl:
                                               e.matmul(
                                    ps[bank][:],
                                    xs[sl][:, cc * TBS + tsub * 128:
                                           cc * TBS + (tsub + 1) * 128],
                                    wv_sb[:, cc * C + mh * 512:
                                          cc * C + (mh + 1) * 512],
                                    start=False, stop=(cc == CCN - 1)),
                                    inc=(cc == CCN - 1))
                            tev = sca.emit(lambda e, vi=vi, bank=bank:
                                           e.activation(
                                out=vev[vi][:], in_=ps[bank][:],
                                func=AF.Identity),
                                waits=[tmm, t_vev_cons[vi]], inc=True)
                            ps_free[bank] = tev
                            t_st = syn.emit(
                                lambda e, tb=tb, tsub=tsub, mh=mh, vi=vi, b=b:
                                e.dma_start(
                                    out=dt["v_s"][b,
                                                  tb * TBS + tsub * 128:
                                                  tb * TBS +
                                                  (tsub + 1) * 128,
                                                  mh * 512:(mh + 1) * 512],
                                    in_=vev[vi][:]),
                                waits=[tev], inc=True, val=16,
                                sem=v_sems[vi])
                            t_vev_cons[vi] = t_st
                    rxs.consumed(tb + b * NTB, ten.tick())

                t_scr_done = (
                    [(s, P.sem_cnt[s.name]) for s in ktq_sems] +
                    [(s, P.sem_cnt[s.name]) for s in v_sems])

            P.barrier()
            if True:
                atsb = Bump(bump_a.off).maker()
                masks_a = atsb("masks", (128, 4, TBS), F32)
                st_sb = atsb("st", (128, 32 * TBS), BF16)
                kis = [atsb(f"kis{i}", (128, CCN * TBS), BF16)
                       for i in range(2)]
                qjs = [atsb(f"qjs{i}", (128, CCN * 128), BF16)
                       for i in range(4)]
                vjs = [atsb(f"vjs{i}", (128, C), BF16) for i in range(4)]
                mske = [atsb(f"mske{i}", (128, TBS), F32) for i in range(2)]
                esb = [atsb(f"esb{i}", (128, TBS), F32) for i in range(2)]
                ot_sb = atsb("ot", (128, CCN * TBS), BF16)
                xnt = [atsb(f"xnt{i}", (128, C), F32) for i in range(4)]
                yt = [atsb(f"yt{i}", (128, C), F32) for i in range(4)]
                x1t = [atsb(f"x1t{i}", (128, C), BF16) for i in range(4)]
                ut = [atsb(f"ut{i}", (128, C), F32) for i in range(4)]
                scrj = atsb("scrj", (128, C), BF16)

                t_msk = ld(syn, masks_a[:], dt["masks"])

                t_mske_cons = [None] * 2
                t_yt_cons = [None] * 4
                t_x1_cons = [[] for _ in range(4)]
                t_ut_cons = [None] * 4
                t_mvt_cons = [None] * 4

                for I in range(NTB):
                    ksl = I % 2
                    t_ki = rki.load(syn, I, lambda e, I=I, ksl=ksl, b=b:
                                    e.dma_start(
                        out=kis[ksl][:].rearrange("p (a f) -> p a f", a=CCN),
                        in_=dt["kt_s"][b, I]), waits=t_scr_done)
                    NJ = 4 * I + 4
                    t_st = []
                    for j in range(NJ):
                        qsl = j % 4
                        t_qj = rqj.load(syn, j, lambda e, j=j, qsl=qsl, b=b:
                                        e.dma_start(
                            out=qjs[qsl][:]
                            .rearrange("p (a f) -> p a f", a=CCN),
                            in_=dt["qt_s"][b, j]
                            .rearrange("p (a f) -> p a f", a=CCN)),
                            waits=t_scr_done)
                        bank = j % 2
                        pairs = [(qjs[qsl][:, cc * 128:(cc + 1) * 128],
                                  kis[ksl][:, cc * TBS:(cc + 1) * TBS])
                                 for cc in range(CCN)]
                        tmm = mm_group(ps[bank][:], pairs, [t_qj, t_ki],
                                       bank)
                        rqj.consumed(j, tmm)
                        esl = j % 2
                        if j >= 4 * I:
                            a = j - 4 * I
                            t_m = vec.emit(lambda e, a=a, bank=bank, esl=esl:
                                           e.scalar_tensor_tensor(
                                out=mske[esl][:], in0=ps[bank][:],
                                scalar=ISCALE, in1=masks_a[:, a, :],
                                op0=ALU.mult, op1=ALU.add),
                                waits=[tmm, t_mske_cons[esl], t_msk],
                                inc=True, w=f"mske{esl}")
                            ps_free[bank] = t_m
                            t_e = sca.emit(lambda e, esl=esl: e.activation(
                                out=esb[esl][:], in_=mske[esl][:],
                                func=AF.Exp), waits=[t_m], inc=True)
                            t_mske_cons[esl] = t_e
                        else:
                            t_e = sca.emit(lambda e, bank=bank, esl=esl:
                                           e.activation(
                                out=esb[esl][:], in_=ps[bank][:],
                                func=AF.Exp, scale=ISCALE),
                                waits=[tmm], inc=True)
                            ps_free[bank] = t_e
                        t_l = sca.emit(lambda e, j=j, esl=esl: e.activation(
                            out=st_sb[:, j * TBS:(j + 1) * TBS],
                            in_=esb[esl][:], func=AF.Ln, bias=b1col[:]),
                            inc=True)
                        t_st.append(t_l)

                    rki.consumed(I, ten.tick())
                    if dbg and b == 0 and I == 1:
                        gps.emit(lambda e: e.dma_start(
                            out=dt["dbg_st"],
                            in_=st_sb[:, 0:8 * TBS]),
                            waits=[t_st[-1]], inc=True, val=16)

                    for j in range(NJ):
                        vsl = j % 4
                        t_vj = rvj.load(syn, j, lambda e, j=j, vsl=vsl, b=b:
                                        e.dma_start(
                            out=vjs[vsl][:],
                            in_=dt["v_s"][b, j * 128:(j + 1) * 128, :]),
                            waits=t_scr_done)
                        for d in range(CCN):
                            w0 = []
                            if j == 0:
                                w0 = [t_vj, t_st[-1], ps_free[d]]
                            elif d == 0:
                                w0 = [t_vj, t_st[j]]
                            tmm = ten.emit(lambda e, j=j, d=d, vsl=vsl, NJ=NJ:
                                           e.matmul(
                                ps[d][:],
                                vjs[vsl][:, d * 128:(d + 1) * 128],
                                st_sb[:, j * TBS:(j + 1) * TBS],
                                start=(j == 0), stop=(j == NJ - 1)),
                                waits=w0, inc=(d == CCN - 1))
                        rvj.consumed(j, tmm)
                    t_ot_mm = ten.tick()
                    t_ote = []
                    for d in range(CCN):
                        t_ev = sca.emit(lambda e, d=d: e.activation(
                            out=ot_sb[:, d * TBS:(d + 1) * TBS],
                            in_=ps[d][:], func=AF.Identity),
                            waits=[t_ot_mm], inc=True)
                        ps_free[d] = t_ev
                        t_ote.append(t_ev)

                    if dbg and b == 0 and I == 1:
                        gps.emit(lambda e: e.dma_start(
                            out=dt["dbg_ot"], in_=ot_sb[:]),
                            waits=[t_ote[-1]], inc=True, val=16)
                    t_y = [None] * 4
                    for isub in range(4):
                        gidx = I * 4 + isub
                        t_xn = rxn.load(syn, gidx,
                                        lambda e, gidx=gidx, isub=isub, b=b:
                                        e.dma_start(
                            out=xnt[isub][:],
                            in_=dt["x_nat"][b,
                                            gidx * 128:(gidx + 1) * 128,
                                            :]))
                        for half in range(2):
                            bank = (isub * 2 + half) % 2
                            ten.emit(lambda e, half=half, bank=bank:
                                     e.matmul(
                                ps[bank][:], ones_r_b[:],
                                bpr[:, half * 512:(half + 1) * 512],
                                start=True, stop=False),
                                waits=[t_ote[-1], ps_free[bank], t_wp])
                            for mc in range(CCN):
                                tmm = ten.emit(lambda e, mc=mc, isub=isub,
                                               half=half, bank=bank:
                                               e.matmul(
                                    ps[bank][:],
                                    ot_sb[:, mc * TBS + isub * 128:
                                          mc * TBS + (isub + 1) * 128],
                                    wp_sb[:, mc * C + half * 512:
                                          mc * C + (half + 1) * 512],
                                    start=False, stop=(mc == CCN - 1)),
                                    inc=(mc == CCN - 1))
                            t_yh = vec.emit(lambda e, isub=isub, half=half,
                                            bank=bank:
                                            e.scalar_tensor_tensor(
                                out=yt[isub][:,
                                             half * 512:(half + 1) * 512],
                                in0=ps[bank][:], scalar=1.0,
                                in1=xnt[isub][:,
                                              half * 512:(half + 1) * 512],
                                op0=ALU.mult, op1=ALU.add),
                                waits=[tmm, t_xn, t_yt_cons[isub]],
                                inc=True, w=f"y{isub}h{half}")
                            ps_free[bank] = t_yh
                            t_y[isub] = t_yh
                        rxn.consumed(gidx, t_y[isub])

                    for isub in range(4):
                        vec.emit(lambda e, i=isub: e.bn_stats(
                            stats[:, i * 12:i * 12 + 6], yt[i][:, 0:512]),
                            waits=[t_y[isub]], w=f"bnsa{isub}",
                            r=(f"y{isub}h0", f"y{isub}h1"))
                        vec.emit(lambda e, i=isub: e.bn_stats(
                            stats[:, i * 12 + 6:i * 12 + 12],
                            yt[i][:, 512:1024]), w=f"bnsb{isub}")
                    t_bn = [None] * 4
                    for isub in range(4):
                        t_bn[isub] = vec.emit(lambda e, i=isub: e.bn_aggr(
                            mvt[:, i * 2:i * 2 + 2],
                            stats[:, i * 12:i * 12 + 12]),
                            waits=[t_mvt_cons[isub]],
                            inc=True, w=f"mv{isub}",
                            r=(f"bnsa{isub}", f"bnsb{isub}"))
                    for isub in range(4):
                        gidx = I * 4 + isub
                        rcol = lncol[:, isub * 4:isub * 4 + 1]
                        nmr = lncol[:, isub * 4 + 1:isub * 4 + 2]
                        t2c = lncol[:, isub * 4 + 2:isub * 4 + 3]
                        lnv = lncol[:, isub * 4 + 3:isub * 4 + 4]
                        sca.emit(lambda e, i=isub, lnv=lnv: e.activation(
                            out=lnv, in_=mvt[:, i * 2 + 1:i * 2 + 2],
                            func=AF.Ln, bias=epsln[:]),
                            waits=[t_bn[isub]])
                        t_r = sca.emit(lambda e, rcol=rcol, lnv=lnv:
                                       e.activation(
                            out=rcol, in_=lnv, func=AF.Exp, scale=-0.5),
                            inc=True)
                        t_mvt_cons[isub] = t_r
                        vec.emit(lambda e, t2c=t2c, rcol=rcol: e.tensor_mul(
                            t2c, rcol, rcol), waits=[t_r], w=f"t2{isub}")
                        vec.emit(lambda e, i=isub, t2c=t2c: e.tensor_mul(
                            t2c, mvt[:, i * 2 + 1:i * 2 + 2], t2c),
                            w=f"t2b{isub}", r=(f"t2{isub}",))
                        vec.emit(lambda e, i=isub, nmr=nmr, rcol=rcol:
                                 e.tensor_mul(
                            nmr, mvt[:, i * 2:i * 2 + 1], rcol),
                            w=f"nm{isub}")
                        t_nmr = vec.emit(lambda e, nmr=nmr:
                                         e.tensor_scalar_mul(
                            nmr, nmr, -1.0), inc=True, w=f"nm2{isub}",
                            r=(f"nm{isub}",))
                        sca.emit(lambda e, t2c=t2c, lnv=lnv: e.activation(
                            out=lnv, in_=t2c, func=AF.Ln, bias=epsln4[:]),
                            waits=[t_nmr])
                        t_s = sca.emit(lambda e, gidx=gidx, lnv=lnv:
                                       e.activation(
                            out=smat[:, gidx:gidx + 1], in_=lnv,
                            func=AF.Exp, scale=-0.5), inc=True)
                        t_x1 = sca.emit(lambda e, i=isub, rcol=rcol,
                                        nmr=nmr: e.activation(
                            out=x1t[i][:], in_=yt[i][:], func=AF.Identity,
                            scale=rcol, bias=nmr),
                            waits=list(t_x1_cons[isub]), inc=True)
                        t_u = sca.emit(lambda e, i=isub, gidx=gidx:
                                       e.activation(
                            out=ut[i][:], in_=x1t[i][:], func=AF.Identity,
                            scale=smat[:, gidx:gidx + 1]),
                            waits=[t_ut_cons[isub]], inc=True)
                        t_yt_cons[isub] = t_x1
                        t_sc = vec.emit(lambda e, i=isub, gidx=gidx:
                                        e.scalar_tensor_tensor(
                            out=scrj[:], in0=x1t[i][:], scalar=1.0,
                            in1=wrrep[:], op0=ALU.mult, op1=ALU.mult,
                            accum_out=scoremat[:, gidx:gidx + 1]),
                            waits=[t_x1, t_grep, t_eps], inc=True, w="scr")
                        t_w1s = syn.emit(lambda e, i=isub, gidx=gidx, b=b:
                                         e.dma_start(
                            out=dt["x1_s"][b,
                                           gidx * 128:(gidx + 1) * 128, :],
                            in_=x1t[i][:]), waits=[t_x1], inc=True, val=16,
                            sem=x1_sems[isub])
                        t_w2s = syn.emit(lambda e, i=isub, gidx=gidx, b=b:
                                         e.dma_start(
                            out=dt["out"][b,
                                          gidx * 128:(gidx + 1) * 128, :],
                            in_=ut[i][:]), waits=[t_u], inc=True, val=16,
                            sem=u_sems[isub])
                        t_x1_cons[isub] = [t_sc, t_w1s]
                        t_ut_cons[isub] = t_w2s

                t_x1_done = [(s, P.sem_cnt[s.name]) for s in x1_sems]
                t_u_done = [(s, P.sem_cnt[s.name]) for s in u_sems]
                t_score_done = vec.tick()
                t_smat_done = sca.tick()

        P.barrier()
        if True:
            bump_f = Bump(0)
            fsb = bump_f.maker()
            xslot = [fsb(f"xslot{a}", (128, C), BF16) for a in range(4)]
            bisdmp = fsb("bisdmp", (128, 4 * NBIS), F32)

            vec.pad = True
            t_abs = vec.emit(lambda e: e.tensor_reduce(
                abscol[:], scoremat[:], AX.X, ALU.add,
                apply_absolute_value=True),
                waits=[t_score_done], inc=True, w="abs")
            t_bnd = mm_group(ps[7][:, 0:1], [(ones_sq_f[:], abscol[:])],
                             [t_abs], 7)
            vec.emit(lambda e: e.tensor_scalar_mul(
                locol[:], ps[7][:, 0:1], -1.0), waits=[t_bnd], w="lo")
            t_bis = vec.emit(lambda e: e.tensor_copy(hicol[:],
                                                     ps[7][:, 0:1]),
                             inc=True, w="hi")
            ps_free[7] = t_bis
            for it in range(NBIS):
                vec.emit(lambda e: e.tensor_add(midA[:], locol[:],
                                                hicol[:]),
                         waits=[t_bis], w="midA", r=("lo", "hi"))
                vec.emit(lambda e: e.tensor_scalar_mul(midB[:], midA[:],
                                                       0.5),
                         w="midB", r=("midA",))
                vec.emit(lambda e: e.tensor_single_scalar(
                    cmp32[:], scoremat[:], midB[:], ALU.is_gt),
                    w="cmp", r=("midB",))
                t_cc = vec.emit(lambda e: e.tensor_reduce(
                    cntcol[:], cmp32[:], AX.X, ALU.add),
                    inc=True, w="cnt", r=("cmp",))
                t_tot = mm_group(ps[7][:, 0:1],
                                 [(ones_sq_f[:], cntcol[:])], [t_cc], 7)
                vec.emit(lambda e: e.tensor_single_scalar(
                    cge[:], ps[7][:, 0:1], 512.0, ALU.is_ge),
                    waits=[t_tot], w="cge")
                vec.emit(lambda e: e.tensor_single_scalar(
                    clt[:], ps[7][:, 0:1], 512.0, ALU.is_lt), w="clt")
                vec.emit(lambda e: e.select(
                    locol[:], cge[:], midB[:], locol[:]),
                    w="lo", r=("cge",))
                t_bis = vec.emit(lambda e: e.select(
                    hicol[:], clt[:], midB[:], hicol[:]),
                    inc=True, w="hi", r=("clt",))
                ps_free[7] = t_bis
                if dbg:
                    vec.emit(lambda e, it=it: e.tensor_copy(
                        bisdmp[:, 4 * it:4 * it + 1], cntcol[:]), w="db0")
                    vec.emit(lambda e, it=it: e.tensor_copy(
                        bisdmp[:, 4 * it + 1:4 * it + 2], ps[7][:, 0:1]),
                        w="db1")
                    vec.emit(lambda e, it=it: e.tensor_copy(
                        bisdmp[:, 4 * it + 2:4 * it + 3], locol[:]),
                        w="db2")
                    t_bis = vec.emit(lambda e, it=it: e.tensor_copy(
                        bisdmp[:, 4 * it + 3:4 * it + 4], hicol[:]),
                        inc=True, w="db3")

            t_sel = vec.emit(lambda e: e.tensor_single_scalar(
                selm[:], scoremat[:], locol[:], ALU.is_gt),
                inc=True, w="selm", r=("lo",))
            if dbg:
                gps.emit(lambda e, b=b: e.dma_start(
                    out=dt["dbg_bis"][b], in_=bisdmp[:]),
                    waits=[t_sel], inc=True, val=16)
            if dbg:
                gps.emit(lambda e, b=b: e.dma_start(
                    out=dt["dbg_selm"][b], in_=selm[:]),
                    waits=[t_sel], inc=True, val=16)
                gps.emit(lambda e, b=b: e.dma_start(
                    out=dt["dbg_lo"][b], in_=locol[:]),
                    waits=[t_sel], inc=True, val=16)
            t_tr1 = ten.emit(lambda e: e.transpose(
                ps[6][0:32, 0:128], selm[:], identf[:]),
                waits=[t_sel, ps_free[6]], inc=True)
            t_selt = vec.emit(lambda e: e.tensor_copy(
                selt[:], ps[6][0:32, 0:128]), waits=[t_tr1], inc=True,
                w="selt")
            ps_free[6] = t_selt
            if dbg:
                gps.emit(lambda e, b=b: e.dma_start(
                    out=dt["dbg_selt"][b], in_=selt[:]),
                    waits=[t_selt], inc=True, val=16)
            cur, nxt = pret[0], pret[1]
            t_pf = vec.emit(lambda e, cc_=cur: e.tensor_copy(cc_[:], selt[:]),
                            waits=[t_selt], inc=True, w="pf1")
            for sh in (1, 2, 4, 8, 16, 32, 64):
                vec.emit(lambda e, sh=sh, cc_=cur, nn=nxt: e.tensor_copy(
                    nn[:, 0:sh], cc_[:, 0:sh]), w="pfc", r=("pf1", "pfa"))
                t_pf = vec.emit(lambda e, sh=sh, cc_=cur, nn=nxt:
                                e.tensor_add(
                    nn[:, sh:128], cc_[:, sh:128], cc_[:, 0:128 - sh]),
                    inc=True, w="pfa", r=("pfc",))
                cur, nxt = nxt, cur
            t_pos = vec.emit(lambda e, cc_=cur: e.tensor_sub(
                post_t[:], cc_[:], selt[:]), waits=[t_pf], inc=True,
                w="pos", r=("pfa",))
            if dbg:
                gps.emit(lambda e, b=b: e.dma_start(
                    out=dt["dbg_pos"][b], in_=post_t[:]),
                    waits=[t_pos], inc=True, val=16)
            vec.emit(lambda e: e.memset(rt32[:], 0.0), w="rt0")
            vec.emit(lambda e, cc_=cur: e.tensor_copy(
                rt32[:, 0:1], cc_[:, 127:128]), w="rt", r=("rt0",))
            t_rt = vec.emit(lambda e: e.transpose(rr32[:], rt32[:]),
                            inc=True, w="rr", r=("rt",))
            e0, e1 = er32[0], er32[1]
            vec.emit(lambda e, a=e0: e.memset(a[:], 0.0), w="e0")
            vec.emit(lambda e, a=e0: e.tensor_copy(
                a[:, 1:32], rr32[0:1, 0:31]), waits=[t_rt], w="e0b",
                r=("rr", "e0"))
            for sh in (1, 2, 4, 8, 16):
                vec.emit(lambda e, sh=sh, a=e0, bb=e1: e.tensor_copy(
                    bb[:, 0:sh], a[:, 0:sh]), w="epc", r=("e0b", "epa"))
                vec.emit(lambda e, sh=sh, a=e0, bb=e1: e.tensor_add(
                    bb[:, sh:32], a[:, sh:32], a[:, 0:32 - sh]),
                    w="epa", r=("epc",))
                e0, e1 = e1, e0
            vec.emit(lambda e: e.memset(rr32[:], 0.0), w="rrz",
                     r=("epa",))
            vec.emit(lambda e, a=e0: e.tensor_copy(rr32[0:1, :], a[:]),
                     w="rr2", r=("rrz", "epa"))
            t_off = vec.emit(lambda e: e.transpose(rt32[:], rr32[:]),
                             inc=True, w="off", r=("rr2",))
            vec.emit(lambda e: e.tensor_copy(offc[:], rt32[:, 0:1]),
                     waits=[t_off], w="offc", r=("off",))
            vec.emit(lambda e: e.tensor_single_scalar(
                slott[:], post_t[:], offc[:], ALU.add),
                w="slott", r=("offc", "pos"))
            vec.emit(lambda e: e.tensor_single_scalar(
                nselu[:], selt[:], 0.5, ALU.is_lt), w="nsel")
            t_slot = vec.emit(lambda e: e.select(
                slott[:], nselu[:], c512t[:], slott[:]),
                inc=True, w="slott2", r=("nsel", "slott"))
            if dbg:
                gps.emit(lambda e, b=b: e.dma_start(
                    out=dt["dbg_offc"][b], in_=offc[:]),
                    waits=[t_slot], inc=True, val=16)
                gps.emit(lambda e, b=b: e.dma_start(
                    out=dt["dbg_slott"][b], in_=slott[:]),
                    waits=[t_slot], inc=True, val=16)
            t_tr2 = ten.emit(lambda e: e.transpose(
                ps[6][0:128, 0:32], slott[:], identf[0:32, 0:32]),
                waits=[t_slot, ps_free[6]], inc=True)
            t_slotc = vec.emit(lambda e: e.tensor_copy(
                slotc[:], ps[6][0:128, 0:32]), waits=[t_tr2], inc=True,
                w="slotc")
            ps_free[6] = t_slotc

            def build_CA(CA_list, waits0, mksb):
                t_out = [None] * 4
                srep = mksb("srepscr", (128, 512), F32)
                t_s_cons = None
                pad_sav = vec.pad
                vec.pad = True
                for kk in range(8):
                    t_row = syn.emit(lambda e, kk=kk: e.dma_start(
                        out=rcflat[0:1, 0:512]
                        .rearrange("o (c p) -> o c p", c=4),
                        in_=slott[4 * kk:4 * kk + 4, :]),
                        waits=waits0 + [t_s_cons], inc=True, val=16)
                    t_mm = mm_group(ps[6][:],
                                    [(ones_r_f[:], rcflat[0:1, 0:512])],
                                    [t_row], 6)
                    t_se = vec.emit(lambda e: e.tensor_copy(
                        srep[:], ps[6][:]), waits=[t_mm], inc=True,
                        w="srep")
                    ps_free[6] = t_se
                    t_s_cons = t_mm
                    for a in range(4):
                        t_out[a] = vec.emit(lambda e, a=a, kk=kk:
                                            e.tensor_single_scalar(
                            CA_list[a][:, kk * 512:(kk + 1) * 512],
                            srep[:], iotacols[:, a:a + 1], ALU.is_equal),
                            waits=[t_se], inc=True, w=f"cab{a}",
                            r=("srep",))
                    t_s_cons = t_out[3]
                vec.pad = pad_sav
                return t_out

            if True:
                f1sb = Bump(bump_f.off).maker()
                tri_f = f1sb("tri", (128, 4, TBS), F32)
                slotio = f1sb("slotio", (128, 512), F32)
                CT = [f1sb(f"CT{c}", (128, 512), BF16) for c in range(32)]
                x1c = [f1sb(f"x1c{i}", (128, C), BF16) for i in range(4)]
                gtt = f1sb("gtt", (128, 512), F32)
                eqt = f1sb("eqt", (128, 512), F32)
                eqm = f1sb("eqm", (128, 512), F32)
                screp = f1sb("screp", (128, 512), F32)

                t_tri = ld(syn, tri_f[:], dt["tri"])
                t_sio = ld(syn, slotio[:], dt["slotiota"])
                t_trisio = [t_tri, t_sio]

                t_CT = [None] * 32
                for c in range(32):
                    t_CT[c] = vec.emit(lambda e, c=c:
                                       e.tensor_single_scalar(
                        CT[c][:], slotio[:], slotc[:, c:c + 1],
                        ALU.is_equal),
                        waits=[t_slotc] + t_trisio, inc=True, w=f"ct{c}")

                t_scb = vec.emit(lambda e: e.tensor_copy(scb[:],
                                                         scoremat[:]),
                                 waits=[t_score_done], inc=True, w="scb")
                t_ssb = vec.emit(lambda e: e.tensor_copy(ssb[:], smat[:]),
                                 waits=[t_smat_done], inc=True, w="ssb")

                t_cols = []
                for (srcb, dstcol, tsrc) in ((scb, sccol, t_scb),
                                             (ssb, sscol, t_ssb)):
                    tmm = None
                    for a in range(4):
                        pairs = [(CT[c][:, a * 128:(a + 1) * 128],
                                  srcb[:, c:c + 1]) for c in range(32)]
                        tmm = mm_group(ps[6][:, a:a + 1], pairs,
                                       [tsrc] + t_CT, 6, skip_check=True)
                    t_cv = vec.emit(lambda e, dstcol=dstcol: e.tensor_copy(
                        dstcol[:], ps[6][:, 0:4]), waits=[tmm], inc=True,
                        w="cols")
                    ps_free[6] = t_cv
                    t_cols.append(t_cv)
                t_nss = vec.emit(lambda e: e.tensor_scalar_mul(
                    nsscol[:], sscol[:], -1.0), waits=[t_cols[1]],
                    inc=True, w="nss", r=("cols",))

                def row_replicate(src4, t_src, dstrep, extra_w=()):
                    t1 = vec.emit(lambda e: e.tensor_copy(cmp32[:, 0:4],
                                                          src4[:]),
                                  waits=[t_src] + list(extra_w), inc=True,
                                  w="rrc", r=("cols", "nss"))
                    t2 = ten.emit(lambda e: e.transpose(
                        ps[6][0:32, 0:128], cmp32[:, 0:32], identf[:]),
                        waits=[t1, ps_free[6]], inc=True)
                    t3 = vec.emit(lambda e: e.tensor_copy(
                        scr32[:], ps[6][0:32, 0:128]), waits=[t2],
                        inc=True, w="rrd")
                    ps_free[6] = t3
                    t4 = syn.emit(lambda e: e.dma_start(
                        out=rcflat[0:1, 0:512]
                        .rearrange("o (a p) -> o a p", a=4),
                        in_=scr32[0:4, :]),
                        waits=[t3], inc=True, val=16)
                    t5 = mm_group(ps[6][:],
                                  [(ones_r_f[:], rcflat[0:1, 0:512])],
                                  [t4], 6)
                    t6 = vec.emit(lambda e: e.tensor_copy(dstrep[:],
                                                          ps[6][:]),
                                  waits=[t5], inc=True, w="rre")
                    ps_free[6] = t6
                    return t6

                t_screp = row_replicate(sccol, t_cols[0], screp)
                t_rank = [None] * 4
                for a in range(4):
                    vec.emit(lambda e, a=a: e.tensor_single_scalar(
                        gtt[:], screp[:], sccol[:, a:a + 1], ALU.is_gt),
                        waits=[t_screp, t_cols[0]] + t_trisio, w="gtt",
                        r=("rre",))
                    vec.emit(lambda e, a=a: e.tensor_single_scalar(
                        eqt[:], screp[:], sccol[:, a:a + 1],
                        ALU.is_equal), w="eqt")
                    vec.emit(lambda e, a=a: e.tensor_mul(
                        eqm[:], eqt[:], tri_f[:, a, :]), w="eqm",
                        r=("eqt",))
                    vec.emit(lambda e: e.tensor_add(eqm[:], eqm[:],
                                                    gtt[:]),
                             w="eqm2", r=("eqm", "gtt"))
                    t_rank[a] = vec.emit(lambda e, a=a: e.tensor_reduce(
                        rankc[:, a:a + 1], eqm[:], AX.X, ALU.add),
                        inc=True, w=f"rk{a}", r=("eqm2",))
                t_rkrep = row_replicate(rankc, t_rank[-1], rankrep,
                                        extra_w=t_rank)
                t_minn = [None] * 4
                for a in range(4):
                    t_minn[a] = vec.emit(lambda e, a=a:
                                         e.tensor_single_scalar(
                        minn[a][:], rankrep[:], rankc[:, a:a + 1],
                        ALU.is_ge),
                        waits=[t_rkrep], inc=True, w=f"mi{a}", r=("rre",))
                vec.pad = False

                if dbg:
                    syn.emit(lambda e, b=b: e.dma_start(
                        out=dt["dbg_score"][b], in_=scoremat[:]),
                        waits=[t_score_done], inc=True, val=16)
                    syn.emit(lambda e, b=b: e.dma_start(
                        out=dt["dbg_smat"][b], in_=smat[:]),
                        waits=[t_smat_done], inc=True, val=16)
                    syn.emit(lambda e, b=b: e.dma_start(
                        out=dt["dbg_slot"][b], in_=slotc[:]),
                        waits=[t_slotc], inc=True, val=16)
                    syn.emit(lambda e, b=b: e.dma_start(
                        out=dt["dbg_rank"][b], in_=rankc[:]),
                        waits=t_rank, inc=True, val=16)

                t_xslot = [None] * 4
                for hp in range(2):
                    tmm_l = [[None, None], [None, None]]
                    for c in range(32):
                        csl = c % 4
                        t_xc = rx1.load(syn, c + 32 * hp,
                                        lambda e, c=c, csl=csl, b=b:
                                        e.dma_start(
                            out=x1c[csl][:],
                            in_=dt["x1_s"][b, c * 128:(c + 1) * 128, :]),
                            waits=t_x1_done)
                        for ai in range(2):
                            a = hp * 2 + ai
                            for half in range(2):
                                bank = ai * 2 + half
                                w0 = []
                                if c == 0:
                                    w0 = [t_xc, t_CT[c], ps_free[bank]]
                                elif ai == 0 and half == 0:
                                    w0 = [t_xc, t_CT[c]]
                                tmm_l[ai][half] = ten.emit(
                                    lambda e, c=c, a=a, half=half, csl=csl,
                                    bank=bank: e.matmul(
                                        ps[bank][:],
                                        CT[c][:, a * 128:(a + 1) * 128],
                                        x1c[csl][:, half * 512:
                                                 (half + 1) * 512],
                                        start=(c == 0), stop=(c == 31)),
                                    waits=w0,
                                    inc=(c == 31 or (ai == 1 and
                                                     half == 1)))
                        rx1.consumed(c + 32 * hp, tmm_l[1][1])
                    for ai in range(2):
                        a = hp * 2 + ai
                        t_xslot[a] = sca.emit(lambda e, a=a, ai=ai: (
                            e.activation(out=xslot[a][:, 0:512],
                                         in_=ps[ai * 2][:],
                                         func=AF.Identity),
                            e.activation(out=xslot[a][:, 512:1024],
                                         in_=ps[ai * 2 + 1][:],
                                         func=AF.Identity))[-1],
                            waits=[tmm_l[ai][0], tmm_l[ai][1]], inc=True)
                        ps_free[ai * 2] = t_xslot[a]
                        ps_free[ai * 2 + 1] = t_xslot[a]

            P.barrier()
            if True:
                bump_f2 = Bump(bump_f.off)
                f2sb = bump_f2.maker()
                hf = [f2sb(f"hf{i}", (128, C), F32) for i in range(2)]
                fjunk = f2sb("fjunk", (128, C), F32)
                flnout = [f2sb(f"flnout{i}", (128, C), F32)
                          for i in range(2)]
                htb = f2sb("htb", (128, CCN * 512), BF16)
                s2 = [f2sb(f"s2_{a}", (128, C), BF16) for a in range(4)]

                bump_f2a = Bump(bump_f2.off)
                f2asb = bump_f2a.maker()
                qkT = f2asb("qkT", (128, 16 * 512), BF16)
                vnat = [f2asb(f"vnat{a}", (128, C), BF16)
                        for a in range(4)]
                expd = [f2asb(f"expd{i}", (128, 512), BF16)
                        for i in range(4)]
                esbf = [f2asb(f"esbf{i}", (128, 512), F32)
                        for i in range(2)]
                rcrep = [f2asb(f"rcrep{i}", (128, 512), BF16)
                         for i in range(4)]
                ot2 = f2asb("ot2", (128, CCN * 512), BF16)
                wvh = [f2asb(f"wvh{i}", (128, CCN * 512), BF16)
                       for i in range(2)]
                wqk = [f2asb(f"wqk{i}", (128, C), BF16) for i in range(2)]


                t_hf_cons = [None] * 2
                t_hTe = None
                for a in range(4):
                    rcol = lncol[:, a * 4:a * 4 + 1]
                    lnv = lncol[:, a * 4 + 1:a * 4 + 2]
                    t_sq = vec.emit(lambda e, a=a: e.scalar_tensor_tensor(
                        out=fjunk[:], in0=xslot[a][:], scalar=1.0,
                        in1=xslot[a][:], op0=ALU.mult, op1=ALU.mult,
                        accum_out=lncol[:, a * 4 + 2:a * 4 + 3]),
                        waits=[t_xslot[a]], inc=True, w="hsq")
                    sca.emit(lambda e, a=a, lnv=lnv: e.activation(
                        out=lnv, in_=lncol[:, a * 4 + 2:a * 4 + 3],
                        func=AF.Ln, scale=1.0 / C, bias=epsrms[:]),
                        waits=[t_sq])
                    t_r = sca.emit(lambda e, rcol=rcol, lnv=lnv:
                                   e.activation(
                        out=rcol, in_=lnv, func=AF.Exp, scale=-0.5),
                        inc=True)
                    t_h = vec.emit(lambda e, a=a, rcol=rcol:
                                   e.scalar_tensor_tensor(
                        out=hf[a % 2][:], in0=xslot[a][:], scalar=rcol,
                        op0=ALU.mult, in1=g1rep[:], op1=ALU.mult),
                        waits=[t_r, t_hf_cons[a % 2]], inc=True,
                        w=f"hf{a % 2}")
                    t_tr = None
                    for cc in range(CCN):
                        bank = 6 + (cc % 2)
                        t_tr = ten.emit(lambda e, a=a, cc=cc, bank=bank:
                                        e.transpose(
                            ps[bank][0:128, 0:128],
                            hf[a % 2][:, cc * 128:(cc + 1) * 128],
                            identf[:]),
                            waits=[t_h, ps_free[bank]], inc=True)
                        t_hTe = sca.emit(lambda e, a=a, cc=cc, bank=bank:
                                         e.activation(
                            out=htb[:, cc * 512 + a * 128:
                                    cc * 512 + (a + 1) * 128],
                            in_=ps[bank][0:128, 0:128], func=AF.Identity),
                            waits=[t_tr], inc=True)
                        ps_free[bank] = t_hTe
                    t_hf_cons[a % 2] = t_tr
                t_hT_done = t_hTe

                t_qk = [None] * 16
                for mc in range(16):
                    wsl = mc % 2
                    t_w = rwq.load(gps, mc, lambda e, mc=mc, wsl=wsl:
                                   e.dma_start(
                        out=wqk[wsl][:]
                        .rearrange("p (a f) -> p a f", a=CCN),
                        in_=dt["wqkv_t"][mc]))
                    bank = mc % 2
                    pairs = [(wqk[wsl][:, cc * 128:(cc + 1) * 128],
                              htb[:, cc * 512:(cc + 1) * 512])
                             for cc in range(CCN)]
                    tmm = mm_group(ps[bank][:], pairs, [t_w, t_hT_done],
                                   bank)
                    rwq.consumed(mc, tmm)
                    t_qk[mc] = sca.emit(lambda e, mc=mc, bank=bank:
                                        e.activation(
                        out=qkT[:, mc * 512:(mc + 1) * 512],
                        in_=ps[bank][:], func=AF.Identity),
                        waits=[tmm], inc=True)
                    ps_free[bank] = t_qk[mc]

                t_vn = [None] * 4
                t_vn_mm = None
                for mh in range(2):
                    t_wv = rwv.load(gps, mh, lambda e, mh=mh: e.dma_start(
                        out=wvh[mh % 2][:]
                        .rearrange("p (a f) -> p a f", a=CCN),
                        in_=dt["wv2"][:, :, mh * 512:(mh + 1) * 512]
                        .rearrange("p a f -> p a f")))
                    for a in range(4):
                        bank = 2 + a % 2
                        pairs = [(htb[:, cc * 512 + a * 128:
                                      cc * 512 + (a + 1) * 128],
                                  wvh[mh % 2][:, cc * 512:(cc + 1) * 512])
                                 for cc in range(CCN)]
                        t_vn_mm = mm_group(ps[bank][:], pairs,
                                           [t_wv, t_hT_done], bank)
                        t_vn[a] = sca.emit(lambda e, a=a, mh=mh, bank=bank:
                                           e.activation(
                            out=vnat[a][:, mh * 512:(mh + 1) * 512],
                            in_=ps[bank][:], func=AF.Identity),
                            waits=[t_vn_mm], inc=True)
                        ps_free[bank] = t_vn[a]
                    rwv.consumed(mh, t_vn_mm)

                t_expd_cons = [None] * 4
                t_esbf_cons = [None] * 2
                t_rcflat_cons = None
                t_ot2e = []
                for g in range(2):
                    t_ote_g = []
                    t_rc_g = []
                    t_cs_last = None
                    for hh in range(8):
                        h = g * 8 + hh
                        qmc, qoff = h // 2, 64 * (h % 2)
                        kmc = 8 + h // 2
                        t_ex = [None] * 4
                        for tkc in range(4):
                            bank = 4 + (tkc % 2)
                            pairs = [(qkT[qoff:qoff + 64,
                                          kmc * 512 + tkc * 128:
                                          kmc * 512 + (tkc + 1) * 128],
                                      qkT[qoff:qoff + 64,
                                          qmc * 512:(qmc + 1) * 512])]
                            tmm = mm_group(ps[bank][:], pairs,
                                           [t_qk[qmc], t_qk[kmc]], bank)
                            t_e = sca.emit(lambda e, tkc=tkc, bank=bank:
                                           e.activation(
                                out=esbf[tkc % 2][:], in_=ps[bank][:],
                                func=AF.Exp, scale=0.125),
                                waits=[tmm, t_esbf_cons[tkc % 2]],
                                inc=True)
                            ps_free[bank] = t_e
                            t_ex[tkc] = vec.emit(lambda e, tkc=tkc:
                                                 e.tensor_mul(
                                expd[tkc][:], esbf[tkc % 2][:],
                                minn[tkc][:]),
                                waits=[t_e, t_minn[tkc],
                                       t_expd_cons[tkc]],
                                inc=True, w=f"ex{tkc}")
                            t_esbf_cons[tkc % 2] = t_ex[tkc]
                        for tqs in range(4):
                            pairs = [(expd[tkc][:,
                                               tqs * 128:(tqs + 1) * 128],
                                      ones_c_b[:]) for tkc in range(4)]
                            t_cs_last = mm_group(
                                ps[6][:, 4 * hh + tqs:4 * hh + tqs + 1],
                                pairs, t_ex + [t_c[3]], 6,
                                skip_check=True)
                        t_rc = vec.emit(lambda e, hh=hh: e.reciprocal(
                            rcall[:, 4 * hh:4 * hh + 4],
                            ps[6][:, 4 * hh:4 * hh + 4]),
                            waits=[t_cs_last], inc=True, w=f"rc{hh}")
                        t_rc_g.append(t_rc)
                        bank = hh // 2
                        poff = 64 * (hh % 2)
                        pairs = [(vnat[tkc][:, h * 64:h * 64 + 64],
                                  expd[tkc][:]) for tkc in range(4)]
                        tmm = mm_group(ps[bank][poff:poff + 64, :], pairs,
                                       list(t_vn) + t_ex,
                                       bank if hh % 2 == 0 else None)
                        t_ote_g.append(tmm)
                        for tkc in range(4):
                            t_expd_cons[tkc] = tmm
                    t1 = vec.emit(lambda e: e.tensor_copy(cmp32[:],
                                                          rcall[:]),
                                  waits=t_rc_g, inc=True, w="rcc",
                                  r=tuple(f"rc{i}" for i in range(8)))
                    t2 = ten.emit(lambda e: e.transpose(
                        ps[6][0:32, 0:128], cmp32[:], identf[:]),
                        waits=[t1, ps_free[6]], inc=True)
                    t3 = vec.emit(lambda e: e.tensor_copy(
                        scr32[:], ps[6][0:32, 0:128]), waits=[t2],
                        inc=True, w="rcd")
                    ps_free[6] = t3
                    for pp in range(4):
                        mc = g * 4 + pp
                        t4 = syn.emit(lambda e, pp=pp: e.dma_start(
                            out=rcflat[0:1, :]
                            .rearrange("o (c p) -> o c p", c=8),
                            in_=scr32[8 * pp:8 * pp + 8, :]),
                            waits=[t3, t_rcflat_cons], inc=True, val=16)
                        t_m1 = ten.emit(lambda e: e.matmul(
                            ps[6][0:64, :], ones_r_f[0:1, 0:64],
                            rcflat[0:1, 0:512],
                            start=True, stop=True, skip_group_check=True),
                            waits=[t4, ps_free[6]], inc=True)
                        t_m2 = ten.emit(lambda e: e.matmul(
                            ps[6][64:128, :], ones_r_f[0:1, 0:64],
                            rcflat[0:1, 512:1024],
                            start=True, stop=True, skip_group_check=True),
                            inc=True)
                        t_rcflat_cons = t_m2
                        t_rr = vec.emit(lambda e, pp=pp: e.tensor_copy(
                            rcrep[pp][:], ps[6][:]), waits=[t_m2],
                            inc=True, w=f"rcr{pp}")
                        ps_free[6] = t_rr
                        t_oe = vec.emit(lambda e, mc=mc, pp=pp:
                                        e.tensor_mul(
                            ot2[:, mc * 512:(mc + 1) * 512],
                            ps[pp][:], rcrep[pp][:]),
                            waits=[t_ote_g[2 * pp], t_ote_g[2 * pp + 1],
                                   t_rr],
                            inc=True, w=f"oe{mc}", r=(f"rcr{pp}",))
                        ps_free[pp] = t_oe
                        t_ot2e.append(t_oe)

                t_s2 = [None] * 4
                for half in range(2):
                    t_wo = rwv.load(gps, 2 + half, lambda e, half=half:
                                    e.dma_start(
                        out=wvh[half % 2][:]
                        .rearrange("p (a f) -> p a f", a=CCN),
                        in_=dt["wout_t"][:, :, half * 512:(half + 1) * 512]
                        .rearrange("p a f -> p a f")))
                    for a in range(4):
                        bank = (a * 2 + half) % 4
                        pairs = [(ot2[:, mc * 512 + a * 128:
                                      mc * 512 + (a + 1) * 128],
                                  wvh[half % 2][:,
                                                mc * 512:(mc + 1) * 512])
                                 for mc in range(CCN)]
                        tmm = mm_group(ps[bank][:], pairs,
                                       [t_wo, t_ot2e[-1]], bank)
                        t_s2h = vec.emit(lambda e, a=a, half=half,
                                         bank=bank:
                                         e.scalar_tensor_tensor(
                            out=s2[a][:, half * 512:(half + 1) * 512],
                            in0=ps[bank][:], scalar=1.0,
                            in1=xslot[a][:, half * 512:(half + 1) * 512],
                            op0=ALU.mult, op1=ALU.add),
                            waits=[tmm], inc=True, w=f"s2_{a}{half}")
                        ps_free[bank] = t_s2h
                        t_s2[a] = t_s2h
                    rwv.consumed(2 + half, tmm)

                P.barrier()
                f2bsb = Bump(bump_f2.off).maker()
                CA = [f2bsb(f"CA{a}", (128, T), BF16) for a in range(4)]
                fT = [f2bsb(f"fT{m}", (128, 512), BF16)
                      for m in range(MCF)]
                wfa = [f2bsb(f"wfa{i}", (128, C), BF16) for i in range(2)]
                wfb = [f2bsb(f"wfb{i}", (128, C), BF16) for i in range(2)]
                w3s = [f2bsb(f"w3s{i}", (128, C), BF16) for i in range(2)]
                silt = [f2bsb(f"silt{i}", (128, 512), BF16)
                        for i in range(2)]
                selout = [f2bsb(f"selout{i}", (128, C), F32)
                          for i in range(2)]
                delta = [f2bsb(f"delta{a}", (128, C), BF16)
                         for a in range(4)]
                sdt = [f2bsb(f"sdt{i}", (128, C), F32) for i in range(2)]

                t_CA = build_CA(CA, [t_slot], f2bsb)

                t_h2Te = None
                for a in range(4):
                    rcol = lncol[:, a * 4 + 3:a * 4 + 4]
                    lnv = lncol[:, a * 4 + 2:a * 4 + 3]
                    t_sq = vec.emit(lambda e, a=a: e.scalar_tensor_tensor(
                        out=fjunk[:], in0=s2[a][:], scalar=1.0,
                        in1=s2[a][:], op0=ALU.mult, op1=ALU.mult,
                        accum_out=lncol[:, a * 4 + 1:a * 4 + 2]),
                        waits=[t_s2[a]], inc=True, w="h2sq")
                    sca.emit(lambda e, a=a, lnv=lnv: e.activation(
                        out=lnv, in_=lncol[:, a * 4 + 1:a * 4 + 2],
                        func=AF.Ln, scale=1.0 / C, bias=epsrms[:]),
                        waits=[t_sq])
                    t_r = sca.emit(lambda e, rcol=rcol, lnv=lnv:
                                   e.activation(
                        out=rcol, in_=lnv, func=AF.Exp, scale=-0.5),
                        inc=True)
                    t_h2 = vec.emit(lambda e, a=a, rcol=rcol:
                                    e.scalar_tensor_tensor(
                        out=hf[a % 2][:], in0=s2[a][:], scalar=rcol,
                        op0=ALU.mult, in1=g2rep[:], op1=ALU.mult),
                        waits=[t_r, t_hf_cons[a % 2]], inc=True,
                        w=f"hf{a % 2}")
                    t_tr = None
                    for cc in range(CCN):
                        bank = 6 + (cc % 2)
                        t_tr = ten.emit(lambda e, a=a, cc=cc, bank=bank:
                                        e.transpose(
                            ps[bank][0:128, 0:128],
                            hf[a % 2][:, cc * 128:(cc + 1) * 128],
                            identf[:]),
                            waits=[t_h2, ps_free[bank], t_vn_mm],
                            inc=True)
                        t_h2Te = sca.emit(lambda e, a=a, cc=cc, bank=bank:
                                          e.activation(
                            out=htb[:, cc * 512 + a * 128:
                                    cc * 512 + (a + 1) * 128],
                            in_=ps[bank][0:128, 0:128],
                            func=AF.Identity),
                            waits=[t_tr], inc=True)
                        ps_free[bank] = t_h2Te
                    t_hf_cons[a % 2] = t_tr
                t_h2T_done = t_h2Te

                t_fT = [None] * MCF
                t_sil = None
                for mc in range(MCF):
                    wsl = mc % 2
                    t_w1 = rwq.load(gps, 16 + mc, lambda e, mc=mc,
                                    wsl=wsl: e.dma_start(
                        out=wfa[wsl][:]
                        .rearrange("p (a f) -> p a f", a=CCN),
                        in_=dt["w1_t"][mc]))
                    t_w2 = rwb.load(gps, mc, lambda e, mc=mc, wsl=wsl:
                                    e.dma_start(
                        out=wfb[wsl][:]
                        .rearrange("p (a f) -> p a f", a=CCN),
                        in_=dt["w2_t"][mc]))
                    pairs1 = [(wfa[wsl][:, cc * 128:(cc + 1) * 128],
                               htb[:, cc * 512:(cc + 1) * 512])
                              for cc in range(CCN)]
                    tmm1 = mm_group(ps[4][:], pairs1, [t_w1, t_h2T_done],
                                    4)
                    pairs2 = [(wfb[wsl][:, cc * 128:(cc + 1) * 128],
                               htb[:, cc * 512:(cc + 1) * 512])
                              for cc in range(CCN)]
                    tmm2 = mm_group(ps[5][:], pairs2, [t_w2], 5)
                    rwq.consumed(16 + mc, tmm1)
                    rwb.consumed(mc, tmm2)
                    t_sil = sca.emit(lambda e, mc=mc: e.activation(
                        out=silt[mc % 2][:], in_=ps[4][:], func=AF.Silu),
                        waits=[tmm1, t_fT[mc - 2] if mc >= 2 else None],
                        inc=True)
                    ps_free[4] = t_sil
                    t_fT[mc] = vec.emit(lambda e, mc=mc: e.tensor_mul(
                        fT[mc][:], silt[mc % 2][:], ps[5][:]),
                        waits=[tmm2, t_sil], inc=True, w=f"fT{mc}")
                    ps_free[5] = t_fT[mc]

                t_xffmm = None
                for mc in range(MCF):
                    t_w3 = rw3b.load(gps, mc, lambda e, mc=mc: e.dma_start(
                        out=w3s[mc % 2][:], in_=dt["w3_t"][mc]))
                    for a in range(4):
                        for half in range(2):
                            bank = a * 2 + half
                            w0 = []
                            if mc == 0:
                                w0 = [t_w3, t_fT[mc], ps_free[bank]]
                            elif a == 0 and half == 0:
                                w0 = [t_w3, t_fT[mc]]
                            t_xffmm = ten.emit(lambda e, mc=mc, a=a,
                                               half=half, bank=bank:
                                               e.matmul(
                                ps[bank][:],
                                fT[mc][:, a * 128:(a + 1) * 128],
                                w3s[mc % 2][:,
                                            half * 512:(half + 1) * 512],
                                start=(mc == 0), stop=(mc == MCF - 1)),
                                waits=w0, inc=(a == 3 and half == 1))
                    rw3b.consumed(mc, t_xffmm)

                t_delta = [None] * 4
                t_selout_cons = [None] * 2
                t_lnf_prev = [None] * 2
                for a in range(4):
                    t_preh = [None, None]
                    for half in range(2):
                        bank = a * 2 + half
                        vec.emit(lambda e, a=a, half=half, bank=bank:
                                 e.scalar_tensor_tensor(
                            out=fjunk[:, half * 512:(half + 1) * 512],
                            in0=ps[bank][:], scalar=1.0,
                            in1=s2[a][:, half * 512:(half + 1) * 512],
                            op0=ALU.mult, op1=ALU.add),
                            waits=[t_xffmm, t_selout_cons[a % 2]],
                            w=f"prh{half}")
                        t_preh[half] = vec.emit(
                            lambda e, a=a, half=half:
                            e.scalar_tensor_tensor(
                                out=selout[a % 2][:,
                                                  half * 512:
                                                  (half + 1) * 512],
                                in0=xslot[a][:,
                                             half * 512:(half + 1) * 512],
                                scalar=1.0,
                                in1=fjunk[:, half * 512:(half + 1) * 512],
                                op0=ALU.mult, op1=ALU.add),
                            inc=True, w=f"pre{a}{half}",
                            r=(f"prh{half}",))
                        ps_free[bank] = t_preh[half]
                    vec.emit(lambda e, a=a: e.bn_stats(
                        stats[:, 0:6], selout[a % 2][:, 0:512]),
                        waits=t_preh, w="fbnsa",
                        r=(f"pre{a}0", f"pre{a}1"))
                    vec.emit(lambda e, a=a: e.bn_stats(
                        stats[:, 6:12], selout[a % 2][:, 512:1024]),
                        w="fbnsb")
                    t_fbn = vec.emit(lambda e: e.bn_aggr(
                        mvt[:, 0:2], stats[:, 0:12]),
                        inc=True, w="fmv", r=("fbnsa", "fbnsb"))
                    rcol = lncol[:, a * 4:a * 4 + 1]
                    nmr = lncol[:, a * 4 + 1:a * 4 + 2]
                    lnv = lncol[:, a * 4 + 2:a * 4 + 3]
                    sca.emit(lambda e, lnv=lnv: e.activation(
                        out=lnv, in_=mvt[:, 1:2], func=AF.Ln,
                        bias=epsln[:]), waits=[t_fbn])
                    t_r = sca.emit(lambda e, rcol=rcol, lnv=lnv:
                                   e.activation(
                        out=rcol, in_=lnv, func=AF.Exp, scale=-0.5),
                        inc=True)
                    vec.emit(lambda e, nmr=nmr, rcol=rcol: e.tensor_mul(
                        nmr, mvt[:, 0:1], rcol),
                        waits=[t_r], w="fnm")
                    t_nmr = vec.emit(lambda e, nmr=nmr:
                                     e.tensor_scalar_mul(
                        nmr, nmr, -1.0), inc=True, w="fnm2", r=("fnm",))
                    t_lnf = sca.emit(lambda e, a=a, rcol=rcol, nmr=nmr:
                                     e.activation(
                        out=flnout[a % 2][:], in_=selout[a % 2][:],
                        func=AF.Identity, scale=rcol, bias=nmr),
                        waits=[t_nmr, t_lnf_prev[a % 2]], inc=True)
                    t_selout_cons[a % 2] = t_lnf
                    t_delta[a] = vec.emit(lambda e, a=a:
                                          e.scalar_tensor_tensor(
                        out=delta[a][:], in0=xslot[a][:],
                        scalar=nsscol[:, a:a + 1], op0=ALU.mult,
                        in1=flnout[a % 2][:], op1=ALU.add),
                        waits=[t_lnf, t_nss], inc=True, w=f"dl{a}")
                    t_lnf_prev[a % 2] = t_delta[a]
                    if dbg:
                        t_dd = syn.emit(lambda e, a=a, b=b: e.dma_start(
                            out=dt["dbg_selout"][b,
                                                 a * 128:(a + 1) * 128,
                                                 :],
                            in_=flnout[a % 2][:]), waits=[t_lnf],
                            inc=True, val=16)
                        t_lnf_prev[a % 2] = t_dd
                        t_ds = gps.emit(lambda e, a=a, b=b: e.dma_start(
                            out=dt["dbg_s2"][b, a * 128:(a + 1) * 128,
                                             :],
                            in_=s2[a][:]), waits=[t_s2[a]], inc=True,
                            val=16)
                        t_xf = vec.emit(lambda e, a=a: e.tensor_copy(
                            fjunk[:], xslot[a][:]),
                            waits=[t_xslot[a], t_preh[1]],
                            inc=True, w="dbgx", r=("prh1",))
                        t_dx = syn.emit(lambda e, a=a, b=b: e.dma_start(
                            out=dt["dbg_xslot"][b,
                                                a * 128:(a + 1) * 128,
                                                :],
                            in_=fjunk[:]), waits=[t_xf], inc=True,
                            val=16)

                t_sd_cons = [None, None]
                for c in range(32):
                    t_ev = None
                    for half in range(2):
                        bank = half
                        pairs = [(CA[a][:, c * 128:(c + 1) * 128],
                                  delta[a][:, half * 512:(half + 1) * 512])
                                 for a in range(4)]
                        tmm = mm_group(ps[bank][:], pairs,
                                       t_delta + t_CA +
                                       [t_sd_cons[c % 2]], bank)
                        t_ev = vec.emit(lambda e, c=c, half=half,
                                        bank=bank: e.tensor_copy(
                            sdt[c % 2][:, half * 512:(half + 1) * 512],
                            ps[bank][:]), waits=[tmm], inc=True,
                            w=f"sd{c % 2}{half}")
                        ps_free[bank] = t_ev
                    t_add = gps.emit(lambda e, c=c, b=b: e.dma_start(
                        out=dt["out"][b, c * 128:(c + 1) * 128, :],
                        in_=sdt[c % 2][:], accum_op=ALU.add),
                        waits=[t_ev] + t_u_done, inc=True, val=16,
                        sem=scat_sems[c % 2])
                    t_sd_cons[c % 2] = t_add

                P.barrier()

    block = ctx.enter_context(nc.Block())
    P.play(block)
    return nc, ctx



_CACHE = {}


def _prep_host(inputs):
    x = np.asarray(inputs["L0_attn_x"], dtype=np.float32)

    perm = np.empty(C, dtype=np.int64)
    perm[:512] = np.arange(512) * 2
    perm[512:] = np.arange(512) * 2 + 1

    def tile_w(w):
        return np.ascontiguousarray(w.reshape(CCN, 128, -1).transpose(1, 0, 2))

    wk = tile_w(np.asarray(inputs["L0_attn_Wk"], np.float32)[:, perm])
    wq = tile_w(np.asarray(inputs["L0_attn_Wq"], np.float32)[:, perm])
    wv = tile_w(np.asarray(inputs["L0_attn_Wv"], np.float32))
    wp = tile_w(np.asarray(inputs["L0_attn_Wp"], np.float32))
    bkcv = np.ascontiguousarray(
        np.asarray(inputs["L0_attn_bk"], np.float32)[perm].reshape(CCN, 128).T)
    bqcv = np.ascontiguousarray(
        np.asarray(inputs["L0_attn_bq"], np.float32)[perm].reshape(CCN, 128).T)

    wqkv = np.asarray(inputs["L0_ffn_W_qkv"], np.float32)
    wqkvT = wqkv.T
    wqkv_t = np.ascontiguousarray(
        wqkvT[:, :2048].reshape(CCN, 128, 16, 128).transpose(2, 1, 0, 3))
    wv2 = np.ascontiguousarray(
        wqkvT[:, 2048:].reshape(CCN, 128, C).transpose(1, 0, 2))
    wout_t = np.ascontiguousarray(
        np.asarray(inputs["L0_ffn_W_out"], np.float32).T
        .reshape(CCN, 128, C).transpose(1, 0, 2))

    w1p = np.zeros((DFFP, C), np.float32)
    w1p[:DFF] = np.asarray(inputs["L0_ffn_W1"], np.float32)
    w2p = np.zeros((DFFP, C), np.float32)
    w2p[:DFF] = np.asarray(inputs["L0_ffn_W2"], np.float32)
    w3p = np.zeros((C, DFFP), np.float32)
    w3p[:, :DFF] = np.asarray(inputs["L0_ffn_W3"], np.float32)
    w1_t = np.ascontiguousarray(
        w1p.reshape(MCF, 128, CCN, 128).transpose(0, 3, 2, 1))
    w2_t = np.ascontiguousarray(
        w2p.reshape(MCF, 128, CCN, 128).transpose(0, 3, 2, 1))
    w3_t = np.ascontiguousarray(w3p.T.reshape(MCF, 128, C))

    jj = np.arange(512, dtype=np.float64)
    tt = np.arange(T, dtype=np.float64)
    ang = np.outer(jj, (T - 1) - tt)
    cosT = np.cos(ang).astype(ml_dtypes.bfloat16)
    sinT = np.sin(ang).astype(ml_dtypes.bfloat16)

    def tile_cs(a):
        return np.ascontiguousarray(
            a.reshape(4, 128, NTB, TBS).transpose(2, 1, 0, 3))

    pidx = np.arange(128)[:, None]
    fidx = np.arange(TBS)[None, :]
    masks_np = np.stack([np.where(128 * a + pidx <= fidx, 0.0, NEG)
                         for a in range(4)], axis=1).astype(np.float32)
    tri_np = np.stack([(fidx < 128 * a + pidx).astype(np.float32)
                       for a in range(4)], axis=1).astype(np.float32)
    slotiota_np = np.broadcast_to(
        np.arange(512, dtype=np.float32)[None, :], (128, 512)).copy()
    iotacols_np = (128.0 * np.arange(4)[None, :] + pidx).astype(np.float32)

    shared = dict(
        wk=wk, wq=wq, wv=wv, wp=wp, bkc=bkcv, bqc=bqcv,
        bv_row=np.asarray(inputs["L0_attn_bv"], np.float32).reshape(1, C),
        bp_row=np.asarray(inputs["L0_attn_bp"], np.float32).reshape(1, C),
        wqkv_t=wqkv_t, wv2=wv2, wout_t=wout_t,
        w1_t=w1_t, w2_t=w2_t, w3_t=w3_t,
        g1_row=np.asarray(inputs["L0_ffn_g1"], np.float32).reshape(1, C),
        g2_row=np.asarray(inputs["L0_ffn_g2"], np.float32).reshape(1, C),
        wr_row=np.asarray(inputs["L0_ffn_w_router"],
                          np.float32).reshape(1, C),
        cos_t=tile_cs(cosT), sin_t=tile_cs(sinT),
        masks=masks_np, tri=tri_np,
        slotiota=slotiota_np, iotacols=iotacols_np,
        identf=np.eye(128, dtype=np.float32),
        ones_r_b=np.ones((1, 128), ml_dtypes.bfloat16),
        ones_r_f=np.ones((1, 128), np.float32),
        ones_c_b=np.ones((128, 1), ml_dtypes.bfloat16),
        ones_sq_f=np.ones((128, 128), np.float32),
    )

    in_maps = []
    for k in range(NCORES):
        xb = np.ascontiguousarray(x[BPC * k: BPC * (k + 1)])
        xtr = np.ascontiguousarray(xb.transpose(0, 2, 1))
        m = dict(shared)
        m["x_nat"] = xb
        m["x_tr"] = xtr
        in_maps.append(m)
    return in_maps


def kernel(**inputs):
    if "prog" not in _CACHE:
        _CACHE["prog"] = build_program(dbg=False)
    nc, _ctx = _CACHE["prog"]
    in_maps = _prep_host(inputs)
    res = run_bass_kernel_spmd(nc, in_maps, core_ids=list(range(NCORES)))
    out = np.concatenate([r["out"] for r in res.results], axis=0)
    return out.reshape(16, T, C)


# version 04 (final)
# speedup vs baseline: 1.2280x; 1.2280x over previous
import numpy as np
import ml_dtypes
from contextlib import ExitStack

import concourse.bass as bass
import concourse.mybir as mybir
from concourse.bass_utils import run_bass_kernel_spmd

F32 = mybir.dt.float32
BF16 = mybir.dt.bfloat16
U32 = mybir.dt.uint32
AF = mybir.ActivationFunctionType
ALU = mybir.AluOpType
AX = mybir.AxisListType

NCORES = 8
BPC = 2
T = 4096
C = 1024
TBS = 512
NTB = T // TBS
CCN = C // 128
CAP = 512
H = 16
DFF = 2730
DFFP = 2816
MCF = DFFP // 128
NBIS = 48
NEG = -1.0e30
ISCALE = float(C) ** -0.5



class Q:
    def __init__(self, prog, name):
        self.prog = prog
        self.name = name
        self.sem = prog.ctx.enter_context(prog.nc.semaphore(f"s_{name}"))
        self.bsem = prog.ctx.enter_context(prog.nc.semaphore(f"sb_{name}"))
        self.ops = []
        self.pad = False

    def emit(self, fn, waits=(), inc=False, sem=None, val=1, w=None, r=()):
        waits = tuple(t for t in waits if t is not None)
        isem = None
        tick = None
        if inc:
            isem = sem if sem is not None else self.sem
            cum = self.prog.sem_cnt.get(isem.name, 0) + val
            self.prog.sem_cnt[isem.name] = cum
            tick = (isem, cum)
        self.ops.append((fn, waits, isem, val, w, tuple(r), self.pad,
                         getattr(self.prog, "scope", None)))
        return tick

    def tick(self):
        return (self.sem, self.prog.sem_cnt.get(self.sem.name, 0))

    def play(self, eng):
        satisfied = {}
        prev_w = None
        first = True
        import contextlib
        for fn, waits, isem, ival, wname, rnames, pad, scope in self.ops:
            byname = {}
            for (sem, v) in waits:
                if sem is self.sem:
                    continue
                if v > byname.get(sem.name, (None, 0))[1]:
                    byname[sem.name] = (sem, v)
            for sem, v in byname.values():
                if v > satisfied.get(sem.name, 0):
                    eng.wait_ge(sem, v)
                    satisfied[sem.name] = v
            with (self.prog.nc.named_scope(scope) if scope
                  else contextlib.nullcontext()):
                if self.name == "vec" and not first and (
                        pad or (prev_w is not None and
                                (prev_w == wname or prev_w in rnames))):
                    padt = self.prog.padt
                    eng.tensor_copy(padt[:, 0:1], padt[:, 2:3])
                    eng.tensor_copy(padt[:, 1:2], padt[:, 2:3])
                inst = fn(eng)
                if isem is not None:
                    inst.then_inc(isem, ival)
            prev_w = wname if self.name == "vec" else None
            first = False


class Ring:

    def __init__(self, prog, name, nslots):
        self.nslots = nslots
        self.sems = [prog.ctx.enter_context(prog.nc.semaphore(f"r_{name}{i}"))
                     for i in range(nslots)]
        self.cnt = [0] * nslots
        self.free_t = [None] * nslots

    def load(self, q, i, fn, waits=()):
        s = i % self.nslots
        self.cnt[s] += 16
        w = list(waits)
        if self.free_t[s] is not None:
            w.append(self.free_t[s])
        q.emit(fn, waits=w, inc=True, sem=self.sems[s], val=16)
        return (self.sems[s], self.cnt[s])

    def consumed(self, i, ticket):
        assert ticket is not None, "Ring.consumed needs an incrementing ticket"
        self.free_t[i % self.nslots] = ticket


class Prog:
    def __init__(self, nc, ctx):
        self.nc = nc
        self.ctx = ctx
        self.sem_cnt = {}
        self.extra_sems = []
        self.ten = Q(self, "ten")
        self.vec = Q(self, "vec")
        self.sca = Q(self, "sca")
        self.syn = Q(self, "syn")
        self.gps = Q(self, "gps")
        self.queues = [self.ten, self.vec, self.sca, self.syn, self.gps]

    def mksem(self, name):
        s = self.ctx.enter_context(self.nc.semaphore(name))
        self.extra_sems.append(s)
        return s

    def barrier(self):
        ticks = [q.emit(lambda e: e.nop(), inc=True, sem=q.bsem)
                 for q in self.queues]
        ticks += [(s, self.sem_cnt[s.name]) for s in self.extra_sems
                  if self.sem_cnt.get(s.name, 0) > 0]
        for q in self.queues:
            q.emit(lambda e: e.nop(), waits=ticks)

    def play(self, block):
        @block.tensor
        def _(eng):
            self.ten.play(eng)

        @block.vector
        def _(eng):
            self.vec.play(eng)

        @block.scalar
        def _(eng):
            self.sca.play(eng)

        @block.sync
        def _(eng):
            self.syn.play(eng)

        @block.gpsimd
        def _(eng):
            self.gps.play(eng)



def build_program(dbg=False):
    nc = bass.Bass("TRN2", target_bir_lowering=False, debug=False,
                   detect_race_conditions=False)
    ctx = ExitStack()
    dt = {}

    def dram(name, shape, dtype, kind="ExternalInput"):
        dt[name] = nc.dram_tensor(name, list(shape), dtype, kind=kind).ap()
        return dt[name]

    dram("x_nat", (BPC, T, C), F32)
    dram("x_tr", (BPC, C, T), F32)
    dram("wk", (128, CCN, C), F32)
    dram("wq", (128, CCN, C), F32)
    dram("wv", (128, CCN, C), F32)
    dram("wp", (128, CCN, C), F32)
    dram("bkc", (128, CCN), F32)
    dram("bqc", (128, CCN), F32)
    dram("bv_row", (1, C), F32)
    dram("bp_row", (1, C), F32)
    dram("wqkv_t", (16, 128, CCN, 128), F32)
    dram("wv2", (128, CCN, C), F32)
    dram("wout_t", (128, CCN, C), F32)
    dram("w1_t", (MCF, 128, CCN, 128), F32)
    dram("w2_t", (MCF, 128, CCN, 128), F32)
    dram("w3_t", (MCF, 128, C), F32)
    dram("g1_row", (1, C), F32)
    dram("g2_row", (1, C), F32)
    dram("wr_row", (1, C), F32)
    dram("cos_t", (NTB, 128, 4, TBS), BF16)
    dram("sin_t", (NTB, 128, 4, TBS), BF16)
    dram("masks", (128, 4, TBS), F32)
    dram("tri", (128, 4, TBS), F32)
    dram("slotiota", (128, 512), F32)
    dram("iotacols", (128, 4), F32)
    dram("identf", (128, 128), F32)
    dram("ones_r_b", (1, 128), BF16)
    dram("ones_r_f", (1, 128), F32)
    dram("ones_c_b", (128, 1), BF16)
    dram("ones_sq_f", (128, 128), F32)

    dram("out", (BPC, T, C), F32, kind="ExternalOutput")
    sk = "ExternalOutput" if dbg else "Internal"
    dram("kt_s", (BPC, NTB, 128, CCN, TBS), BF16, kind=sk)
    dram("qt_s", (BPC, 32, 128, C), BF16, kind=sk)
    dram("v_s", (BPC, T, C), BF16, kind=sk)
    dram("x1_s", (BPC, T, C), BF16, kind=sk)
    if dbg:
        dram("dbg_score", (BPC, 128, 32), F32, kind="ExternalOutput")
        dram("dbg_st", (128, 8 * TBS), F32, kind="ExternalOutput")
        dram("dbg_ot", (128, CCN * TBS), F32, kind="ExternalOutput")
        dram("dbg_selm", (BPC, 128, 32), F32, kind="ExternalOutput")
        dram("dbg_selt", (BPC, 32, 128), F32, kind="ExternalOutput")
        dram("dbg_pos", (BPC, 32, 128), F32, kind="ExternalOutput")
        dram("dbg_offc", (BPC, 32, 1), F32, kind="ExternalOutput")
        dram("dbg_slott", (BPC, 32, 128), F32, kind="ExternalOutput")
        dram("dbg_lo", (BPC, 128, 1), F32, kind="ExternalOutput")
        dram("dbg_bis", (BPC, 128, 4 * NBIS), F32, kind="ExternalOutput")
        dram("dbg_smat", (BPC, 128, 32), F32, kind="ExternalOutput")
        dram("dbg_slot", (BPC, 128, 32), F32, kind="ExternalOutput")
        dram("dbg_rank", (BPC, 128, 4), F32, kind="ExternalOutput")
        dram("dbg_xslot", (BPC, 512, C), F32, kind="ExternalOutput")
        dram("dbg_s2", (BPC, 512, C), F32, kind="ExternalOutput")
        dram("dbg_selout", (BPC, 512, C), F32, kind="ExternalOutput")

    P = Prog(nc, ctx)
    ten, vec, sca, syn, gps = P.ten, P.vec, P.sca, P.syn, P.gps

    def sb(name, shape, dtype):
        return ctx.enter_context(nc.sbuf_tensor("t_" + name, list(shape), dtype))

    ARENA_B = 164 * 1024
    arena = sb("arena", (128, ARENA_B), mybir.dt.uint8)

    class Bump:
        def __init__(self, base):
            self.off = base

        def maker(self):
            def mk(name, shape, dtyp):
                per = 1
                for s_ in shape[1:]:
                    per *= s_
                per *= mybir.dt.size(dtyp)
                off = (self.off + 63) & ~63
                self.off = off + per
                assert self.off <= ARENA_B, (name, self.off)
                v = arena[0:shape[0], off:off + per].bitcast(dtyp)
                if len(shape) == 3:
                    v = v.rearrange("p (a f) -> p a f", a=shape[1])
                return v
            return mk

    identf = sb("identf", (128, 128), F32)
    ones_r_b = sb("ones_r_b", (1, 128), BF16)
    ones_r_f = sb("ones_r_f", (1, 128), F32)
    ones_c_b = sb("ones_c_b", (128, 1), BF16)
    ones_sq_f = sb("ones_sq_f", (128, 128), F32)
    iotacols = sb("iotacols", (128, 4), F32)
    bkc = sb("bkc", (128, CCN), F32)
    bqc = sb("bqc", (128, CCN), F32)
    bvr = sb("bvr", (1, C), BF16)
    bpr = sb("bpr", (1, C), BF16)
    grow = sb("grow", (1, 3 * C), BF16)
    g1rep = sb("g1rep", (128, C), BF16)
    g2rep = sb("g2rep", (128, C), BF16)
    wrrep = sb("wrrep", (128, C), BF16)
    scoremat = sb("scoremat", (128, 32), F32)
    smat = sb("smat", (128, 32), F32)
    scb = sb("scb", (128, 32), BF16)
    ssb = sb("ssb", (128, 32), BF16)
    locol = sb("locol", (128, 1), F32)
    hicol = sb("hicol", (128, 1), F32)
    midA = sb("midA", (128, 1), F32)
    midB = sb("midB", (128, 1), F32)
    cmp32 = sb("cmp32", (128, 32), F32)
    cntcol = sb("cntcol", (128, 1), F32)
    cge = sb("cge", (128, 1), U32)
    clt = sb("clt", (128, 1), U32)
    selm = sb("selm", (128, 32), F32)
    selt = sb("selt", (32, 128), F32)
    scr32 = sb("scr32", (32, 128), F32)
    pret = [sb(f"pret{i}", (32, 128), F32) for i in range(2)]
    post_t = sb("post_t", (32, 128), F32)
    slott = sb("slott", (32, 128), F32)
    nselu = sb("nselu", (32, 128), U32)
    c512t = sb("c512t", (32, 128), F32)
    rt32 = sb("rt32", (32, 32), F32)
    rr32 = sb("rr32", (32, 32), F32)
    er32 = [sb(f"er32_{i}", (1, 32), F32) for i in range(2)]
    offc = sb("offc", (32, 1), F32)
    slotc = sb("slotc", (128, 32), F32)
    epsln = sb("epsln", (128, 1), F32)
    epsln4 = sb("epsln4", (128, 1), F32)
    epsrms = sb("epsrms", (128, 1), F32)
    b1col = sb("b1col", (128, 1), F32)
    sccol = sb("sccol", (128, 4), F32)
    sscol = sb("sscol", (128, 4), F32)
    nsscol = sb("nsscol", (128, 4), F32)
    rankc = sb("rankc", (128, 4), F32)
    rankrep = sb("rankrep", (128, 512), F32)
    minn = [sb(f"minn{i}", (128, 512), BF16) for i in range(4)]
    stats = sb("stats", (128, 4 * 12), F32)
    mvt = sb("mvt", (128, 4 * 2), F32)
    lncol = sb("lncol", (128, 16), F32)
    abscol = sb("abscol", (128, 1), F32)
    padt = sb("padt", (128, 4), F32)
    P.padt = padt
    rcall = sb("rcall", (128, 32), F32)
    rcflat = sb("rcflat", (1, 1024), F32)

    ps = [ctx.enter_context(nc.psum_tensor(f"ps{i}", (128, TBS), F32))
          for i in range(8)]
    ps_free = [None] * 8

    def mm_group(psum_ap, pairs, waits, bank, skip_check=False):
        n = len(pairs)
        tmm = None
        for i, (lhsT, rhs) in enumerate(pairs):
            w = ()
            if i == 0:
                w = list(waits)
                if bank is not None:
                    w.append(ps_free[bank])
            tmm = ten.emit(
                lambda e, lhsT=lhsT, rhs=rhs, i=i, n=n: e.matmul(
                    psum_ap, lhsT, rhs, start=(i == 0), stop=(i == n - 1),
                    skip_group_check=skip_check),
                waits=w, inc=(i == n - 1))
        return tmm

    def ld(q, dst, src, waits=()):
        return q.emit(lambda e, d=dst, s=src: e.dma_start(out=d, in_=s),
                      waits=waits, inc=True, val=16)

    P.scope = "startup"
    t_c = []
    for dst, src in ((identf, "identf"), (ones_r_b, "ones_r_b"),
                     (ones_r_f, "ones_r_f"), (ones_c_b, "ones_c_b"),
                     (ones_sq_f, "ones_sq_f"),
                     (iotacols, "iotacols"), (bkc, "bkc"), (bqc, "bqc")):
        t_c.append(ld(syn, dst[:], dt[src]))
    for dst, src in ((bvr[:], "bv_row"), (bpr[:], "bp_row"),
                     (grow[:, 0:C], "g1_row"), (grow[:, C:2 * C], "g2_row"),
                     (grow[:, 2 * C:3 * C], "wr_row")):
        t_c.append(ld(gps, dst, dt[src]))
    t_eps = None
    vec.emit(lambda e: e.memset(padt[:], 0.0))
    for tile, v in ((epsln, 1e-5), (epsln4, 2.5e-6), (epsrms, 1e-6),
                    (b1col, 1.0), (c512t, 512.0)):
        t_eps = vec.emit(lambda e, t=tile, vv=v: e.memset(t[:], vv), inc=True)
    t_grep = None
    for gi, gdst in enumerate((g1rep, g2rep, wrrep)):
        tmm0 = mm_group(ps[0][:], [(ones_r_b[:],
                                    grow[:, gi * C:gi * C + 512])],
                        list(t_c), 0)
        tmm1 = mm_group(ps[1][:], [(ones_r_b[:],
                                    grow[:, gi * C + 512:(gi + 1) * C])],
                        list(t_c), 1)
        t_grep = vec.emit(lambda e, d=gdst: (
            e.tensor_copy(d[:, 0:512], ps[0][:]),
            e.tensor_copy(d[:, 512:1024], ps[1][:]))[-1],
            waits=[tmm0, tmm1], inc=True, w=f"grep{gi}")
        ps_free[0] = t_grep
        ps_free[1] = t_grep
    P.barrier()

    rxs = Ring(P, "xs", 2)
    rcs = Ring(P, "cs", 2)
    rki = Ring(P, "ki", 2)
    rqj = Ring(P, "qj", 4)
    rvj = Ring(P, "vj", 4)
    rxn = Ring(P, "xn", 4)
    rx1 = Ring(P, "x1r", 4)
    rwq = Ring(P, "wqk", 2)
    rwb = Ring(P, "wb", 2)
    rw3b = Ring(P, "w3b", 2)
    rwv = Ring(P, "wv2", 2)
    rsx = Ring(P, "sx", 2)
    ktq_sems = [P.mksem(f"ktq{i}") for i in range(2)]
    v_sems = [P.mksem(f"vst{i}") for i in range(4)]
    x1_sems = [P.mksem(f"x1s{i}") for i in range(4)]
    u_sems = [P.mksem(f"ust{i}") for i in range(4)]
    scat_sems = [P.mksem(f"scat{i}") for i in range(2)]

    for b in range(BPC):
        P.barrier()
        if True:
            P.scope = f"b{b}A1"
            bump_a = Bump(0)
            asb = bump_a.maker()
            wp_sb = asb("wp", (128, CCN * C), BF16)
            t_wp = ld(gps, wp_sb[:], dt["wp"].rearrange("p a b -> p (a b)"))

            if True:
                psb = Bump(bump_a.off).maker()
                wk_sb = psb("wk", (128, CCN * C), BF16)
                wq_sb = psb("wq", (128, CCN * C), BF16)
                wv_sb = psb("wv", (128, CCN * C), BF16)
                xs = [psb(f"xs{i}", (128, CCN * TBS), BF16) for i in range(2)]
                cs = [psb(f"cs{i}", (128, 4 * TBS), BF16) for i in range(2)]
                sn = [psb(f"sn{i}", (128, 4 * TBS), BF16) for i in range(2)]
                kraw = psb("kraw", (128, CCN * TBS), BF16)
                qraw = psb("qraw", (128, CCN * TBS), BF16)
                mrope = [psb(f"mrope{i}", (128, TBS), BF16) for i in range(4)]
                ktf = psb("ktf", (128, CCN * TBS), BF16)
                qtf = psb("qtf", (128, CCN * TBS), BF16)
                vev = [psb(f"vev{i}", (128, TBS), BF16) for i in range(4)]


                t_wA = [t_wp]
                for dst, srcn in ((wk_sb, "wk"), (wq_sb, "wq"),
                                  (wv_sb, "wv")):
                    t_wA.append(ld(gps, dst[:],
                                   dt[srcn].rearrange("p a b -> p (a b)")))

                t_rope_prev = None
                t_store_prev = None
                t_vev_cons = [None] * 4

                for tb in range(NTB):
                    sl = tb % 2
                    t_x = rxs.load(gps, tb + b * NTB, lambda e, tb=tb, sl=sl, b=b:
                                   e.dma_start(
                        out=xs[sl][:].rearrange("p (a f) -> p a f", a=CCN),
                        in_=dt["x_tr"][b, :, tb * TBS:(tb + 1) * TBS]
                        .rearrange("(a p) f -> p a f", p=128)))
                    t_co = rcs.load(syn, 2 * tb + b, lambda e, tb=tb, sl=sl:
                                    e.dma_start(
                        out=cs[sl][:].rearrange("p (a f) -> p a f", a=4),
                        in_=dt["cos_t"][tb]))
                    t_si = rcs.load(syn, 2 * tb + 1 + b, lambda e, tb=tb, sl=sl:
                                    e.dma_start(
                        out=sn[sl][:].rearrange("p (a f) -> p a f", a=4),
                        in_=dt["sin_t"][tb]))

                    ev_kq = []
                    for which, wsb, braw, bias in ((0, wk_sb, kraw, bkc),
                                                   (1, wq_sb, qraw, bqc)):
                        for oc in range(CCN):
                            bank = (which * CCN + oc) % 2
                            pairs = [(wsb[:, cc * C + oc * 128:
                                          cc * C + (oc + 1) * 128],
                                      xs[sl][:, cc * TBS:(cc + 1) * TBS])
                                     for cc in range(CCN)]
                            tmm = mm_group(ps[bank][:], pairs,
                                           [t_x] + (t_wA if tb == 0 else []),
                                           bank)
                            tev = sca.emit(lambda e, braw=braw, oc=oc,
                                           bias=bias, bank=bank:
                                           e.activation(
                                out=braw[:, oc * TBS:(oc + 1) * TBS],
                                in_=ps[bank][:], func=AF.Identity,
                                bias=bias[:, oc:oc + 1]),
                                waits=[tmm, t_rope_prev, t_c[-1]],
                                inc=True)
                            ps_free[bank] = tev
                            ev_kq.append(tev)

                    t_rope = []
                    for src_raw, bout in ((kraw, ktf), (qraw, qtf)):
                        for p4 in range(4):
                            z0 = src_raw[:, p4 * TBS:(p4 + 1) * TBS]
                            z1 = src_raw[:, (p4 + 4) * TBS:(p4 + 5) * TBS]
                            cs_ = cs[sl][:, p4 * TBS:(p4 + 1) * TBS]
                            sn_ = sn[sl][:, p4 * TBS:(p4 + 1) * TBS]
                            w0 = [t_co, t_si, ev_kq[-1], t_store_prev]
                            vec.emit(lambda e, x=cs_, y=z0: e.tensor_mul(
                                mrope[0][:], x, y), waits=w0, w="m0")
                            vec.emit(lambda e, x=sn_, y=z1: e.tensor_mul(
                                mrope[1][:], x, y), w="m1")
                            vec.emit(lambda e, x=sn_, y=z0: e.tensor_mul(
                                mrope[2][:], x, y), w="m2")
                            vec.emit(lambda e, x=cs_, y=z1: e.tensor_mul(
                                mrope[3][:], x, y), w="m3")
                            vec.emit(lambda e,
                                     o=bout[:, p4 * TBS:(p4 + 1) * TBS]:
                                     e.tensor_add(o, mrope[0][:],
                                                  mrope[1][:]),
                                     w="r0", r=("m0", "m1"))
                            t_r = vec.emit(
                                lambda e,
                                o=bout[:, (p4 + 4) * TBS:(p4 + 5) * TBS]:
                                e.tensor_sub(o, mrope[3][:], mrope[2][:]),
                                inc=True, w="r1", r=("m2", "m3"))
                            t_rope.append(t_r)
                    t_rope_prev = t_rope[-1]
                    rcs.consumed(2 * tb + b, t_rope_prev)
                    rcs.consumed(2 * tb + 1 + b, t_rope_prev)

                    ksem = ktq_sems[tb % 2]
                    st_kq = []
                    for oc in range(CCN):
                        st_kq.append(syn.emit(
                            lambda e, oc=oc, tb=tb, b=b: e.dma_start(
                                out=dt["kt_s"][b, tb, :, oc, :],
                                in_=ktf[:, oc * TBS:(oc + 1) * TBS]),
                            waits=[t_rope[oc % 4]], inc=True, val=16,
                            sem=ksem))
                        st_kq.append(syn.emit(
                            lambda e, oc=oc, tb=tb, b=b: e.dma_start(
                                out=dt["qt_s"][b, 4 * tb:4 * tb + 4, :,
                                               oc * 128:(oc + 1) * 128]
                                .rearrange("j p f -> p j f"),
                                in_=qtf[:, oc * TBS:(oc + 1) * TBS]
                                .rearrange("p (j f) -> p j f", j=4)),
                            waits=[t_rope[4 + (oc % 4)]], inc=True, val=16,
                            sem=ksem))
                    t_store_prev = st_kq[-1]

                    for mh in range(2):
                        for tsub in range(4):
                            bank = 2 + (mh * 4 + tsub) % 2
                            vi = (mh * 4 + tsub) % 4
                            ten.emit(lambda e, mh=mh, bank=bank: e.matmul(
                                ps[bank][:], ones_r_b[:],
                                bvr[:, mh * 512:(mh + 1) * 512],
                                start=True, stop=False),
                                waits=[t_x, ps_free[bank]] +
                                ([t_wA[3], t_c[-1]] if tb == 0 else []))
                            for cc in range(CCN):
                                tmm = ten.emit(lambda e, cc=cc, mh=mh,
                                               tsub=tsub, bank=bank, sl=sl:
                                               e.matmul(
                                    ps[bank][:],
                                    xs[sl][:, cc * TBS + tsub * 128:
                                           cc * TBS + (tsub + 1) * 128],
                                    wv_sb[:, cc * C + mh * 512:
                                          cc * C + (mh + 1) * 512],
                                    start=False, stop=(cc == CCN - 1)),
                                    inc=(cc == CCN - 1))
                            tev = sca.emit(lambda e, vi=vi, bank=bank:
                                           e.activation(
                                out=vev[vi][:], in_=ps[bank][:],
                                func=AF.Identity),
                                waits=[tmm, t_vev_cons[vi]], inc=True)
                            ps_free[bank] = tev
                            t_st = syn.emit(
                                lambda e, tb=tb, tsub=tsub, mh=mh, vi=vi, b=b:
                                e.dma_start(
                                    out=dt["v_s"][b,
                                                  tb * TBS + tsub * 128:
                                                  tb * TBS +
                                                  (tsub + 1) * 128,
                                                  mh * 512:(mh + 1) * 512],
                                    in_=vev[vi][:]),
                                waits=[tev], inc=True, val=16,
                                sem=v_sems[vi])
                            t_vev_cons[vi] = t_st
                    rxs.consumed(tb + b * NTB, ten.tick())

                t_scr_done = (
                    [(s, P.sem_cnt[s.name]) for s in ktq_sems] +
                    [(s, P.sem_cnt[s.name]) for s in v_sems])

            P.scope = f"b{b}A2"
            P.barrier()
            if True:
                atsb = Bump(bump_a.off).maker()
                masks_a = atsb("masks", (128, 4, TBS), F32)
                st_sb = atsb("st", (128, 32 * TBS), BF16)
                kis = [atsb(f"kis{i}", (128, CCN * TBS), BF16)
                       for i in range(2)]
                qjs = [atsb(f"qjs{i}", (128, CCN * 128), BF16)
                       for i in range(4)]
                vjs = [atsb(f"vjs{i}", (128, C), BF16) for i in range(4)]
                mske = [atsb(f"mske{i}", (128, TBS), F32) for i in range(2)]
                esb = [atsb(f"esb{i}", (128, TBS), F32) for i in range(2)]
                ot_sb = atsb("ot", (128, CCN * TBS), BF16)
                xnt = [atsb(f"xnt{i}", (128, C), F32) for i in range(4)]
                yt = [atsb(f"yt{i}", (128, C), F32) for i in range(4)]
                x1t = [atsb(f"x1t{i}", (128, C), BF16) for i in range(4)]
                scrj = atsb("scrj", (128, C), BF16)

                t_msk = ld(syn, masks_a[:], dt["masks"])

                t_mske_cons = [None] * 2
                t_yt_cons = [None] * 4
                t_x1_cons = [[] for _ in range(4)]
                t_mvt_cons = [None] * 4

                pending_wp = None
                for I in range(NTB):
                    ksl = I % 2
                    t_ki = rki.load(gps, I, lambda e, I=I, ksl=ksl, b=b:
                                    e.dma_start(
                        out=kis[ksl][:].rearrange("p (a f) -> p a f", a=CCN),
                        in_=dt["kt_s"][b, I]), waits=t_scr_done)
                    NJ = 4 * I + 4
                    t_st = []
                    for j in range(NJ):
                        qsl = j % 4
                        t_qj = rqj.load(syn, j, lambda e, j=j, qsl=qsl, b=b:
                                        e.dma_start(
                            out=qjs[qsl][:]
                            .rearrange("p (a f) -> p a f", a=CCN),
                            in_=dt["qt_s"][b, j]
                            .rearrange("p (a f) -> p a f", a=CCN)),
                            waits=t_scr_done)
                        bank = j % 2
                        pairs = [(qjs[qsl][:, cc * 128:(cc + 1) * 128],
                                  kis[ksl][:, cc * TBS:(cc + 1) * TBS])
                                 for cc in range(CCN)]
                        tmm = mm_group(ps[bank][:], pairs, [t_qj, t_ki],
                                       bank)
                        rqj.consumed(j, tmm)
                        esl = j % 2
                        if j >= 4 * I:
                            a = j - 4 * I
                            t_m = vec.emit(lambda e, a=a, bank=bank, esl=esl:
                                           e.scalar_tensor_tensor(
                                out=mske[esl][:], in0=ps[bank][:],
                                scalar=ISCALE, in1=masks_a[:, a, :],
                                op0=ALU.mult, op1=ALU.add),
                                waits=[tmm, t_mske_cons[esl], t_msk],
                                inc=True, w=f"mske{esl}")
                            ps_free[bank] = t_m
                            t_e = sca.emit(lambda e, esl=esl: e.activation(
                                out=esb[esl][:], in_=mske[esl][:],
                                func=AF.Exp), waits=[t_m], inc=True)
                            t_mske_cons[esl] = t_e
                        else:
                            t_e = sca.emit(lambda e, bank=bank, esl=esl:
                                           e.activation(
                                out=esb[esl][:], in_=ps[bank][:],
                                func=AF.Exp, scale=ISCALE),
                                waits=[tmm], inc=True)
                            ps_free[bank] = t_e
                        t_l = sca.emit(lambda e, j=j, esl=esl: e.activation(
                            out=st_sb[:, j * TBS:(j + 1) * TBS],
                            in_=esb[esl][:], func=AF.Ln, bias=b1col[:]),
                            inc=True)
                        t_st.append(t_l)

                    rki.consumed(I, ten.tick())
                    if dbg and b == 0 and I == 1:
                        gps.emit(lambda e: e.dma_start(
                            out=dt["dbg_st"],
                            in_=st_sb[:, 0:8 * TBS]),
                            waits=[t_st[-1]], inc=True, val=16)

                    if pending_wp is not None:
                        pending_wp()
                        pending_wp = None

                    for j in range(NJ):
                        vsl = j % 4
                        t_vj = rvj.load(sca, j, lambda e, j=j, vsl=vsl, b=b:
                                        e.dma_start(
                            out=vjs[vsl][:],
                            in_=dt["v_s"][b, j * 128:(j + 1) * 128, :]),
                            waits=t_scr_done)
                        for d in range(CCN):
                            w0 = []
                            if j == 0:
                                w0 = [t_vj, t_st[-1], ps_free[d]]
                            elif d == 0:
                                w0 = [t_vj, t_st[j]]
                            tmm = ten.emit(lambda e, j=j, d=d, vsl=vsl, NJ=NJ:
                                           e.matmul(
                                ps[d][:],
                                vjs[vsl][:, d * 128:(d + 1) * 128],
                                st_sb[:, j * TBS:(j + 1) * TBS],
                                start=(j == 0), stop=(j == NJ - 1)),
                                waits=w0, inc=(d == CCN - 1))
                        rvj.consumed(j, tmm)
                    t_ot_mm = ten.tick()
                    t_ote = []
                    for d in range(CCN):
                        t_ev = sca.emit(lambda e, d=d: e.activation(
                            out=ot_sb[:, d * TBS:(d + 1) * TBS],
                            in_=ps[d][:], func=AF.Identity),
                            waits=[t_ot_mm], inc=True)
                        ps_free[d] = t_ev
                        t_ote.append(t_ev)

                    if dbg and b == 0 and I == 1:
                        gps.emit(lambda e: e.dma_start(
                            out=dt["dbg_ot"], in_=ot_sb[:]),
                            waits=[t_ote[-1]], inc=True, val=16)
                    def _wp(I=I, t_ote=t_ote, b=b):
                        t_y = [None] * 4
                        for isub in range(4):
                            gidx = I * 4 + isub
                            t_xn = rxn.load(sca, gidx,
                                            lambda e, gidx=gidx, isub=isub, b=b:
                                            e.dma_start(
                                out=xnt[isub][:],
                                in_=dt["x_nat"][b,
                                                gidx * 128:(gidx + 1) * 128,
                                                :]))
                            for half in range(2):
                                bank = 6 + (isub * 2 + half) % 2
                                ten.emit(lambda e, half=half, bank=bank:
                                         e.matmul(
                                    ps[bank][:], ones_r_b[:],
                                    bpr[:, half * 512:(half + 1) * 512],
                                    start=True, stop=False),
                                    waits=[ps_free[bank], t_wp,
                                           t_ote[-1]])
                                for mc in range(CCN):
                                    tmm = ten.emit(lambda e, mc=mc, isub=isub,
                                                   half=half, bank=bank:
                                                   e.matmul(
                                        ps[bank][:],
                                        ot_sb[:, mc * TBS + isub * 128:
                                              mc * TBS + (isub + 1) * 128],
                                        wp_sb[:, mc * C + half * 512:
                                              mc * C + (half + 1) * 512],
                                        start=False, stop=(mc == CCN - 1)),
                                        inc=(mc == CCN - 1))
                                t_yh = vec.emit(lambda e, isub=isub, half=half,
                                                bank=bank:
                                                e.scalar_tensor_tensor(
                                    out=yt[isub][:,
                                                 half * 512:(half + 1) * 512],
                                    in0=ps[bank][:], scalar=1.0,
                                    in1=xnt[isub][:,
                                                  half * 512:(half + 1) * 512],
                                    op0=ALU.mult, op1=ALU.add),
                                    waits=[tmm, t_xn, t_yt_cons[isub]],
                                    inc=True, w=f"y{isub}h{half}")
                                ps_free[bank] = t_yh
                                t_y[isub] = t_yh
                            rxn.consumed(gidx, t_y[isub])

                        for isub in range(4):
                            vec.emit(lambda e, i=isub: e.bn_stats(
                                stats[:, i * 12:i * 12 + 6], yt[i][:, 0:512]),
                                waits=[t_y[isub]], w=f"bnsa{isub}",
                                r=(f"y{isub}h0", f"y{isub}h1"))
                            vec.emit(lambda e, i=isub: e.bn_stats(
                                stats[:, i * 12 + 6:i * 12 + 12],
                                yt[i][:, 512:1024]), w=f"bnsb{isub}")
                        t_bn = [None] * 4
                        for isub in range(4):
                            t_bn[isub] = vec.emit(lambda e, i=isub: e.bn_aggr(
                                mvt[:, i * 2:i * 2 + 2],
                                stats[:, i * 12:i * 12 + 12]),
                                waits=[t_mvt_cons[isub]],
                                inc=True, w=f"mv{isub}",
                                r=(f"bnsa{isub}", f"bnsb{isub}"))
                        for isub in range(4):
                            gidx = I * 4 + isub
                            rcol = lncol[:, isub * 4:isub * 4 + 1]
                            nmr = lncol[:, isub * 4 + 1:isub * 4 + 2]
                            t2c = lncol[:, isub * 4 + 2:isub * 4 + 3]
                            lnv = lncol[:, isub * 4 + 3:isub * 4 + 4]
                            sca.emit(lambda e, i=isub, lnv=lnv: e.activation(
                                out=lnv, in_=mvt[:, i * 2 + 1:i * 2 + 2],
                                func=AF.Ln, bias=epsln[:]),
                                waits=[t_bn[isub]])
                            t_r = sca.emit(lambda e, rcol=rcol, lnv=lnv:
                                           e.activation(
                                out=rcol, in_=lnv, func=AF.Exp, scale=-0.5),
                                inc=True)
                            t_mvt_cons[isub] = t_r
                            vec.emit(lambda e, t2c=t2c, rcol=rcol: e.tensor_mul(
                                t2c, rcol, rcol), waits=[t_r], w=f"t2{isub}")
                            vec.emit(lambda e, i=isub, t2c=t2c: e.tensor_mul(
                                t2c, mvt[:, i * 2 + 1:i * 2 + 2], t2c),
                                w=f"t2b{isub}", r=(f"t2{isub}",))
                            vec.emit(lambda e, i=isub, nmr=nmr, rcol=rcol:
                                     e.tensor_mul(
                                nmr, mvt[:, i * 2:i * 2 + 1], rcol),
                                w=f"nm{isub}")
                            t_nmr = vec.emit(lambda e, nmr=nmr:
                                             e.tensor_scalar_mul(
                                nmr, nmr, -1.0), inc=True, w=f"nm2{isub}",
                                r=(f"nm{isub}",))
                            sca.emit(lambda e, t2c=t2c, lnv=lnv: e.activation(
                                out=lnv, in_=t2c, func=AF.Ln, bias=epsln4[:]),
                                waits=[t_nmr])
                            t_s = sca.emit(lambda e, gidx=gidx, lnv=lnv:
                                           e.activation(
                                out=smat[:, gidx:gidx + 1], in_=lnv,
                                func=AF.Exp, scale=-0.5), inc=True)
                            t_x1 = sca.emit(lambda e, i=isub, rcol=rcol,
                                            nmr=nmr: e.activation(
                                out=x1t[i][:], in_=yt[i][:], func=AF.Identity,
                                scale=rcol, bias=nmr),
                                waits=list(t_x1_cons[isub]), inc=True)
                            t_yt_cons[isub] = t_x1
                            t_sc = vec.emit(lambda e, i=isub, gidx=gidx:
                                            e.scalar_tensor_tensor(
                                out=scrj[:], in0=x1t[i][:], scalar=1.0,
                                in1=wrrep[:], op0=ALU.mult, op1=ALU.mult,
                                accum_out=scoremat[:, gidx:gidx + 1]),
                                waits=[t_x1, t_grep, t_eps], inc=True, w="scr")
                            t_w1s = syn.emit(lambda e, i=isub, gidx=gidx, b=b:
                                             e.dma_start(
                                out=dt["x1_s"][b,
                                               gidx * 128:(gidx + 1) * 128, :],
                                in_=x1t[i][:]), waits=[t_x1], inc=True, val=16,
                                sem=x1_sems[isub])
                            t_x1_cons[isub] = [t_sc, t_w1s]
                    pending_wp = _wp

                pending_wp()

                t_x1_done = [(s, P.sem_cnt[s.name]) for s in x1_sems]
                t_score_done = vec.tick()
                t_smat_done = sca.tick()

        P.scope = f"b{b}SEL"
        P.barrier()
        if True:
            bump_f = Bump(0)
            fsb = bump_f.maker()
            xslot = [fsb(f"xslot{a}", (128, C), BF16) for a in range(4)]
            bisdmp = fsb("bisdmp", (128, 4 * NBIS), F32)

            P.scope = f"b{b}SELbis"
            vec.pad = True
            t_abs = vec.emit(lambda e: e.tensor_reduce(
                abscol[:], scoremat[:], AX.X, ALU.add,
                apply_absolute_value=True),
                waits=[t_score_done], inc=True, w="abs")
            t_bnd = mm_group(ps[7][:, 0:1], [(ones_sq_f[:], abscol[:])],
                             [t_abs], 7)
            vec.emit(lambda e: e.tensor_scalar_mul(
                locol[:], ps[7][:, 0:1], -1.0), waits=[t_bnd], w="lo")
            t_bis = vec.emit(lambda e: e.tensor_copy(hicol[:],
                                                     ps[7][:, 0:1]),
                             inc=True, w="hi")
            ps_free[7] = t_bis
            for it in range(NBIS):
                vec.emit(lambda e: e.tensor_add(midA[:], locol[:],
                                                hicol[:]),
                         waits=[t_bis], w="midA", r=("lo", "hi"))
                vec.emit(lambda e: e.tensor_scalar_mul(midB[:], midA[:],
                                                       0.5),
                         w="midB", r=("midA",))
                vec.emit(lambda e: e.tensor_single_scalar(
                    cmp32[:], scoremat[:], midB[:], ALU.is_gt),
                    w="cmp", r=("midB",))
                t_cc = vec.emit(lambda e: e.tensor_reduce(
                    cntcol[:], cmp32[:], AX.X, ALU.add),
                    inc=True, w="cnt", r=("cmp",))
                t_tot = mm_group(ps[7][:, 0:1],
                                 [(ones_sq_f[:], cntcol[:])], [t_cc], 7)
                vec.emit(lambda e: e.tensor_single_scalar(
                    cge[:], ps[7][:, 0:1], 512.0, ALU.is_ge),
                    waits=[t_tot], w="cge")
                vec.emit(lambda e: e.tensor_single_scalar(
                    clt[:], ps[7][:, 0:1], 512.0, ALU.is_lt), w="clt")
                vec.emit(lambda e: e.select(
                    locol[:], cge[:], midB[:], locol[:]),
                    w="lo", r=("cge",))
                t_bis = vec.emit(lambda e: e.select(
                    hicol[:], clt[:], midB[:], hicol[:]),
                    inc=True, w="hi", r=("clt",))
                ps_free[7] = t_bis
                if dbg:
                    vec.emit(lambda e, it=it: e.tensor_copy(
                        bisdmp[:, 4 * it:4 * it + 1], cntcol[:]), w="db0")
                    vec.emit(lambda e, it=it: e.tensor_copy(
                        bisdmp[:, 4 * it + 1:4 * it + 2], ps[7][:, 0:1]),
                        w="db1")
                    vec.emit(lambda e, it=it: e.tensor_copy(
                        bisdmp[:, 4 * it + 2:4 * it + 3], locol[:]),
                        w="db2")
                    t_bis = vec.emit(lambda e, it=it: e.tensor_copy(
                        bisdmp[:, 4 * it + 3:4 * it + 4], hicol[:]),
                        inc=True, w="db3")

            P.scope = f"b{b}SELpfx"
            t_sel = vec.emit(lambda e: e.tensor_single_scalar(
                selm[:], scoremat[:], locol[:], ALU.is_gt),
                inc=True, w="selm", r=("lo",))
            if dbg:
                gps.emit(lambda e, b=b: e.dma_start(
                    out=dt["dbg_bis"][b], in_=bisdmp[:]),
                    waits=[t_sel], inc=True, val=16)
            if dbg:
                gps.emit(lambda e, b=b: e.dma_start(
                    out=dt["dbg_selm"][b], in_=selm[:]),
                    waits=[t_sel], inc=True, val=16)
                gps.emit(lambda e, b=b: e.dma_start(
                    out=dt["dbg_lo"][b], in_=locol[:]),
                    waits=[t_sel], inc=True, val=16)
            t_tr1 = ten.emit(lambda e: e.transpose(
                ps[6][0:32, 0:128], selm[:], identf[:]),
                waits=[t_sel, ps_free[6]], inc=True)
            t_selt = vec.emit(lambda e: e.tensor_copy(
                selt[:], ps[6][0:32, 0:128]), waits=[t_tr1], inc=True,
                w="selt")
            ps_free[6] = t_selt
            if dbg:
                gps.emit(lambda e, b=b: e.dma_start(
                    out=dt["dbg_selt"][b], in_=selt[:]),
                    waits=[t_selt], inc=True, val=16)
            cur, nxt = pret[0], pret[1]
            t_pf = vec.emit(lambda e, cc_=cur: e.tensor_copy(cc_[:], selt[:]),
                            waits=[t_selt], inc=True, w="pf1")
            for sh in (1, 2, 4, 8, 16, 32, 64):
                vec.emit(lambda e, sh=sh, cc_=cur, nn=nxt: e.tensor_copy(
                    nn[:, 0:sh], cc_[:, 0:sh]), w="pfc", r=("pf1", "pfa"))
                t_pf = vec.emit(lambda e, sh=sh, cc_=cur, nn=nxt:
                                e.tensor_add(
                    nn[:, sh:128], cc_[:, sh:128], cc_[:, 0:128 - sh]),
                    inc=True, w="pfa", r=("pfc",))
                cur, nxt = nxt, cur
            t_pos = vec.emit(lambda e, cc_=cur: e.tensor_sub(
                post_t[:], cc_[:], selt[:]), waits=[t_pf], inc=True,
                w="pos", r=("pfa",))
            if dbg:
                gps.emit(lambda e, b=b: e.dma_start(
                    out=dt["dbg_pos"][b], in_=post_t[:]),
                    waits=[t_pos], inc=True, val=16)
            vec.emit(lambda e: e.memset(rt32[:], 0.0), w="rt0")
            vec.emit(lambda e, cc_=cur: e.tensor_copy(
                rt32[:, 0:1], cc_[:, 127:128]), w="rt", r=("rt0",))
            t_rt = vec.emit(lambda e: e.transpose(rr32[:], rt32[:]),
                            inc=True, w="rr", r=("rt",))
            e0, e1 = er32[0], er32[1]
            vec.emit(lambda e, a=e0: e.memset(a[:], 0.0), w="e0")
            vec.emit(lambda e, a=e0: e.tensor_copy(
                a[:, 1:32], rr32[0:1, 0:31]), waits=[t_rt], w="e0b",
                r=("rr", "e0"))
            for sh in (1, 2, 4, 8, 16):
                vec.emit(lambda e, sh=sh, a=e0, bb=e1: e.tensor_copy(
                    bb[:, 0:sh], a[:, 0:sh]), w="epc", r=("e0b", "epa"))
                vec.emit(lambda e, sh=sh, a=e0, bb=e1: e.tensor_add(
                    bb[:, sh:32], a[:, sh:32], a[:, 0:32 - sh]),
                    w="epa", r=("epc",))
                e0, e1 = e1, e0
            vec.emit(lambda e: e.memset(rr32[:], 0.0), w="rrz",
                     r=("epa",))
            vec.emit(lambda e, a=e0: e.tensor_copy(rr32[0:1, :], a[:]),
                     w="rr2", r=("rrz", "epa"))
            t_off = vec.emit(lambda e: e.transpose(rt32[:], rr32[:]),
                             inc=True, w="off", r=("rr2",))
            vec.emit(lambda e: e.tensor_copy(offc[:], rt32[:, 0:1]),
                     waits=[t_off], w="offc", r=("off",))
            vec.emit(lambda e: e.tensor_single_scalar(
                slott[:], post_t[:], offc[:], ALU.add),
                w="slott", r=("offc", "pos"))
            vec.emit(lambda e: e.tensor_single_scalar(
                nselu[:], selt[:], 0.5, ALU.is_lt), w="nsel")
            t_slot = vec.emit(lambda e: e.select(
                slott[:], nselu[:], c512t[:], slott[:]),
                inc=True, w="slott2", r=("nsel", "slott"))
            if dbg:
                gps.emit(lambda e, b=b: e.dma_start(
                    out=dt["dbg_offc"][b], in_=offc[:]),
                    waits=[t_slot], inc=True, val=16)
                gps.emit(lambda e, b=b: e.dma_start(
                    out=dt["dbg_slott"][b], in_=slott[:]),
                    waits=[t_slot], inc=True, val=16)
            t_tr2 = ten.emit(lambda e: e.transpose(
                ps[6][0:128, 0:32], slott[:], identf[0:32, 0:32]),
                waits=[t_slot, ps_free[6]], inc=True)
            t_slotc = vec.emit(lambda e: e.tensor_copy(
                slotc[:], ps[6][0:128, 0:32]), waits=[t_tr2], inc=True,
                w="slotc")
            ps_free[6] = t_slotc

            def build_CA(CA_list, waits0, mksb):
                t_out = [None] * 4
                srep = mksb("srepscr", (128, 512), F32)
                t_s_cons = None
                pad_sav = vec.pad
                vec.pad = True
                for kk in range(8):
                    t_row = syn.emit(lambda e, kk=kk: e.dma_start(
                        out=rcflat[0:1, 0:512]
                        .rearrange("o (c p) -> o c p", c=4),
                        in_=slott[4 * kk:4 * kk + 4, :]),
                        waits=waits0 + [t_s_cons], inc=True, val=16)
                    t_mm = mm_group(ps[6][:],
                                    [(ones_r_f[:], rcflat[0:1, 0:512])],
                                    [t_row], 6)
                    t_se = vec.emit(lambda e: e.tensor_copy(
                        srep[:], ps[6][:]), waits=[t_mm], inc=True,
                        w="srep")
                    ps_free[6] = t_se
                    t_s_cons = t_mm
                    for a in range(4):
                        t_out[a] = vec.emit(lambda e, a=a, kk=kk:
                                            e.tensor_single_scalar(
                            CA_list[a][:, kk * 512:(kk + 1) * 512],
                            srep[:], iotacols[:, a:a + 1], ALU.is_equal),
                            waits=[t_se], inc=True, w=f"cab{a}",
                            r=("srep",))
                    t_s_cons = t_out[3]
                vec.pad = pad_sav
                return t_out

            P.scope = f"b{b}SELf1"
            if True:
                f1sb = Bump(bump_f.off).maker()
                tri_f = f1sb("tri", (128, 4, TBS), F32)
                slotio = f1sb("slotio", (128, 512), F32)
                CT = [f1sb(f"CT{c}", (128, 512), BF16) for c in range(32)]
                x1c = [f1sb(f"x1c{i}", (128, C), BF16) for i in range(4)]
                gtt = f1sb("gtt", (128, 512), F32)
                eqt = f1sb("eqt", (128, 512), F32)
                eqm = f1sb("eqm", (128, 512), F32)
                screp = f1sb("screp", (128, 512), F32)

                t_tri = ld(syn, tri_f[:], dt["tri"])
                t_sio = ld(syn, slotio[:], dt["slotiota"])
                t_trisio = [t_tri, t_sio]

                t_CT = [None] * 32
                for c in range(32):
                    t_CT[c] = vec.emit(lambda e, c=c:
                                       e.tensor_single_scalar(
                        CT[c][:], slotio[:], slotc[:, c:c + 1],
                        ALU.is_equal),
                        waits=[t_slotc] + t_trisio, inc=True, w=f"ct{c}")

                t_scb = vec.emit(lambda e: e.tensor_copy(scb[:],
                                                         scoremat[:]),
                                 waits=[t_score_done], inc=True, w="scb")
                t_ssb = vec.emit(lambda e: e.tensor_copy(ssb[:], smat[:]),
                                 waits=[t_smat_done], inc=True, w="ssb")

                t_cols = []
                for (srcb, dstcol, tsrc) in ((scb, sccol, t_scb),
                                             (ssb, sscol, t_ssb)):
                    tmm = None
                    for a in range(4):
                        pairs = [(CT[c][:, a * 128:(a + 1) * 128],
                                  srcb[:, c:c + 1]) for c in range(32)]
                        tmm = mm_group(ps[6][:, a:a + 1], pairs,
                                       [tsrc] + t_CT, 6, skip_check=True)
                    t_cv = vec.emit(lambda e, dstcol=dstcol: e.tensor_copy(
                        dstcol[:], ps[6][:, 0:4]), waits=[tmm], inc=True,
                        w="cols")
                    ps_free[6] = t_cv
                    t_cols.append(t_cv)
                t_nss = vec.emit(lambda e: e.tensor_scalar_mul(
                    nsscol[:], sscol[:], -1.0), waits=[t_cols[1]],
                    inc=True, w="nss", r=("cols",))

                def row_replicate(src4, t_src, dstrep, extra_w=()):
                    t1 = vec.emit(lambda e: e.tensor_copy(cmp32[:, 0:4],
                                                          src4[:]),
                                  waits=[t_src] + list(extra_w), inc=True,
                                  w="rrc", r=("cols", "nss"))
                    t2 = ten.emit(lambda e: e.transpose(
                        ps[6][0:32, 0:128], cmp32[:, 0:32], identf[:]),
                        waits=[t1, ps_free[6]], inc=True)
                    t3 = vec.emit(lambda e: e.tensor_copy(
                        scr32[:], ps[6][0:32, 0:128]), waits=[t2],
                        inc=True, w="rrd")
                    ps_free[6] = t3
                    t4 = syn.emit(lambda e: e.dma_start(
                        out=rcflat[0:1, 0:512]
                        .rearrange("o (a p) -> o a p", a=4),
                        in_=scr32[0:4, :]),
                        waits=[t3], inc=True, val=16)
                    t5 = mm_group(ps[6][:],
                                  [(ones_r_f[:], rcflat[0:1, 0:512])],
                                  [t4], 6)
                    t6 = vec.emit(lambda e: e.tensor_copy(dstrep[:],
                                                          ps[6][:]),
                                  waits=[t5], inc=True, w="rre")
                    ps_free[6] = t6
                    return t6

                t_screp = row_replicate(sccol, t_cols[0], screp)
                t_rank = [None] * 4
                for a in range(4):
                    vec.emit(lambda e, a=a: e.tensor_single_scalar(
                        gtt[:], screp[:], sccol[:, a:a + 1], ALU.is_gt),
                        waits=[t_screp, t_cols[0]] + t_trisio, w="gtt",
                        r=("rre",))
                    vec.emit(lambda e, a=a: e.tensor_single_scalar(
                        eqt[:], screp[:], sccol[:, a:a + 1],
                        ALU.is_equal), w="eqt")
                    vec.emit(lambda e, a=a: e.tensor_mul(
                        eqm[:], eqt[:], tri_f[:, a, :]), w="eqm",
                        r=("eqt",))
                    vec.emit(lambda e: e.tensor_add(eqm[:], eqm[:],
                                                    gtt[:]),
                             w="eqm2", r=("eqm", "gtt"))
                    t_rank[a] = vec.emit(lambda e, a=a: e.tensor_reduce(
                        rankc[:, a:a + 1], eqm[:], AX.X, ALU.add),
                        inc=True, w=f"rk{a}", r=("eqm2",))
                t_rkrep = row_replicate(rankc, t_rank[-1], rankrep,
                                        extra_w=t_rank)
                t_minn = [None] * 4
                for a in range(4):
                    t_minn[a] = vec.emit(lambda e, a=a:
                                         e.tensor_single_scalar(
                        minn[a][:], rankrep[:], rankc[:, a:a + 1],
                        ALU.is_ge),
                        waits=[t_rkrep], inc=True, w=f"mi{a}", r=("rre",))
                vec.pad = False

                if dbg:
                    syn.emit(lambda e, b=b: e.dma_start(
                        out=dt["dbg_score"][b], in_=scoremat[:]),
                        waits=[t_score_done], inc=True, val=16)
                    syn.emit(lambda e, b=b: e.dma_start(
                        out=dt["dbg_smat"][b], in_=smat[:]),
                        waits=[t_smat_done], inc=True, val=16)
                    syn.emit(lambda e, b=b: e.dma_start(
                        out=dt["dbg_slot"][b], in_=slotc[:]),
                        waits=[t_slotc], inc=True, val=16)
                    syn.emit(lambda e, b=b: e.dma_start(
                        out=dt["dbg_rank"][b], in_=rankc[:]),
                        waits=t_rank, inc=True, val=16)

                P.scope = f"b{b}SELgat"
                t_xslot = [None] * 4
                for hp in range(2):
                    tmm_l = [[None, None], [None, None]]
                    for c in range(32):
                        csl = c % 4
                        t_xc = rx1.load(sca, c + 32 * hp,
                                        lambda e, c=c, csl=csl, b=b:
                                        e.dma_start(
                            out=x1c[csl][:],
                            in_=dt["x1_s"][b, c * 128:(c + 1) * 128, :]),
                            waits=t_x1_done)
                        for ai in range(2):
                            a = hp * 2 + ai
                            for half in range(2):
                                bank = ai * 2 + half
                                w0 = []
                                if c == 0:
                                    w0 = [t_xc, t_CT[c], ps_free[bank]]
                                elif ai == 0 and half == 0:
                                    w0 = [t_xc, t_CT[c]]
                                tmm_l[ai][half] = ten.emit(
                                    lambda e, c=c, a=a, half=half, csl=csl,
                                    bank=bank: e.matmul(
                                        ps[bank][:],
                                        CT[c][:, a * 128:(a + 1) * 128],
                                        x1c[csl][:, half * 512:
                                                 (half + 1) * 512],
                                        start=(c == 0), stop=(c == 31)),
                                    waits=w0,
                                    inc=(c == 31 or (ai == 1 and
                                                     half == 1)))
                        rx1.consumed(c + 32 * hp, tmm_l[1][1])
                    for ai in range(2):
                        a = hp * 2 + ai
                        t_xslot[a] = sca.emit(lambda e, a=a, ai=ai: (
                            e.activation(out=xslot[a][:, 0:512],
                                         in_=ps[ai * 2][:],
                                         func=AF.Identity),
                            e.activation(out=xslot[a][:, 512:1024],
                                         in_=ps[ai * 2 + 1][:],
                                         func=AF.Identity))[-1],
                            waits=[tmm_l[ai][0], tmm_l[ai][1]], inc=True)
                        ps_free[ai * 2] = t_xslot[a]
                        ps_free[ai * 2 + 1] = t_xslot[a]

            P.scope = f"b{b}F2a"
            P.barrier()
            if True:
                bump_f2 = Bump(bump_f.off)
                f2sb = bump_f2.maker()
                hf = [f2sb(f"hf{i}", (128, C), F32) for i in range(2)]
                fjunk = f2sb("fjunk", (128, C), F32)
                flnout = [f2sb(f"flnout{i}", (128, C), F32)
                          for i in range(2)]
                htb = f2sb("htb", (128, CCN * 512), BF16)
                s2 = [f2sb(f"s2_{a}", (128, C), BF16) for a in range(4)]

                bump_f2a = Bump(bump_f2.off)
                f2asb = bump_f2a.maker()
                qkT = f2asb("qkT", (128, 16 * 512), BF16)
                vnat = [f2asb(f"vnat{a}", (128, C), BF16)
                        for a in range(4)]
                expd = [f2asb(f"expd{i}", (128, 512), BF16)
                        for i in range(4)]
                esbf = [f2asb(f"esbf{i}", (128, 512), F32)
                        for i in range(2)]
                rcrep = [f2asb(f"rcrep{i}", (128, 512), BF16)
                         for i in range(4)]
                ot2 = f2asb("ot2", (128, CCN * 512), BF16)
                wvh = [f2asb(f"wvh{i}", (128, CCN * 512), BF16)
                       for i in range(2)]
                wqk = [f2asb(f"wqk{i}", (128, C), BF16) for i in range(2)]


                t_hf_cons = [None] * 2
                t_hTe = None
                for a in range(4):
                    rcol = lncol[:, a * 4:a * 4 + 1]
                    lnv = lncol[:, a * 4 + 1:a * 4 + 2]
                    t_sq = vec.emit(lambda e, a=a: e.scalar_tensor_tensor(
                        out=fjunk[:], in0=xslot[a][:], scalar=1.0,
                        in1=xslot[a][:], op0=ALU.mult, op1=ALU.mult,
                        accum_out=lncol[:, a * 4 + 2:a * 4 + 3]),
                        waits=[t_xslot[a]], inc=True, w="hsq")
                    sca.emit(lambda e, a=a, lnv=lnv: e.activation(
                        out=lnv, in_=lncol[:, a * 4 + 2:a * 4 + 3],
                        func=AF.Ln, scale=1.0 / C, bias=epsrms[:]),
                        waits=[t_sq])
                    t_r = sca.emit(lambda e, rcol=rcol, lnv=lnv:
                                   e.activation(
                        out=rcol, in_=lnv, func=AF.Exp, scale=-0.5),
                        inc=True)
                    t_h = vec.emit(lambda e, a=a, rcol=rcol:
                                   e.scalar_tensor_tensor(
                        out=hf[a % 2][:], in0=xslot[a][:], scalar=rcol,
                        op0=ALU.mult, in1=g1rep[:], op1=ALU.mult),
                        waits=[t_r, t_hf_cons[a % 2]], inc=True,
                        w=f"hf{a % 2}")
                    t_tr = None
                    for cc in range(CCN):
                        bank = 6 + (cc % 2)
                        t_tr = ten.emit(lambda e, a=a, cc=cc, bank=bank:
                                        e.transpose(
                            ps[bank][0:128, 0:128],
                            hf[a % 2][:, cc * 128:(cc + 1) * 128],
                            identf[:]),
                            waits=[t_h, ps_free[bank]], inc=True)
                        t_hTe = sca.emit(lambda e, a=a, cc=cc, bank=bank:
                                         e.activation(
                            out=htb[:, cc * 512 + a * 128:
                                    cc * 512 + (a + 1) * 128],
                            in_=ps[bank][0:128, 0:128], func=AF.Identity),
                            waits=[t_tr], inc=True)
                        ps_free[bank] = t_hTe
                    t_hf_cons[a % 2] = t_tr
                t_hT_done = t_hTe

                t_qk = [None] * 16
                for mc in range(16):
                    wsl = mc % 2
                    t_w = rwq.load(gps, mc, lambda e, mc=mc, wsl=wsl:
                                   e.dma_start(
                        out=wqk[wsl][:]
                        .rearrange("p (a f) -> p a f", a=CCN),
                        in_=dt["wqkv_t"][mc]))
                    bank = mc % 2
                    pairs = [(wqk[wsl][:, cc * 128:(cc + 1) * 128],
                              htb[:, cc * 512:(cc + 1) * 512])
                             for cc in range(CCN)]
                    tmm = mm_group(ps[bank][:], pairs, [t_w, t_hT_done],
                                   bank)
                    rwq.consumed(mc, tmm)
                    t_qk[mc] = sca.emit(lambda e, mc=mc, bank=bank:
                                        e.activation(
                        out=qkT[:, mc * 512:(mc + 1) * 512],
                        in_=ps[bank][:], func=AF.Identity),
                        waits=[tmm], inc=True)
                    ps_free[bank] = t_qk[mc]

                t_vn = [None] * 4
                t_vn_mm = None
                for mh in range(2):
                    t_wv = rwv.load(gps, mh, lambda e, mh=mh: e.dma_start(
                        out=wvh[mh % 2][:]
                        .rearrange("p (a f) -> p a f", a=CCN),
                        in_=dt["wv2"][:, :, mh * 512:(mh + 1) * 512]
                        .rearrange("p a f -> p a f")))
                    for a in range(4):
                        bank = 2 + a % 2
                        pairs = [(htb[:, cc * 512 + a * 128:
                                      cc * 512 + (a + 1) * 128],
                                  wvh[mh % 2][:, cc * 512:(cc + 1) * 512])
                                 for cc in range(CCN)]
                        t_vn_mm = mm_group(ps[bank][:], pairs,
                                           [t_wv, t_hT_done], bank)
                        t_vn[a] = sca.emit(lambda e, a=a, mh=mh, bank=bank:
                                           e.activation(
                            out=vnat[a][:, mh * 512:(mh + 1) * 512],
                            in_=ps[bank][:], func=AF.Identity),
                            waits=[t_vn_mm], inc=True)
                        ps_free[bank] = t_vn[a]
                    rwv.consumed(mh, t_vn_mm)

                t_expd_cons = [None] * 4
                t_esbf_cons = [None] * 2
                t_rcflat_cons = None
                t_ot2e = []
                for g in range(2):
                    t_ote_g = []
                    t_rc_g = []
                    t_cs_last = None
                    for hh in range(8):
                        h = g * 8 + hh
                        qmc, qoff = h // 2, 64 * (h % 2)
                        kmc = 8 + h // 2
                        t_ex = [None] * 4
                        for tkc in range(4):
                            bank = 4 + (tkc % 2)
                            pairs = [(qkT[qoff:qoff + 64,
                                          kmc * 512 + tkc * 128:
                                          kmc * 512 + (tkc + 1) * 128],
                                      qkT[qoff:qoff + 64,
                                          qmc * 512:(qmc + 1) * 512])]
                            tmm = mm_group(ps[bank][:], pairs,
                                           [t_qk[qmc], t_qk[kmc]], bank)
                            t_e = sca.emit(lambda e, tkc=tkc, bank=bank:
                                           e.activation(
                                out=esbf[tkc % 2][:], in_=ps[bank][:],
                                func=AF.Exp, scale=0.125),
                                waits=[tmm, t_esbf_cons[tkc % 2]],
                                inc=True)
                            ps_free[bank] = t_e
                            t_ex[tkc] = vec.emit(lambda e, tkc=tkc:
                                                 e.tensor_mul(
                                expd[tkc][:], esbf[tkc % 2][:],
                                minn[tkc][:]),
                                waits=[t_e, t_minn[tkc],
                                       t_expd_cons[tkc]],
                                inc=True, w=f"ex{tkc}")
                            t_esbf_cons[tkc % 2] = t_ex[tkc]
                        for tqs in range(4):
                            pairs = [(expd[tkc][:,
                                               tqs * 128:(tqs + 1) * 128],
                                      ones_c_b[:]) for tkc in range(4)]
                            t_cs_last = mm_group(
                                ps[6][:, 4 * hh + tqs:4 * hh + tqs + 1],
                                pairs, t_ex + [t_c[3]], 6,
                                skip_check=True)
                        t_rc = vec.emit(lambda e, hh=hh: e.reciprocal(
                            rcall[:, 4 * hh:4 * hh + 4],
                            ps[6][:, 4 * hh:4 * hh + 4]),
                            waits=[t_cs_last], inc=True, w=f"rc{hh}")
                        t_rc_g.append(t_rc)
                        bank = hh // 2
                        poff = 64 * (hh % 2)
                        pairs = [(vnat[tkc][:, h * 64:h * 64 + 64],
                                  expd[tkc][:]) for tkc in range(4)]
                        tmm = mm_group(ps[bank][poff:poff + 64, :], pairs,
                                       list(t_vn) + t_ex,
                                       bank if hh % 2 == 0 else None)
                        t_ote_g.append(tmm)
                        for tkc in range(4):
                            t_expd_cons[tkc] = tmm
                    t1 = vec.emit(lambda e: e.tensor_copy(cmp32[:],
                                                          rcall[:]),
                                  waits=t_rc_g, inc=True, w="rcc",
                                  r=tuple(f"rc{i}" for i in range(8)))
                    t2 = ten.emit(lambda e: e.transpose(
                        ps[6][0:32, 0:128], cmp32[:], identf[:]),
                        waits=[t1, ps_free[6]], inc=True)
                    t3 = vec.emit(lambda e: e.tensor_copy(
                        scr32[:], ps[6][0:32, 0:128]), waits=[t2],
                        inc=True, w="rcd")
                    ps_free[6] = t3
                    for pp in range(4):
                        mc = g * 4 + pp
                        t4 = syn.emit(lambda e, pp=pp: e.dma_start(
                            out=rcflat[0:1, :]
                            .rearrange("o (c p) -> o c p", c=8),
                            in_=scr32[8 * pp:8 * pp + 8, :]),
                            waits=[t3, t_rcflat_cons], inc=True, val=16)
                        t_m1 = ten.emit(lambda e: e.matmul(
                            ps[6][0:64, :], ones_r_f[0:1, 0:64],
                            rcflat[0:1, 0:512],
                            start=True, stop=True, skip_group_check=True),
                            waits=[t4, ps_free[6]], inc=True)
                        t_m2 = ten.emit(lambda e: e.matmul(
                            ps[6][64:128, :], ones_r_f[0:1, 0:64],
                            rcflat[0:1, 512:1024],
                            start=True, stop=True, skip_group_check=True),
                            inc=True)
                        t_rcflat_cons = t_m2
                        t_rr = vec.emit(lambda e, pp=pp: e.tensor_copy(
                            rcrep[pp][:], ps[6][:]), waits=[t_m2],
                            inc=True, w=f"rcr{pp}")
                        ps_free[6] = t_rr
                        t_oe = vec.emit(lambda e, mc=mc, pp=pp:
                                        e.tensor_mul(
                            ot2[:, mc * 512:(mc + 1) * 512],
                            ps[pp][:], rcrep[pp][:]),
                            waits=[t_ote_g[2 * pp], t_ote_g[2 * pp + 1],
                                   t_rr],
                            inc=True, w=f"oe{mc}", r=(f"rcr{pp}",))
                        ps_free[pp] = t_oe
                        t_ot2e.append(t_oe)

                t_s2 = [None] * 4
                for half in range(2):
                    t_wo = rwv.load(gps, 2 + half, lambda e, half=half:
                                    e.dma_start(
                        out=wvh[half % 2][:]
                        .rearrange("p (a f) -> p a f", a=CCN),
                        in_=dt["wout_t"][:, :, half * 512:(half + 1) * 512]
                        .rearrange("p a f -> p a f")))
                    for a in range(4):
                        bank = (a * 2 + half) % 4
                        pairs = [(ot2[:, mc * 512 + a * 128:
                                      mc * 512 + (a + 1) * 128],
                                  wvh[half % 2][:,
                                                mc * 512:(mc + 1) * 512])
                                 for mc in range(CCN)]
                        tmm = mm_group(ps[bank][:], pairs,
                                       [t_wo, t_ot2e[-1]], bank)
                        t_s2h = vec.emit(lambda e, a=a, half=half,
                                         bank=bank:
                                         e.scalar_tensor_tensor(
                            out=s2[a][:, half * 512:(half + 1) * 512],
                            in0=ps[bank][:], scalar=1.0,
                            in1=xslot[a][:, half * 512:(half + 1) * 512],
                            op0=ALU.mult, op1=ALU.add),
                            waits=[tmm], inc=True, w=f"s2_{a}{half}")
                        ps_free[bank] = t_s2h
                        t_s2[a] = t_s2h
                    rwv.consumed(2 + half, tmm)

                P.scope = f"b{b}F2b"
                P.barrier()
                f2bsb = Bump(bump_f2.off).maker()
                CA = [f2bsb(f"CA{a}", (128, T), BF16) for a in range(4)]
                fT = [f2bsb(f"fT{m}", (128, 512), BF16)
                      for m in range(MCF)]
                wfa = [f2bsb(f"wfa{i}", (128, C), BF16) for i in range(2)]
                wfb = [f2bsb(f"wfb{i}", (128, C), BF16) for i in range(2)]
                w3s = [f2bsb(f"w3s{i}", (128, C), BF16) for i in range(2)]
                silt = [f2bsb(f"silt{i}", (128, 512), BF16)
                        for i in range(2)]
                selout = [f2bsb(f"selout{i}", (128, C), F32)
                          for i in range(2)]
                delta = [f2bsb(f"delta{a}", (128, C), BF16)
                         for a in range(4)]
                sdt = [f2bsb(f"sdt{i}", (128, C), F32) for i in range(2)]
                sxt = [f2bsb(f"sxt{i}", (128, C), BF16) for i in range(2)]
                uh = [f2bsb(f"uh{i}", (128, TBS), F32) for i in range(2)]

                t_CA = build_CA(CA, [t_slot], f2bsb)

                t_h2Te = None
                for a in range(4):
                    rcol = lncol[:, a * 4 + 3:a * 4 + 4]
                    lnv = lncol[:, a * 4 + 2:a * 4 + 3]
                    t_sq = vec.emit(lambda e, a=a: e.scalar_tensor_tensor(
                        out=fjunk[:], in0=s2[a][:], scalar=1.0,
                        in1=s2[a][:], op0=ALU.mult, op1=ALU.mult,
                        accum_out=lncol[:, a * 4 + 1:a * 4 + 2]),
                        waits=[t_s2[a]], inc=True, w="h2sq")
                    sca.emit(lambda e, a=a, lnv=lnv: e.activation(
                        out=lnv, in_=lncol[:, a * 4 + 1:a * 4 + 2],
                        func=AF.Ln, scale=1.0 / C, bias=epsrms[:]),
                        waits=[t_sq])
                    t_r = sca.emit(lambda e, rcol=rcol, lnv=lnv:
                                   e.activation(
                        out=rcol, in_=lnv, func=AF.Exp, scale=-0.5),
                        inc=True)
                    t_h2 = vec.emit(lambda e, a=a, rcol=rcol:
                                    e.scalar_tensor_tensor(
                        out=hf[a % 2][:], in0=s2[a][:], scalar=rcol,
                        op0=ALU.mult, in1=g2rep[:], op1=ALU.mult),
                        waits=[t_r, t_hf_cons[a % 2]], inc=True,
                        w=f"hf{a % 2}")
                    t_tr = None
                    for cc in range(CCN):
                        bank = 6 + (cc % 2)
                        t_tr = ten.emit(lambda e, a=a, cc=cc, bank=bank:
                                        e.transpose(
                            ps[bank][0:128, 0:128],
                            hf[a % 2][:, cc * 128:(cc + 1) * 128],
                            identf[:]),
                            waits=[t_h2, ps_free[bank], t_vn_mm],
                            inc=True)
                        t_h2Te = sca.emit(lambda e, a=a, cc=cc, bank=bank:
                                          e.activation(
                            out=htb[:, cc * 512 + a * 128:
                                    cc * 512 + (a + 1) * 128],
                            in_=ps[bank][0:128, 0:128],
                            func=AF.Identity),
                            waits=[t_tr], inc=True)
                        ps_free[bank] = t_h2Te
                    t_hf_cons[a % 2] = t_tr
                t_h2T_done = t_h2Te

                t_fT = [None] * MCF
                t_sil = None
                for mc in range(MCF):
                    wsl = mc % 2
                    t_w1 = rwq.load(gps, 16 + mc, lambda e, mc=mc,
                                    wsl=wsl: e.dma_start(
                        out=wfa[wsl][:]
                        .rearrange("p (a f) -> p a f", a=CCN),
                        in_=dt["w1_t"][mc]))
                    t_w2 = rwb.load(gps, mc, lambda e, mc=mc, wsl=wsl:
                                    e.dma_start(
                        out=wfb[wsl][:]
                        .rearrange("p (a f) -> p a f", a=CCN),
                        in_=dt["w2_t"][mc]))
                    pairs1 = [(wfa[wsl][:, cc * 128:(cc + 1) * 128],
                               htb[:, cc * 512:(cc + 1) * 512])
                              for cc in range(CCN)]
                    tmm1 = mm_group(ps[4][:], pairs1, [t_w1, t_h2T_done],
                                    4)
                    pairs2 = [(wfb[wsl][:, cc * 128:(cc + 1) * 128],
                               htb[:, cc * 512:(cc + 1) * 512])
                              for cc in range(CCN)]
                    tmm2 = mm_group(ps[5][:], pairs2, [t_w2], 5)
                    rwq.consumed(16 + mc, tmm1)
                    rwb.consumed(mc, tmm2)
                    t_sil = sca.emit(lambda e, mc=mc: e.activation(
                        out=silt[mc % 2][:], in_=ps[4][:], func=AF.Silu),
                        waits=[tmm1, t_fT[mc - 2] if mc >= 2 else None],
                        inc=True)
                    ps_free[4] = t_sil
                    t_fT[mc] = vec.emit(lambda e, mc=mc: e.tensor_mul(
                        fT[mc][:], silt[mc % 2][:], ps[5][:]),
                        waits=[tmm2, t_sil], inc=True, w=f"fT{mc}")
                    ps_free[5] = t_fT[mc]

                t_xffmm = None
                for mc in range(MCF):
                    t_w3 = rw3b.load(gps, mc, lambda e, mc=mc: e.dma_start(
                        out=w3s[mc % 2][:], in_=dt["w3_t"][mc]))
                    for a in range(4):
                        for half in range(2):
                            bank = a * 2 + half
                            w0 = []
                            if mc == 0:
                                w0 = [t_w3, t_fT[mc], ps_free[bank]]
                            elif a == 0 and half == 0:
                                w0 = [t_w3, t_fT[mc]]
                            t_xffmm = ten.emit(lambda e, mc=mc, a=a,
                                               half=half, bank=bank:
                                               e.matmul(
                                ps[bank][:],
                                fT[mc][:, a * 128:(a + 1) * 128],
                                w3s[mc % 2][:,
                                            half * 512:(half + 1) * 512],
                                start=(mc == 0), stop=(mc == MCF - 1)),
                                waits=w0, inc=(a == 3 and half == 1))
                    rw3b.consumed(mc, t_xffmm)

                t_delta = [None] * 4
                t_selout_cons = [None] * 2
                t_lnf_prev = [None] * 2
                for a in range(4):
                    t_preh = [None, None]
                    for half in range(2):
                        bank = a * 2 + half
                        vec.emit(lambda e, a=a, half=half, bank=bank:
                                 e.scalar_tensor_tensor(
                            out=fjunk[:, half * 512:(half + 1) * 512],
                            in0=ps[bank][:], scalar=1.0,
                            in1=s2[a][:, half * 512:(half + 1) * 512],
                            op0=ALU.mult, op1=ALU.add),
                            waits=[t_xffmm, t_selout_cons[a % 2]],
                            w=f"prh{half}")
                        t_preh[half] = vec.emit(
                            lambda e, a=a, half=half:
                            e.scalar_tensor_tensor(
                                out=selout[a % 2][:,
                                                  half * 512:
                                                  (half + 1) * 512],
                                in0=xslot[a][:,
                                             half * 512:(half + 1) * 512],
                                scalar=1.0,
                                in1=fjunk[:, half * 512:(half + 1) * 512],
                                op0=ALU.mult, op1=ALU.add),
                            inc=True, w=f"pre{a}{half}",
                            r=(f"prh{half}",))
                        ps_free[bank] = t_preh[half]
                    vec.emit(lambda e, a=a: e.bn_stats(
                        stats[:, 0:6], selout[a % 2][:, 0:512]),
                        waits=t_preh, w="fbnsa",
                        r=(f"pre{a}0", f"pre{a}1"))
                    vec.emit(lambda e, a=a: e.bn_stats(
                        stats[:, 6:12], selout[a % 2][:, 512:1024]),
                        w="fbnsb")
                    t_fbn = vec.emit(lambda e: e.bn_aggr(
                        mvt[:, 0:2], stats[:, 0:12]),
                        inc=True, w="fmv", r=("fbnsa", "fbnsb"))
                    rcol = lncol[:, a * 4:a * 4 + 1]
                    nmr = lncol[:, a * 4 + 1:a * 4 + 2]
                    lnv = lncol[:, a * 4 + 2:a * 4 + 3]
                    sca.emit(lambda e, lnv=lnv: e.activation(
                        out=lnv, in_=mvt[:, 1:2], func=AF.Ln,
                        bias=epsln[:]), waits=[t_fbn])
                    t_r = sca.emit(lambda e, rcol=rcol, lnv=lnv:
                                   e.activation(
                        out=rcol, in_=lnv, func=AF.Exp, scale=-0.5),
                        inc=True)
                    vec.emit(lambda e, nmr=nmr, rcol=rcol: e.tensor_mul(
                        nmr, mvt[:, 0:1], rcol),
                        waits=[t_r], w="fnm")
                    t_nmr = vec.emit(lambda e, nmr=nmr:
                                     e.tensor_scalar_mul(
                        nmr, nmr, -1.0), inc=True, w="fnm2", r=("fnm",))
                    t_lnf = sca.emit(lambda e, a=a, rcol=rcol, nmr=nmr:
                                     e.activation(
                        out=flnout[a % 2][:], in_=selout[a % 2][:],
                        func=AF.Identity, scale=rcol, bias=nmr),
                        waits=[t_nmr, t_lnf_prev[a % 2]], inc=True)
                    t_selout_cons[a % 2] = t_lnf
                    t_delta[a] = vec.emit(lambda e, a=a:
                                          e.scalar_tensor_tensor(
                        out=delta[a][:], in0=xslot[a][:],
                        scalar=nsscol[:, a:a + 1], op0=ALU.mult,
                        in1=flnout[a % 2][:], op1=ALU.add),
                        waits=[t_lnf, t_nss], inc=True, w=f"dl{a}")
                    t_lnf_prev[a % 2] = t_delta[a]
                    if dbg:
                        t_dd = syn.emit(lambda e, a=a, b=b: e.dma_start(
                            out=dt["dbg_selout"][b,
                                                 a * 128:(a + 1) * 128,
                                                 :],
                            in_=flnout[a % 2][:]), waits=[t_lnf],
                            inc=True, val=16)
                        t_lnf_prev[a % 2] = t_dd
                        t_ds = gps.emit(lambda e, a=a, b=b: e.dma_start(
                            out=dt["dbg_s2"][b, a * 128:(a + 1) * 128,
                                             :],
                            in_=s2[a][:]), waits=[t_s2[a]], inc=True,
                            val=16)
                        t_xf = vec.emit(lambda e, a=a: e.tensor_copy(
                            fjunk[:], xslot[a][:]),
                            waits=[t_xslot[a], t_preh[1]],
                            inc=True, w="dbgx", r=("prh1",))
                        t_dx = syn.emit(lambda e, a=a, b=b: e.dma_start(
                            out=dt["dbg_xslot"][b,
                                                a * 128:(a + 1) * 128,
                                                :],
                            in_=fjunk[:]), waits=[t_xf], inc=True,
                            val=16)

                t_sd_cons = [None, None]
                for c in range(32):
                    t_sx = rsx.load(sca, c + 32 * b,
                                    lambda e, c=c, b=b: e.dma_start(
                        out=sxt[c % 2][:],
                        in_=dt["x1_s"][b, c * 128:(c + 1) * 128, :]))
                    t_ev = None
                    for half in range(2):
                        bank = half
                        pairs = [(CA[a][:, c * 128:(c + 1) * 128],
                                  delta[a][:, half * 512:(half + 1) * 512])
                                 for a in range(4)]
                        tmm = mm_group(ps[bank][:], pairs,
                                       t_delta + t_CA +
                                       [t_sd_cons[c % 2]], bank)
                        vec.emit(lambda e, c=c, half=half:
                                 e.tensor_single_scalar(
                            uh[half][:],
                            sxt[c % 2][:, half * 512:(half + 1) * 512],
                            smat[:, c:c + 1], ALU.mult),
                            waits=[t_sx], w=f"uh{half}")
                        t_ev = vec.emit(lambda e, c=c, half=half,
                                        bank=bank:
                                        e.scalar_tensor_tensor(
                            out=sdt[c % 2][:, half * 512:(half + 1) * 512],
                            in0=ps[bank][:], scalar=1.0,
                            in1=uh[half][:], op0=ALU.mult, op1=ALU.add),
                            waits=[tmm], inc=True,
                            w=f"sd{c % 2}{half}", r=(f"uh{half}",))
                        ps_free[bank] = t_ev
                    rsx.consumed(c + 32 * b, t_ev)
                    t_st = gps.emit(lambda e, c=c, b=b: e.dma_start(
                        out=dt["out"][b, c * 128:(c + 1) * 128, :],
                        in_=sdt[c % 2][:]),
                        waits=[t_ev], inc=True, val=16,
                        sem=scat_sems[c % 2])
                    t_sd_cons[c % 2] = t_st

                P.barrier()

    block = ctx.enter_context(nc.Block())
    P.play(block)
    return nc, ctx



_CACHE = {}


def _prep_host(inputs):
    x = np.asarray(inputs["L0_attn_x"], dtype=np.float32)

    perm = np.empty(C, dtype=np.int64)
    perm[:512] = np.arange(512) * 2
    perm[512:] = np.arange(512) * 2 + 1

    def tile_w(w):
        return np.ascontiguousarray(w.reshape(CCN, 128, -1).transpose(1, 0, 2))

    wk = tile_w(np.asarray(inputs["L0_attn_Wk"], np.float32)[:, perm])
    wq = tile_w(np.asarray(inputs["L0_attn_Wq"], np.float32)[:, perm])
    wv = tile_w(np.asarray(inputs["L0_attn_Wv"], np.float32))
    wp = tile_w(np.asarray(inputs["L0_attn_Wp"], np.float32))
    bkcv = np.ascontiguousarray(
        np.asarray(inputs["L0_attn_bk"], np.float32)[perm].reshape(CCN, 128).T)
    bqcv = np.ascontiguousarray(
        np.asarray(inputs["L0_attn_bq"], np.float32)[perm].reshape(CCN, 128).T)

    wqkv = np.asarray(inputs["L0_ffn_W_qkv"], np.float32)
    wqkvT = wqkv.T
    wqkv_t = np.ascontiguousarray(
        wqkvT[:, :2048].reshape(CCN, 128, 16, 128).transpose(2, 1, 0, 3))
    wv2 = np.ascontiguousarray(
        wqkvT[:, 2048:].reshape(CCN, 128, C).transpose(1, 0, 2))
    wout_t = np.ascontiguousarray(
        np.asarray(inputs["L0_ffn_W_out"], np.float32).T
        .reshape(CCN, 128, C).transpose(1, 0, 2))

    w1p = np.zeros((DFFP, C), np.float32)
    w1p[:DFF] = np.asarray(inputs["L0_ffn_W1"], np.float32)
    w2p = np.zeros((DFFP, C), np.float32)
    w2p[:DFF] = np.asarray(inputs["L0_ffn_W2"], np.float32)
    w3p = np.zeros((C, DFFP), np.float32)
    w3p[:, :DFF] = np.asarray(inputs["L0_ffn_W3"], np.float32)
    w1_t = np.ascontiguousarray(
        w1p.reshape(MCF, 128, CCN, 128).transpose(0, 3, 2, 1))
    w2_t = np.ascontiguousarray(
        w2p.reshape(MCF, 128, CCN, 128).transpose(0, 3, 2, 1))
    w3_t = np.ascontiguousarray(w3p.T.reshape(MCF, 128, C))

    jj = np.arange(512, dtype=np.float64)
    tt = np.arange(T, dtype=np.float64)
    ang = np.outer(jj, (T - 1) - tt)
    cosT = np.cos(ang).astype(ml_dtypes.bfloat16)
    sinT = np.sin(ang).astype(ml_dtypes.bfloat16)

    def tile_cs(a):
        return np.ascontiguousarray(
            a.reshape(4, 128, NTB, TBS).transpose(2, 1, 0, 3))

    pidx = np.arange(128)[:, None]
    fidx = np.arange(TBS)[None, :]
    masks_np = np.stack([np.where(128 * a + pidx <= fidx, 0.0, NEG)
                         for a in range(4)], axis=1).astype(np.float32)
    tri_np = np.stack([(fidx < 128 * a + pidx).astype(np.float32)
                       for a in range(4)], axis=1).astype(np.float32)
    slotiota_np = np.broadcast_to(
        np.arange(512, dtype=np.float32)[None, :], (128, 512)).copy()
    iotacols_np = (128.0 * np.arange(4)[None, :] + pidx).astype(np.float32)

    shared = dict(
        wk=wk, wq=wq, wv=wv, wp=wp, bkc=bkcv, bqc=bqcv,
        bv_row=np.asarray(inputs["L0_attn_bv"], np.float32).reshape(1, C),
        bp_row=np.asarray(inputs["L0_attn_bp"], np.float32).reshape(1, C),
        wqkv_t=wqkv_t, wv2=wv2, wout_t=wout_t,
        w1_t=w1_t, w2_t=w2_t, w3_t=w3_t,
        g1_row=np.asarray(inputs["L0_ffn_g1"], np.float32).reshape(1, C),
        g2_row=np.asarray(inputs["L0_ffn_g2"], np.float32).reshape(1, C),
        wr_row=np.asarray(inputs["L0_ffn_w_router"],
                          np.float32).reshape(1, C),
        cos_t=tile_cs(cosT), sin_t=tile_cs(sinT),
        masks=masks_np, tri=tri_np,
        slotiota=slotiota_np, iotacols=iotacols_np,
        identf=np.eye(128, dtype=np.float32),
        ones_r_b=np.ones((1, 128), ml_dtypes.bfloat16),
        ones_r_f=np.ones((1, 128), np.float32),
        ones_c_b=np.ones((128, 1), ml_dtypes.bfloat16),
        ones_sq_f=np.ones((128, 128), np.float32),
    )

    in_maps = []
    for k in range(NCORES):
        xb = np.ascontiguousarray(x[BPC * k: BPC * (k + 1)])
        xtr = np.ascontiguousarray(xb.transpose(0, 2, 1))
        m = dict(shared)
        m["x_nat"] = xb
        m["x_tr"] = xtr
        in_maps.append(m)
    return in_maps


def kernel(**inputs):
    if "prog" not in _CACHE:
        _CACHE["prog"] = build_program(dbg=False)
    nc, _ctx = _CACHE["prog"]
    in_maps = _prep_host(inputs)
    res = run_bass_kernel_spmd(nc, in_maps, core_ids=list(range(NCORES)))
    out = np.concatenate([r["out"] for r in res.results], axis=0)
    return out.reshape(16, T, C)


# version 05
# speedup vs baseline: 1.2337x; 1.0046x over previous
import numpy as np
import ml_dtypes
from contextlib import ExitStack

import concourse.bass as bass
import concourse.mybir as mybir
from concourse.bass_utils import run_bass_kernel_spmd

F32 = mybir.dt.float32
BF16 = mybir.dt.bfloat16
U32 = mybir.dt.uint32
AF = mybir.ActivationFunctionType
ALU = mybir.AluOpType
AX = mybir.AxisListType

NCORES = 8
BPC = 2
T = 4096
C = 1024
TBS = 512
NTB = T // TBS
CCN = C // 128
CAP = 512
H = 16
DFF = 2730
DFFP = 2816
MCF = DFFP // 128
NBIS = 48
NEG = -1.0e30
ISCALE = float(C) ** -0.5



class Q:
    def __init__(self, prog, name):
        self.prog = prog
        self.name = name
        self.sem = prog.ctx.enter_context(prog.nc.semaphore(f"s_{name}"))
        self.bsem = prog.ctx.enter_context(prog.nc.semaphore(f"sb_{name}"))
        self.ops = []
        self.pad = False

    def emit(self, fn, waits=(), inc=False, sem=None, val=1, w=None, r=()):
        waits = tuple(t for t in waits if t is not None)
        isem = None
        tick = None
        if inc:
            isem = sem if sem is not None else self.sem
            cum = self.prog.sem_cnt.get(isem.name, 0) + val
            self.prog.sem_cnt[isem.name] = cum
            tick = (isem, cum)
        self.ops.append((fn, waits, isem, val, w, tuple(r), self.pad,
                         getattr(self.prog, "scope", None)))
        return tick

    def tick(self):
        return (self.sem, self.prog.sem_cnt.get(self.sem.name, 0))

    def play(self, eng):
        satisfied = {}
        prev_w = None
        first = True
        import contextlib
        for fn, waits, isem, ival, wname, rnames, pad, scope in self.ops:
            byname = {}
            for (sem, v) in waits:
                if sem is self.sem:
                    continue
                if v > byname.get(sem.name, (None, 0))[1]:
                    byname[sem.name] = (sem, v)
            for sem, v in byname.values():
                if v > satisfied.get(sem.name, 0):
                    eng.wait_ge(sem, v)
                    satisfied[sem.name] = v
            with (self.prog.nc.named_scope(scope) if scope
                  else contextlib.nullcontext()):
                if self.name == "vec" and not first and (
                        pad or (prev_w is not None and
                                (prev_w == wname or prev_w in rnames))):
                    padt = self.prog.padt
                    eng.tensor_copy(padt[:, 0:1], padt[:, 2:3])
                    eng.tensor_copy(padt[:, 1:2], padt[:, 2:3])
                inst = fn(eng)
                if isem is not None:
                    inst.then_inc(isem, ival)
            prev_w = wname if self.name == "vec" else None
            first = False


class Ring:

    def __init__(self, prog, name, nslots):
        self.nslots = nslots
        self.sems = [prog.ctx.enter_context(prog.nc.semaphore(f"r_{name}{i}"))
                     for i in range(nslots)]
        self.cnt = [0] * nslots
        self.free_t = [None] * nslots

    def load(self, q, i, fn, waits=()):
        s = i % self.nslots
        self.cnt[s] += 16
        w = list(waits)
        if self.free_t[s] is not None:
            w.append(self.free_t[s])
        q.emit(fn, waits=w, inc=True, sem=self.sems[s], val=16)
        return (self.sems[s], self.cnt[s])

    def consumed(self, i, ticket):
        assert ticket is not None, "Ring.consumed needs an incrementing ticket"
        self.free_t[i % self.nslots] = ticket


class Prog:
    def __init__(self, nc, ctx):
        self.nc = nc
        self.ctx = ctx
        self.sem_cnt = {}
        self.extra_sems = []
        self.ten = Q(self, "ten")
        self.vec = Q(self, "vec")
        self.sca = Q(self, "sca")
        self.syn = Q(self, "syn")
        self.gps = Q(self, "gps")
        self.queues = [self.ten, self.vec, self.sca, self.syn, self.gps]

    def mksem(self, name):
        s = self.ctx.enter_context(self.nc.semaphore(name))
        self.extra_sems.append(s)
        return s

    def barrier(self):
        ticks = [q.emit(lambda e: e.nop(), inc=True, sem=q.bsem)
                 for q in self.queues]
        ticks += [(s, self.sem_cnt[s.name]) for s in self.extra_sems
                  if self.sem_cnt.get(s.name, 0) > 0]
        for q in self.queues:
            q.emit(lambda e: e.nop(), waits=ticks)

    def play(self, block):
        @block.tensor
        def _(eng):
            self.ten.play(eng)

        @block.vector
        def _(eng):
            self.vec.play(eng)

        @block.scalar
        def _(eng):
            self.sca.play(eng)

        @block.sync
        def _(eng):
            self.syn.play(eng)

        @block.gpsimd
        def _(eng):
            self.gps.play(eng)



def build_program(dbg=False):
    nc = bass.Bass("TRN2", target_bir_lowering=False, debug=False,
                   detect_race_conditions=False)
    ctx = ExitStack()
    dt = {}

    def dram(name, shape, dtype, kind="ExternalInput"):
        dt[name] = nc.dram_tensor(name, list(shape), dtype, kind=kind).ap()
        return dt[name]

    dram("x_nat", (BPC, T, C), F32)
    dram("x_tr", (BPC, C, T), F32)
    dram("wk", (128, CCN, C), F32)
    dram("wq", (128, CCN, C), F32)
    dram("wv", (128, CCN, C), F32)
    dram("wp", (128, CCN, C), F32)
    dram("bkc", (128, CCN), F32)
    dram("bqc", (128, CCN), F32)
    dram("bv_row", (1, C), F32)
    dram("bp_row", (1, C), F32)
    dram("wqkv_t", (16, 128, CCN, 128), F32)
    dram("wv2", (128, CCN, C), F32)
    dram("wout_t", (128, CCN, C), F32)
    dram("w1_t", (MCF, 128, CCN, 128), F32)
    dram("w2_t", (MCF, 128, CCN, 128), F32)
    dram("w3_t", (MCF, 128, C), F32)
    dram("g1_row", (1, C), F32)
    dram("g2_row", (1, C), F32)
    dram("wr_row", (1, C), F32)
    dram("cos_t", (NTB, 128, 4, TBS), BF16)
    dram("sin_t", (NTB, 128, 4, TBS), BF16)
    dram("masks", (128, 4, TBS), F32)
    dram("tri", (128, 4, TBS), F32)
    dram("slotiota", (128, 512), F32)
    dram("iotacols", (128, 4), F32)
    dram("identf", (128, 128), F32)
    dram("ones_r_b", (1, 128), BF16)
    dram("ones_r_f", (1, 128), F32)
    dram("ones_c_b", (128, 1), BF16)
    dram("ones_sq_f", (128, 128), F32)

    dram("out", (BPC, T, C), F32, kind="ExternalOutput")
    sk = "ExternalOutput" if dbg else "Internal"
    dram("kt_s", (BPC, NTB, 128, CCN, TBS), BF16, kind=sk)
    dram("qt_s", (BPC, 32, 128, C), BF16, kind=sk)
    dram("v_s", (BPC, T, C), BF16, kind=sk)
    dram("x1_s", (BPC, T, C), BF16, kind=sk)
    if dbg:
        dram("dbg_score", (BPC, 128, 32), F32, kind="ExternalOutput")
        dram("dbg_st", (128, 8 * TBS), F32, kind="ExternalOutput")
        dram("dbg_ot", (128, CCN * TBS), F32, kind="ExternalOutput")
        dram("dbg_selm", (BPC, 128, 32), F32, kind="ExternalOutput")
        dram("dbg_selt", (BPC, 32, 128), F32, kind="ExternalOutput")
        dram("dbg_pos", (BPC, 32, 128), F32, kind="ExternalOutput")
        dram("dbg_offc", (BPC, 32, 1), F32, kind="ExternalOutput")
        dram("dbg_slott", (BPC, 32, 128), F32, kind="ExternalOutput")
        dram("dbg_lo", (BPC, 128, 1), F32, kind="ExternalOutput")
        dram("dbg_bis", (BPC, 128, 4 * NBIS), F32, kind="ExternalOutput")
        dram("dbg_smat", (BPC, 128, 32), F32, kind="ExternalOutput")
        dram("dbg_slot", (BPC, 128, 32), F32, kind="ExternalOutput")
        dram("dbg_rank", (BPC, 128, 4), F32, kind="ExternalOutput")
        dram("dbg_xslot", (BPC, 512, C), F32, kind="ExternalOutput")
        dram("dbg_s2", (BPC, 512, C), F32, kind="ExternalOutput")
        dram("dbg_selout", (BPC, 512, C), F32, kind="ExternalOutput")

    P = Prog(nc, ctx)
    ten, vec, sca, syn, gps = P.ten, P.vec, P.sca, P.syn, P.gps

    def sb(name, shape, dtype):
        return ctx.enter_context(nc.sbuf_tensor("t_" + name, list(shape), dtype))

    ARENA_B = 164 * 1024
    arena = sb("arena", (128, ARENA_B), mybir.dt.uint8)

    class Bump:
        def __init__(self, base):
            self.off = base

        def maker(self):
            def mk(name, shape, dtyp):
                per = 1
                for s_ in shape[1:]:
                    per *= s_
                per *= mybir.dt.size(dtyp)
                off = (self.off + 63) & ~63
                self.off = off + per
                assert self.off <= ARENA_B, (name, self.off)
                v = arena[0:shape[0], off:off + per].bitcast(dtyp)
                if len(shape) == 3:
                    v = v.rearrange("p (a f) -> p a f", a=shape[1])
                return v
            return mk

    identf = sb("identf", (128, 128), F32)
    ones_r_b = sb("ones_r_b", (1, 128), BF16)
    ones_r_f = sb("ones_r_f", (1, 128), F32)
    ones_c_b = sb("ones_c_b", (128, 1), BF16)
    ones_sq_f = sb("ones_sq_f", (128, 128), F32)
    iotacols = sb("iotacols", (128, 4), F32)
    bkc = sb("bkc", (128, CCN), F32)
    bqc = sb("bqc", (128, CCN), F32)
    bvr = sb("bvr", (1, C), BF16)
    bpr = sb("bpr", (1, C), BF16)
    grow = sb("grow", (1, 3 * C), BF16)
    g1rep = sb("g1rep", (128, C), BF16)
    g2rep = sb("g2rep", (128, C), BF16)
    wrrep = sb("wrrep", (128, C), BF16)
    scoremat = sb("scoremat", (128, 32), F32)
    smat = sb("smat", (128, 32), F32)
    scb = sb("scb", (128, 32), BF16)
    ssb = sb("ssb", (128, 32), BF16)
    locol = sb("locol", (128, 1), F32)
    hicol = sb("hicol", (128, 1), F32)
    midA = sb("midA", (128, 1), F32)
    midB = sb("midB", (128, 1), F32)
    cmp32 = sb("cmp32", (128, 32), F32)
    cntcol = sb("cntcol", (128, 1), F32)
    cge = sb("cge", (128, 1), U32)
    clt = sb("clt", (128, 1), U32)
    selm = sb("selm", (128, 32), F32)
    selt = sb("selt", (32, 128), F32)
    scr32 = sb("scr32", (32, 128), F32)
    pret = [sb(f"pret{i}", (32, 128), F32) for i in range(2)]
    post_t = sb("post_t", (32, 128), F32)
    slott = sb("slott", (32, 128), F32)
    nselu = sb("nselu", (32, 128), U32)
    c512t = sb("c512t", (32, 128), F32)
    rt32 = sb("rt32", (32, 32), F32)
    rr32 = sb("rr32", (32, 32), F32)
    er32 = [sb(f"er32_{i}", (1, 32), F32) for i in range(2)]
    offc = sb("offc", (32, 1), F32)
    slotc = sb("slotc", (128, 32), F32)
    epsln = sb("epsln", (128, 1), F32)
    epsln4 = sb("epsln4", (128, 1), F32)
    epsrms = sb("epsrms", (128, 1), F32)
    b1col = sb("b1col", (128, 1), F32)
    sccol = sb("sccol", (128, 4), F32)
    sscol = sb("sscol", (128, 4), F32)
    nsscol = sb("nsscol", (128, 4), F32)
    rankc = sb("rankc", (128, 4), F32)
    rankrep = sb("rankrep", (128, 512), F32)
    minn = [sb(f"minn{i}", (128, 512), BF16) for i in range(4)]
    stats = sb("stats", (128, 4 * 12), F32)
    mvt = sb("mvt", (128, 4 * 2), F32)
    lncol = sb("lncol", (128, 16), F32)
    abscol = sb("abscol", (128, 1), F32)
    padt = sb("padt", (128, 4), F32)
    P.padt = padt
    rcall = sb("rcall", (128, 32), F32)
    rcflat = sb("rcflat", (1, 1024), F32)

    ps = [ctx.enter_context(nc.psum_tensor(f"ps{i}", (128, TBS), F32))
          for i in range(8)]
    ps_free = [None] * 8

    def mm_group(psum_ap, pairs, waits, bank, skip_check=False):
        n = len(pairs)
        tmm = None
        for i, (lhsT, rhs) in enumerate(pairs):
            w = ()
            if i == 0:
                w = list(waits)
                if bank is not None:
                    w.append(ps_free[bank])
            tmm = ten.emit(
                lambda e, lhsT=lhsT, rhs=rhs, i=i, n=n: e.matmul(
                    psum_ap, lhsT, rhs, start=(i == 0), stop=(i == n - 1),
                    skip_group_check=skip_check),
                waits=w, inc=(i == n - 1))
        return tmm

    def ld(q, dst, src, waits=()):
        return q.emit(lambda e, d=dst, s=src: e.dma_start(out=d, in_=s),
                      waits=waits, inc=True, val=16)

    P.scope = "startup"
    t_c = []
    for dst, src in ((identf, "identf"), (ones_r_b, "ones_r_b"),
                     (ones_r_f, "ones_r_f"), (ones_c_b, "ones_c_b"),
                     (ones_sq_f, "ones_sq_f"),
                     (iotacols, "iotacols"), (bkc, "bkc"), (bqc, "bqc")):
        t_c.append(ld(syn, dst[:], dt[src]))
    for dst, src in ((bvr[:], "bv_row"), (bpr[:], "bp_row"),
                     (grow[:, 0:C], "g1_row"), (grow[:, C:2 * C], "g2_row"),
                     (grow[:, 2 * C:3 * C], "wr_row")):
        t_c.append(ld(gps, dst, dt[src]))
    t_eps = None
    vec.emit(lambda e: e.memset(padt[:], 0.0))
    for tile, v in ((epsln, 1e-5), (epsln4, 2.5e-6), (epsrms, 1e-6),
                    (b1col, 1.0), (c512t, 512.0)):
        t_eps = vec.emit(lambda e, t=tile, vv=v: e.memset(t[:], vv), inc=True)
    t_grep = None
    for gi, gdst in enumerate((g1rep, g2rep, wrrep)):
        tmm0 = mm_group(ps[0][:], [(ones_r_b[:],
                                    grow[:, gi * C:gi * C + 512])],
                        list(t_c), 0)
        tmm1 = mm_group(ps[1][:], [(ones_r_b[:],
                                    grow[:, gi * C + 512:(gi + 1) * C])],
                        list(t_c), 1)
        t_grep = vec.emit(lambda e, d=gdst: (
            e.tensor_copy(d[:, 0:512], ps[0][:]),
            e.tensor_copy(d[:, 512:1024], ps[1][:]))[-1],
            waits=[tmm0, tmm1], inc=True, w=f"grep{gi}")
        ps_free[0] = t_grep
        ps_free[1] = t_grep
    P.barrier()

    rxs = Ring(P, "xs", 2)
    rcs = Ring(P, "cs", 2)
    rki = Ring(P, "ki", 2)
    rqj = Ring(P, "qj", 4)
    rvj = Ring(P, "vj", 4)
    rxn = Ring(P, "xn", 4)
    rx1 = Ring(P, "x1r", 4)
    rwq = Ring(P, "wqk", 2)
    rwb = Ring(P, "wb", 2)
    rw3b = Ring(P, "w3b", 2)
    rwv = Ring(P, "wv2", 2)
    rsx = Ring(P, "sx", 2)
    ktq_sems = [P.mksem(f"ktq{i}") for i in range(2)]
    v_sems = [P.mksem(f"vst{i}") for i in range(4)]
    x1_sems = [P.mksem(f"x1s{i}") for i in range(4)]
    u_sems = [P.mksem(f"ust{i}") for i in range(4)]
    scat_sems = [P.mksem(f"scat{i}") for i in range(2)]

    for b in range(BPC):
        P.barrier()
        if True:
            P.scope = f"b{b}A1"
            bump_a = Bump(0)
            asb = bump_a.maker()
            wp_sb = asb("wp", (128, CCN * C), BF16)
            t_wp = ld(gps, wp_sb[:], dt["wp"].rearrange("p a b -> p (a b)"))

            if True:
                psb = Bump(bump_a.off).maker()
                wk_sb = psb("wk", (128, CCN * C), BF16)
                wq_sb = psb("wq", (128, CCN * C), BF16)
                wv_sb = psb("wv", (128, CCN * C), BF16)
                xs = [psb(f"xs{i}", (128, CCN * TBS), BF16) for i in range(2)]
                cs = [psb(f"cs{i}", (128, 4 * TBS), BF16) for i in range(2)]
                sn = [psb(f"sn{i}", (128, 4 * TBS), BF16) for i in range(2)]
                kraw = psb("kraw", (128, CCN * TBS), BF16)
                qraw = psb("qraw", (128, CCN * TBS), BF16)
                mrope = [psb(f"mrope{i}", (128, TBS), BF16) for i in range(4)]
                ktf = psb("ktf", (128, CCN * TBS), BF16)
                qtf = psb("qtf", (128, CCN * TBS), BF16)
                vev = [psb(f"vev{i}", (128, TBS), BF16) for i in range(4)]


                t_wA = [t_wp]
                for dst, srcn in ((wk_sb, "wk"), (wq_sb, "wq"),
                                  (wv_sb, "wv")):
                    t_wA.append(ld(gps, dst[:],
                                   dt[srcn].rearrange("p a b -> p (a b)")))

                t_rope_prev = None
                t_store_prev = None
                t_vev_cons = [None] * 4

                for tb in range(NTB):
                    sl = tb % 2
                    t_x = rxs.load(gps, tb + b * NTB, lambda e, tb=tb, sl=sl, b=b:
                                   e.dma_start(
                        out=xs[sl][:].rearrange("p (a f) -> p a f", a=CCN),
                        in_=dt["x_tr"][b, :, tb * TBS:(tb + 1) * TBS]
                        .rearrange("(a p) f -> p a f", p=128)))
                    t_co = rcs.load(syn, 2 * tb + b, lambda e, tb=tb, sl=sl:
                                    e.dma_start(
                        out=cs[sl][:].rearrange("p (a f) -> p a f", a=4),
                        in_=dt["cos_t"][tb]))
                    t_si = rcs.load(syn, 2 * tb + 1 + b, lambda e, tb=tb, sl=sl:
                                    e.dma_start(
                        out=sn[sl][:].rearrange("p (a f) -> p a f", a=4),
                        in_=dt["sin_t"][tb]))

                    ev_kq = []
                    for which, wsb, braw, bias in ((0, wk_sb, kraw, bkc),
                                                   (1, wq_sb, qraw, bqc)):
                        for oc in range(CCN):
                            bank = (which * CCN + oc) % 2
                            pairs = [(wsb[:, cc * C + oc * 128:
                                          cc * C + (oc + 1) * 128],
                                      xs[sl][:, cc * TBS:(cc + 1) * TBS])
                                     for cc in range(CCN)]
                            tmm = mm_group(ps[bank][:], pairs,
                                           [t_x] + (t_wA if tb == 0 else []),
                                           bank)
                            tev = sca.emit(lambda e, braw=braw, oc=oc,
                                           bias=bias, bank=bank:
                                           e.activation(
                                out=braw[:, oc * TBS:(oc + 1) * TBS],
                                in_=ps[bank][:], func=AF.Identity,
                                bias=bias[:, oc:oc + 1]),
                                waits=[tmm, t_rope_prev, t_c[-1]],
                                inc=True)
                            ps_free[bank] = tev
                            ev_kq.append(tev)

                    t_rope = []
                    for src_raw, bout in ((kraw, ktf), (qraw, qtf)):
                        for p4 in range(4):
                            z0 = src_raw[:, p4 * TBS:(p4 + 1) * TBS]
                            z1 = src_raw[:, (p4 + 4) * TBS:(p4 + 5) * TBS]
                            cs_ = cs[sl][:, p4 * TBS:(p4 + 1) * TBS]
                            sn_ = sn[sl][:, p4 * TBS:(p4 + 1) * TBS]
                            w0 = [t_co, t_si, ev_kq[-1], t_store_prev]
                            vec.emit(lambda e, x=cs_, y=z0: e.tensor_mul(
                                mrope[0][:], x, y), waits=w0, w="m0")
                            vec.emit(lambda e, x=sn_, y=z1: e.tensor_mul(
                                mrope[1][:], x, y), w="m1")
                            vec.emit(lambda e, x=sn_, y=z0: e.tensor_mul(
                                mrope[2][:], x, y), w="m2")
                            vec.emit(lambda e, x=cs_, y=z1: e.tensor_mul(
                                mrope[3][:], x, y), w="m3")
                            vec.emit(lambda e,
                                     o=bout[:, p4 * TBS:(p4 + 1) * TBS]:
                                     e.tensor_add(o, mrope[0][:],
                                                  mrope[1][:]),
                                     w="r0", r=("m0", "m1"))
                            t_r = vec.emit(
                                lambda e,
                                o=bout[:, (p4 + 4) * TBS:(p4 + 5) * TBS]:
                                e.tensor_sub(o, mrope[3][:], mrope[2][:]),
                                inc=True, w="r1", r=("m2", "m3"))
                            t_rope.append(t_r)
                    t_rope_prev = t_rope[-1]
                    rcs.consumed(2 * tb + b, t_rope_prev)
                    rcs.consumed(2 * tb + 1 + b, t_rope_prev)

                    ksem = ktq_sems[tb % 2]
                    st_kq = []
                    for oc in range(CCN):
                        st_kq.append(syn.emit(
                            lambda e, oc=oc, tb=tb, b=b: e.dma_start(
                                out=dt["kt_s"][b, tb, :, oc, :],
                                in_=ktf[:, oc * TBS:(oc + 1) * TBS]),
                            waits=[t_rope[oc % 4]], inc=True, val=16,
                            sem=ksem))
                        st_kq.append(syn.emit(
                            lambda e, oc=oc, tb=tb, b=b: e.dma_start(
                                out=dt["qt_s"][b, 4 * tb:4 * tb + 4, :,
                                               oc * 128:(oc + 1) * 128]
                                .rearrange("j p f -> p j f"),
                                in_=qtf[:, oc * TBS:(oc + 1) * TBS]
                                .rearrange("p (j f) -> p j f", j=4)),
                            waits=[t_rope[4 + (oc % 4)]], inc=True, val=16,
                            sem=ksem))
                    t_store_prev = st_kq[-1]

                    for mh in range(2):
                        for tsub in range(4):
                            bank = 2 + (mh * 4 + tsub) % 2
                            vi = (mh * 4 + tsub) % 4
                            ten.emit(lambda e, mh=mh, bank=bank: e.matmul(
                                ps[bank][:], ones_r_b[:],
                                bvr[:, mh * 512:(mh + 1) * 512],
                                start=True, stop=False),
                                waits=[t_x, ps_free[bank]] +
                                ([t_wA[3], t_c[-1]] if tb == 0 else []))
                            for cc in range(CCN):
                                tmm = ten.emit(lambda e, cc=cc, mh=mh,
                                               tsub=tsub, bank=bank, sl=sl:
                                               e.matmul(
                                    ps[bank][:],
                                    xs[sl][:, cc * TBS + tsub * 128:
                                           cc * TBS + (tsub + 1) * 128],
                                    wv_sb[:, cc * C + mh * 512:
                                          cc * C + (mh + 1) * 512],
                                    start=False, stop=(cc == CCN - 1)),
                                    inc=(cc == CCN - 1))
                            tev = sca.emit(lambda e, vi=vi, bank=bank:
                                           e.activation(
                                out=vev[vi][:], in_=ps[bank][:],
                                func=AF.Identity),
                                waits=[tmm, t_vev_cons[vi]], inc=True)
                            ps_free[bank] = tev
                            t_st = syn.emit(
                                lambda e, tb=tb, tsub=tsub, mh=mh, vi=vi, b=b:
                                e.dma_start(
                                    out=dt["v_s"][b,
                                                  tb * TBS + tsub * 128:
                                                  tb * TBS +
                                                  (tsub + 1) * 128,
                                                  mh * 512:(mh + 1) * 512],
                                    in_=vev[vi][:]),
                                waits=[tev], inc=True, val=16,
                                sem=v_sems[vi])
                            t_vev_cons[vi] = t_st
                    rxs.consumed(tb + b * NTB, ten.tick())

                t_scr_done = (
                    [(s, P.sem_cnt[s.name]) for s in ktq_sems] +
                    [(s, P.sem_cnt[s.name]) for s in v_sems])

            P.scope = f"b{b}A2"
            P.barrier()
            if True:
                atsb = Bump(bump_a.off).maker()
                masks_a = atsb("masks", (128, 4, TBS), F32)
                st_sb = atsb("st", (128, 32 * TBS), BF16)
                kis = [atsb(f"kis{i}", (128, CCN * TBS), BF16)
                       for i in range(2)]
                qjs = [atsb(f"qjs{i}", (128, CCN * 128), BF16)
                       for i in range(4)]
                vjs = [atsb(f"vjs{i}", (128, C), BF16) for i in range(4)]
                mske = [atsb(f"mske{i}", (128, TBS), F32) for i in range(2)]
                esb = [atsb(f"esb{i}", (128, TBS), F32) for i in range(2)]
                ot_sb = atsb("ot", (128, CCN * TBS), BF16)
                xnt = [atsb(f"xnt{i}", (128, C), F32) for i in range(4)]
                yt = [atsb(f"yt{i}", (128, C), F32) for i in range(4)]
                x1t = [atsb(f"x1t{i}", (128, C), BF16) for i in range(4)]
                scrj = atsb("scrj", (128, C), BF16)

                t_msk = ld(syn, masks_a[:], dt["masks"])

                t_mske_cons = [None] * 2
                t_yt_cons = [None] * 4
                t_x1_cons = [[] for _ in range(4)]
                t_mvt_cons = [None] * 4

                pending_wp = None
                for I in range(NTB):
                    ksl = I % 2
                    t_ki = rki.load(gps, I, lambda e, I=I, ksl=ksl, b=b:
                                    e.dma_start(
                        out=kis[ksl][:].rearrange("p (a f) -> p a f", a=CCN),
                        in_=dt["kt_s"][b, I]), waits=t_scr_done)
                    NJ = 4 * I + 4
                    t_st = []
                    for j in range(NJ):
                        qsl = j % 4
                        t_qj = rqj.load(syn, j, lambda e, j=j, qsl=qsl, b=b:
                                        e.dma_start(
                            out=qjs[qsl][:]
                            .rearrange("p (a f) -> p a f", a=CCN),
                            in_=dt["qt_s"][b, j]
                            .rearrange("p (a f) -> p a f", a=CCN)),
                            waits=t_scr_done)
                        bank = j % 2
                        pairs = [(qjs[qsl][:, cc * 128:(cc + 1) * 128],
                                  kis[ksl][:, cc * TBS:(cc + 1) * TBS])
                                 for cc in range(CCN)]
                        tmm = mm_group(ps[bank][:], pairs, [t_qj, t_ki],
                                       bank)
                        rqj.consumed(j, tmm)
                        esl = j % 2
                        if j >= 4 * I:
                            a = j - 4 * I
                            t_m = vec.emit(lambda e, a=a, bank=bank, esl=esl:
                                           e.scalar_tensor_tensor(
                                out=mske[esl][:], in0=ps[bank][:],
                                scalar=ISCALE, in1=masks_a[:, a, :],
                                op0=ALU.mult, op1=ALU.add),
                                waits=[tmm, t_mske_cons[esl], t_msk],
                                inc=True, w=f"mske{esl}")
                            ps_free[bank] = t_m
                            t_e = sca.emit(lambda e, esl=esl: e.activation(
                                out=esb[esl][:], in_=mske[esl][:],
                                func=AF.Exp), waits=[t_m], inc=True)
                            t_mske_cons[esl] = t_e
                        else:
                            t_e = sca.emit(lambda e, bank=bank, esl=esl:
                                           e.activation(
                                out=esb[esl][:], in_=ps[bank][:],
                                func=AF.Exp, scale=ISCALE),
                                waits=[tmm], inc=True)
                            ps_free[bank] = t_e
                        t_l = sca.emit(lambda e, j=j, esl=esl: e.activation(
                            out=st_sb[:, j * TBS:(j + 1) * TBS],
                            in_=esb[esl][:], func=AF.Ln, bias=b1col[:]),
                            inc=True)
                        t_st.append(t_l)

                    rki.consumed(I, ten.tick())
                    if dbg and b == 0 and I == 1:
                        gps.emit(lambda e: e.dma_start(
                            out=dt["dbg_st"],
                            in_=st_sb[:, 0:8 * TBS]),
                            waits=[t_st[-1]], inc=True, val=16)

                    if pending_wp is not None:
                        pending_wp()
                        pending_wp = None

                    for j in range(NJ):
                        vsl = j % 4
                        t_vj = rvj.load(sca, j, lambda e, j=j, vsl=vsl, b=b:
                                        e.dma_start(
                            out=vjs[vsl][:],
                            in_=dt["v_s"][b, j * 128:(j + 1) * 128, :]),
                            waits=t_scr_done)
                        for d in range(CCN):
                            w0 = []
                            if j == 0:
                                w0 = [t_vj, t_st[-1], ps_free[d]]
                            elif d == 0:
                                w0 = [t_vj, t_st[j]]
                            tmm = ten.emit(lambda e, j=j, d=d, vsl=vsl, NJ=NJ:
                                           e.matmul(
                                ps[d][:],
                                vjs[vsl][:, d * 128:(d + 1) * 128],
                                st_sb[:, j * TBS:(j + 1) * TBS],
                                start=(j == 0), stop=(j == NJ - 1)),
                                waits=w0, inc=(d == CCN - 1))
                        rvj.consumed(j, tmm)
                    t_ot_mm = ten.tick()
                    t_ote = []
                    for d in range(CCN):
                        t_ev = sca.emit(lambda e, d=d: e.activation(
                            out=ot_sb[:, d * TBS:(d + 1) * TBS],
                            in_=ps[d][:], func=AF.Identity),
                            waits=[t_ot_mm], inc=True)
                        ps_free[d] = t_ev
                        t_ote.append(t_ev)

                    if dbg and b == 0 and I == 1:
                        gps.emit(lambda e: e.dma_start(
                            out=dt["dbg_ot"], in_=ot_sb[:]),
                            waits=[t_ote[-1]], inc=True, val=16)
                    def _wp(I=I, t_ote=t_ote, b=b):
                        t_y = [None] * 4
                        for isub in range(4):
                            gidx = I * 4 + isub
                            t_xn = rxn.load(sca, gidx,
                                            lambda e, gidx=gidx, isub=isub, b=b:
                                            e.dma_start(
                                out=xnt[isub][:],
                                in_=dt["x_nat"][b,
                                                gidx * 128:(gidx + 1) * 128,
                                                :]))
                            for half in range(2):
                                bank = 6 + (isub * 2 + half) % 2
                                ten.emit(lambda e, half=half, bank=bank:
                                         e.matmul(
                                    ps[bank][:], ones_r_b[:],
                                    bpr[:, half * 512:(half + 1) * 512],
                                    start=True, stop=False),
                                    waits=[ps_free[bank], t_wp,
                                           t_ote[-1]])
                                for mc in range(CCN):
                                    tmm = ten.emit(lambda e, mc=mc, isub=isub,
                                                   half=half, bank=bank:
                                                   e.matmul(
                                        ps[bank][:],
                                        ot_sb[:, mc * TBS + isub * 128:
                                              mc * TBS + (isub + 1) * 128],
                                        wp_sb[:, mc * C + half * 512:
                                              mc * C + (half + 1) * 512],
                                        start=False, stop=(mc == CCN - 1)),
                                        inc=(mc == CCN - 1))
                                t_yh = vec.emit(lambda e, isub=isub, half=half,
                                                bank=bank:
                                                e.scalar_tensor_tensor(
                                    out=yt[isub][:,
                                                 half * 512:(half + 1) * 512],
                                    in0=ps[bank][:], scalar=1.0,
                                    in1=xnt[isub][:,
                                                  half * 512:(half + 1) * 512],
                                    op0=ALU.mult, op1=ALU.add),
                                    waits=[tmm, t_xn, t_yt_cons[isub]],
                                    inc=True, w=f"y{isub}h{half}")
                                ps_free[bank] = t_yh
                                t_y[isub] = t_yh
                            rxn.consumed(gidx, t_y[isub])

                        for isub in range(4):
                            vec.emit(lambda e, i=isub: e.bn_stats(
                                stats[:, i * 12:i * 12 + 6], yt[i][:, 0:512]),
                                waits=[t_y[isub]], w=f"bnsa{isub}",
                                r=(f"y{isub}h0", f"y{isub}h1"))
                            vec.emit(lambda e, i=isub: e.bn_stats(
                                stats[:, i * 12 + 6:i * 12 + 12],
                                yt[i][:, 512:1024]), w=f"bnsb{isub}")
                        t_bn = [None] * 4
                        for isub in range(4):
                            t_bn[isub] = vec.emit(lambda e, i=isub: e.bn_aggr(
                                mvt[:, i * 2:i * 2 + 2],
                                stats[:, i * 12:i * 12 + 12]),
                                waits=[t_mvt_cons[isub]],
                                inc=True, w=f"mv{isub}",
                                r=(f"bnsa{isub}", f"bnsb{isub}"))
                        for isub in range(4):
                            gidx = I * 4 + isub
                            rcol = lncol[:, isub * 4:isub * 4 + 1]
                            nmr = lncol[:, isub * 4 + 1:isub * 4 + 2]
                            t2c = lncol[:, isub * 4 + 2:isub * 4 + 3]
                            lnv = lncol[:, isub * 4 + 3:isub * 4 + 4]
                            sca.emit(lambda e, i=isub, lnv=lnv: e.activation(
                                out=lnv, in_=mvt[:, i * 2 + 1:i * 2 + 2],
                                func=AF.Ln, bias=epsln[:]),
                                waits=[t_bn[isub]])
                            t_r = sca.emit(lambda e, rcol=rcol, lnv=lnv:
                                           e.activation(
                                out=rcol, in_=lnv, func=AF.Exp, scale=-0.5),
                                inc=True)
                            t_mvt_cons[isub] = t_r
                            vec.emit(lambda e, t2c=t2c, rcol=rcol: e.tensor_mul(
                                t2c, rcol, rcol), waits=[t_r], w=f"t2{isub}")
                            vec.emit(lambda e, i=isub, t2c=t2c: e.tensor_mul(
                                t2c, mvt[:, i * 2 + 1:i * 2 + 2], t2c),
                                w=f"t2b{isub}", r=(f"t2{isub}",))
                            vec.emit(lambda e, i=isub, nmr=nmr, rcol=rcol:
                                     e.tensor_mul(
                                nmr, mvt[:, i * 2:i * 2 + 1], rcol),
                                w=f"nm{isub}")
                            t_nmr = vec.emit(lambda e, nmr=nmr:
                                             e.tensor_scalar_mul(
                                nmr, nmr, -1.0), inc=True, w=f"nm2{isub}",
                                r=(f"nm{isub}",))
                            sca.emit(lambda e, t2c=t2c, lnv=lnv: e.activation(
                                out=lnv, in_=t2c, func=AF.Ln, bias=epsln4[:]),
                                waits=[t_nmr])
                            t_s = sca.emit(lambda e, gidx=gidx, lnv=lnv:
                                           e.activation(
                                out=smat[:, gidx:gidx + 1], in_=lnv,
                                func=AF.Exp, scale=-0.5), inc=True)
                            t_x1 = sca.emit(lambda e, i=isub, rcol=rcol,
                                            nmr=nmr: e.activation(
                                out=x1t[i][:], in_=yt[i][:], func=AF.Identity,
                                scale=rcol, bias=nmr),
                                waits=list(t_x1_cons[isub]), inc=True)
                            t_yt_cons[isub] = t_x1
                            t_sc = vec.emit(lambda e, i=isub, gidx=gidx:
                                            e.scalar_tensor_tensor(
                                out=scrj[:], in0=x1t[i][:], scalar=1.0,
                                in1=wrrep[:], op0=ALU.mult, op1=ALU.mult,
                                accum_out=scoremat[:, gidx:gidx + 1]),
                                waits=[t_x1, t_grep, t_eps], inc=True, w="scr")
                            t_w1s = syn.emit(lambda e, i=isub, gidx=gidx, b=b:
                                             e.dma_start(
                                out=dt["x1_s"][b,
                                               gidx * 128:(gidx + 1) * 128, :],
                                in_=x1t[i][:]), waits=[t_x1], inc=True, val=16,
                                sem=x1_sems[isub])
                            t_x1_cons[isub] = [t_sc, t_w1s]
                    pending_wp = _wp

                pending_wp()

                t_x1_done = [(s, P.sem_cnt[s.name]) for s in x1_sems]
                t_score_done = vec.tick()
                t_smat_done = sca.tick()

        P.scope = f"b{b}SEL"
        P.barrier()
        if True:
            bump_f = Bump(0)
            fsb = bump_f.maker()
            xslot = [fsb(f"xslot{a}", (128, C), BF16) for a in range(4)]
            bisdmp = fsb("bisdmp", (128, 4 * NBIS), F32)

            P.scope = f"b{b}SELbis"
            vec.pad = True
            t_abs = vec.emit(lambda e: e.tensor_reduce(
                abscol[:], scoremat[:], AX.X, ALU.add,
                apply_absolute_value=True),
                waits=[t_score_done], inc=True, w="abs")
            t_bnd = mm_group(ps[7][:, 0:1], [(ones_sq_f[:], abscol[:])],
                             [t_abs], 7)
            vec.emit(lambda e: e.tensor_scalar_mul(
                locol[:], ps[7][:, 0:1], -1.0), waits=[t_bnd], w="lo")
            t_bis = vec.emit(lambda e: e.tensor_copy(hicol[:],
                                                     ps[7][:, 0:1]),
                             inc=True, w="hi")
            ps_free[7] = t_bis
            for it in range(NBIS):
                vec.emit(lambda e: e.tensor_add(midA[:], locol[:],
                                                hicol[:]),
                         waits=[t_bis], w="midA", r=("lo", "hi"))
                vec.emit(lambda e: e.tensor_scalar_mul(midB[:], midA[:],
                                                       0.5),
                         w="midB", r=("midA",))
                vec.emit(lambda e: e.tensor_single_scalar(
                    cmp32[:], scoremat[:], midB[:], ALU.is_gt),
                    w="cmp", r=("midB",))
                t_cc = vec.emit(lambda e: e.tensor_reduce(
                    cntcol[:], cmp32[:], AX.X, ALU.add),
                    inc=True, w="cnt", r=("cmp",))
                t_tot = mm_group(ps[7][:, 0:1],
                                 [(ones_sq_f[:], cntcol[:])], [t_cc], 7)
                vec.emit(lambda e: e.tensor_single_scalar(
                    cge[:], ps[7][:, 0:1], 512.0, ALU.is_ge),
                    waits=[t_tot], w="cge")
                vec.emit(lambda e: e.tensor_single_scalar(
                    clt[:], ps[7][:, 0:1], 512.0, ALU.is_lt), w="clt")
                vec.emit(lambda e: e.select(
                    locol[:], cge[:], midB[:], locol[:]),
                    w="lo", r=("cge",))
                t_bis = vec.emit(lambda e: e.select(
                    hicol[:], clt[:], midB[:], hicol[:]),
                    inc=True, w="hi", r=("clt",))
                ps_free[7] = t_bis
                if dbg:
                    vec.emit(lambda e, it=it: e.tensor_copy(
                        bisdmp[:, 4 * it:4 * it + 1], cntcol[:]), w="db0")
                    vec.emit(lambda e, it=it: e.tensor_copy(
                        bisdmp[:, 4 * it + 1:4 * it + 2], ps[7][:, 0:1]),
                        w="db1")
                    vec.emit(lambda e, it=it: e.tensor_copy(
                        bisdmp[:, 4 * it + 2:4 * it + 3], locol[:]),
                        w="db2")
                    t_bis = vec.emit(lambda e, it=it: e.tensor_copy(
                        bisdmp[:, 4 * it + 3:4 * it + 4], hicol[:]),
                        inc=True, w="db3")

            P.scope = f"b{b}SELpfx"
            t_sel = vec.emit(lambda e: e.tensor_single_scalar(
                selm[:], scoremat[:], locol[:], ALU.is_gt),
                inc=True, w="selm", r=("lo",))
            if dbg:
                gps.emit(lambda e, b=b: e.dma_start(
                    out=dt["dbg_bis"][b], in_=bisdmp[:]),
                    waits=[t_sel], inc=True, val=16)
            if dbg:
                gps.emit(lambda e, b=b: e.dma_start(
                    out=dt["dbg_selm"][b], in_=selm[:]),
                    waits=[t_sel], inc=True, val=16)
                gps.emit(lambda e, b=b: e.dma_start(
                    out=dt["dbg_lo"][b], in_=locol[:]),
                    waits=[t_sel], inc=True, val=16)
            t_tr1 = ten.emit(lambda e: e.transpose(
                ps[6][0:32, 0:128], selm[:], identf[:]),
                waits=[t_sel, ps_free[6]], inc=True)
            t_selt = vec.emit(lambda e: e.tensor_copy(
                selt[:], ps[6][0:32, 0:128]), waits=[t_tr1], inc=True,
                w="selt")
            ps_free[6] = t_selt
            if dbg:
                gps.emit(lambda e, b=b: e.dma_start(
                    out=dt["dbg_selt"][b], in_=selt[:]),
                    waits=[t_selt], inc=True, val=16)
            cur, nxt = pret[0], pret[1]
            t_pf = vec.emit(lambda e, cc_=cur: e.tensor_copy(cc_[:], selt[:]),
                            waits=[t_selt], inc=True, w="pf1")
            for sh in (1, 2, 4, 8, 16, 32, 64):
                vec.emit(lambda e, sh=sh, cc_=cur, nn=nxt: e.tensor_copy(
                    nn[:, 0:sh], cc_[:, 0:sh]), w="pfc", r=("pf1", "pfa"))
                t_pf = vec.emit(lambda e, sh=sh, cc_=cur, nn=nxt:
                                e.tensor_add(
                    nn[:, sh:128], cc_[:, sh:128], cc_[:, 0:128 - sh]),
                    inc=True, w="pfa", r=("pfc",))
                cur, nxt = nxt, cur
            t_pos = vec.emit(lambda e, cc_=cur: e.tensor_sub(
                post_t[:], cc_[:], selt[:]), waits=[t_pf], inc=True,
                w="pos", r=("pfa",))
            if dbg:
                gps.emit(lambda e, b=b: e.dma_start(
                    out=dt["dbg_pos"][b], in_=post_t[:]),
                    waits=[t_pos], inc=True, val=16)
            vec.emit(lambda e: e.memset(rt32[:], 0.0), w="rt0")
            vec.emit(lambda e, cc_=cur: e.tensor_copy(
                rt32[:, 0:1], cc_[:, 127:128]), w="rt", r=("rt0",))
            t_rt = vec.emit(lambda e: e.transpose(rr32[:], rt32[:]),
                            inc=True, w="rr", r=("rt",))
            e0, e1 = er32[0], er32[1]
            vec.emit(lambda e, a=e0: e.memset(a[:], 0.0), w="e0")
            vec.emit(lambda e, a=e0: e.tensor_copy(
                a[:, 1:32], rr32[0:1, 0:31]), waits=[t_rt], w="e0b",
                r=("rr", "e0"))
            for sh in (1, 2, 4, 8, 16):
                vec.emit(lambda e, sh=sh, a=e0, bb=e1: e.tensor_copy(
                    bb[:, 0:sh], a[:, 0:sh]), w="epc", r=("e0b", "epa"))
                vec.emit(lambda e, sh=sh, a=e0, bb=e1: e.tensor_add(
                    bb[:, sh:32], a[:, sh:32], a[:, 0:32 - sh]),
                    w="epa", r=("epc",))
                e0, e1 = e1, e0
            vec.emit(lambda e: e.memset(rr32[:], 0.0), w="rrz",
                     r=("epa",))
            vec.emit(lambda e, a=e0: e.tensor_copy(rr32[0:1, :], a[:]),
                     w="rr2", r=("rrz", "epa"))
            t_off = vec.emit(lambda e: e.transpose(rt32[:], rr32[:]),
                             inc=True, w="off", r=("rr2",))
            vec.emit(lambda e: e.tensor_copy(offc[:], rt32[:, 0:1]),
                     waits=[t_off], w="offc", r=("off",))
            vec.emit(lambda e: e.tensor_single_scalar(
                slott[:], post_t[:], offc[:], ALU.add),
                w="slott", r=("offc", "pos"))
            vec.emit(lambda e: e.tensor_single_scalar(
                nselu[:], selt[:], 0.5, ALU.is_lt), w="nsel")
            t_slot = vec.emit(lambda e: e.select(
                slott[:], nselu[:], c512t[:], slott[:]),
                inc=True, w="slott2", r=("nsel", "slott"))
            if dbg:
                gps.emit(lambda e, b=b: e.dma_start(
                    out=dt["dbg_offc"][b], in_=offc[:]),
                    waits=[t_slot], inc=True, val=16)
                gps.emit(lambda e, b=b: e.dma_start(
                    out=dt["dbg_slott"][b], in_=slott[:]),
                    waits=[t_slot], inc=True, val=16)
            t_tr2 = ten.emit(lambda e: e.transpose(
                ps[6][0:128, 0:32], slott[:], identf[0:32, 0:32]),
                waits=[t_slot, ps_free[6]], inc=True)
            t_slotc = vec.emit(lambda e: e.tensor_copy(
                slotc[:], ps[6][0:128, 0:32]), waits=[t_tr2], inc=True,
                w="slotc")
            ps_free[6] = t_slotc

            def build_CA(CA_list, waits0, mksb):
                t_out = [None] * 4
                srep = mksb("srepscr", (128, 512), F32)
                t_s_cons = None
                pad_sav = vec.pad
                vec.pad = True
                for kk in range(8):
                    t_row = syn.emit(lambda e, kk=kk: e.dma_start(
                        out=rcflat[0:1, 0:512]
                        .rearrange("o (c p) -> o c p", c=4),
                        in_=slott[4 * kk:4 * kk + 4, :]),
                        waits=waits0 + [t_s_cons], inc=True, val=16)
                    t_mm = mm_group(ps[6][:],
                                    [(ones_r_f[:], rcflat[0:1, 0:512])],
                                    [t_row], 6)
                    t_se = vec.emit(lambda e: e.tensor_copy(
                        srep[:], ps[6][:]), waits=[t_mm], inc=True,
                        w="srep")
                    ps_free[6] = t_se
                    t_s_cons = t_mm
                    for a in range(4):
                        t_out[a] = vec.emit(lambda e, a=a, kk=kk:
                                            e.tensor_single_scalar(
                            CA_list[a][:, kk * 512:(kk + 1) * 512],
                            srep[:], iotacols[:, a:a + 1], ALU.is_equal),
                            waits=[t_se], inc=True, w=f"cab{a}",
                            r=("srep",))
                    t_s_cons = t_out[3]
                vec.pad = pad_sav
                return t_out

            P.scope = f"b{b}SELf1"
            if True:
                f1sb = Bump(bump_f.off).maker()
                tri_f = f1sb("tri", (128, 4, TBS), F32)
                slotio = f1sb("slotio", (128, 512), F32)
                CT = [f1sb(f"CT{c}", (128, 512), BF16) for c in range(32)]
                x1c = [f1sb(f"x1c{i}", (128, C), BF16) for i in range(4)]
                gtt = f1sb("gtt", (128, 512), F32)
                eqt = f1sb("eqt", (128, 512), F32)
                eqm = f1sb("eqm", (128, 512), F32)
                screp = f1sb("screp", (128, 512), F32)

                t_tri = ld(syn, tri_f[:], dt["tri"])
                t_sio = ld(syn, slotio[:], dt["slotiota"])
                t_trisio = [t_tri, t_sio]

                t_CT = [None] * 32
                for c in range(32):
                    t_CT[c] = vec.emit(lambda e, c=c:
                                       e.tensor_single_scalar(
                        CT[c][:], slotio[:], slotc[:, c:c + 1],
                        ALU.is_equal),
                        waits=[t_slotc] + t_trisio, inc=True, w=f"ct{c}")

                t_scb = vec.emit(lambda e: e.tensor_copy(scb[:],
                                                         scoremat[:]),
                                 waits=[t_score_done], inc=True, w="scb")
                t_ssb = vec.emit(lambda e: e.tensor_copy(ssb[:], smat[:]),
                                 waits=[t_smat_done], inc=True, w="ssb")

                t_cols = []
                for (srcb, dstcol, tsrc) in ((scb, sccol, t_scb),
                                             (ssb, sscol, t_ssb)):
                    tmm = None
                    for a in range(4):
                        pairs = [(CT[c][:, a * 128:(a + 1) * 128],
                                  srcb[:, c:c + 1]) for c in range(32)]
                        tmm = mm_group(ps[6][:, a:a + 1], pairs,
                                       [tsrc] + t_CT, 6, skip_check=True)
                    t_cv = vec.emit(lambda e, dstcol=dstcol: e.tensor_copy(
                        dstcol[:], ps[6][:, 0:4]), waits=[tmm], inc=True,
                        w="cols")
                    ps_free[6] = t_cv
                    t_cols.append(t_cv)
                t_nss = vec.emit(lambda e: e.tensor_scalar_mul(
                    nsscol[:], sscol[:], -1.0), waits=[t_cols[1]],
                    inc=True, w="nss", r=("cols",))

                def row_replicate(src4, t_src, dstrep, extra_w=()):
                    t1 = vec.emit(lambda e: e.tensor_copy(cmp32[:, 0:4],
                                                          src4[:]),
                                  waits=[t_src] + list(extra_w), inc=True,
                                  w="rrc", r=("cols", "nss"))
                    t2 = ten.emit(lambda e: e.transpose(
                        ps[6][0:32, 0:128], cmp32[:, 0:32], identf[:]),
                        waits=[t1, ps_free[6]], inc=True)
                    t3 = vec.emit(lambda e: e.tensor_copy(
                        scr32[:], ps[6][0:32, 0:128]), waits=[t2],
                        inc=True, w="rrd")
                    ps_free[6] = t3
                    t4 = syn.emit(lambda e: e.dma_start(
                        out=rcflat[0:1, 0:512]
                        .rearrange("o (a p) -> o a p", a=4),
                        in_=scr32[0:4, :]),
                        waits=[t3], inc=True, val=16)
                    t5 = mm_group(ps[6][:],
                                  [(ones_r_f[:], rcflat[0:1, 0:512])],
                                  [t4], 6)
                    t6 = vec.emit(lambda e: e.tensor_copy(dstrep[:],
                                                          ps[6][:]),
                                  waits=[t5], inc=True, w="rre")
                    ps_free[6] = t6
                    return t6

                t_screp = row_replicate(sccol, t_cols[0], screp)
                t_rank = [None] * 4
                for a in range(4):
                    vec.emit(lambda e, a=a: e.tensor_single_scalar(
                        gtt[:], screp[:], sccol[:, a:a + 1], ALU.is_gt),
                        waits=[t_screp, t_cols[0]] + t_trisio, w="gtt",
                        r=("rre",))
                    vec.emit(lambda e, a=a: e.tensor_single_scalar(
                        eqt[:], screp[:], sccol[:, a:a + 1],
                        ALU.is_equal), w="eqt")
                    vec.emit(lambda e, a=a: e.tensor_mul(
                        eqm[:], eqt[:], tri_f[:, a, :]), w="eqm",
                        r=("eqt",))
                    vec.emit(lambda e: e.tensor_add(eqm[:], eqm[:],
                                                    gtt[:]),
                             w="eqm2", r=("eqm", "gtt"))
                    t_rank[a] = vec.emit(lambda e, a=a: e.tensor_reduce(
                        rankc[:, a:a + 1], eqm[:], AX.X, ALU.add),
                        inc=True, w=f"rk{a}", r=("eqm2",))
                t_rkrep = row_replicate(rankc, t_rank[-1], rankrep,
                                        extra_w=t_rank)
                t_minn = [None] * 4
                for a in range(4):
                    t_minn[a] = vec.emit(lambda e, a=a:
                                         e.tensor_single_scalar(
                        minn[a][:], rankrep[:], rankc[:, a:a + 1],
                        ALU.is_ge),
                        waits=[t_rkrep], inc=True, w=f"mi{a}", r=("rre",))
                vec.pad = False

                if dbg:
                    syn.emit(lambda e, b=b: e.dma_start(
                        out=dt["dbg_score"][b], in_=scoremat[:]),
                        waits=[t_score_done], inc=True, val=16)
                    syn.emit(lambda e, b=b: e.dma_start(
                        out=dt["dbg_smat"][b], in_=smat[:]),
                        waits=[t_smat_done], inc=True, val=16)
                    syn.emit(lambda e, b=b: e.dma_start(
                        out=dt["dbg_slot"][b], in_=slotc[:]),
                        waits=[t_slotc], inc=True, val=16)
                    syn.emit(lambda e, b=b: e.dma_start(
                        out=dt["dbg_rank"][b], in_=rankc[:]),
                        waits=t_rank, inc=True, val=16)

                P.scope = f"b{b}SELgat"
                t_xslot = [None] * 4
                for hp in range(2):
                    tmm_l = [[None, None], [None, None]]
                    for c in range(32):
                        csl = c % 4
                        t_xc = rx1.load(syn if c % 2 == 0 else gps,
                                        c + 32 * hp,
                                        lambda e, c=c, csl=csl, b=b:
                                        e.dma_start(
                            out=x1c[csl][:],
                            in_=dt["x1_s"][b, c * 128:(c + 1) * 128, :]),
                            waits=t_x1_done)
                        for ai in range(2):
                            a = hp * 2 + ai
                            for half in range(2):
                                bank = ai * 2 + half
                                w0 = []
                                if c == 0:
                                    w0 = [t_xc, t_CT[c], ps_free[bank]]
                                elif ai == 0 and half == 0:
                                    w0 = [t_xc, t_CT[c]]
                                tmm_l[ai][half] = ten.emit(
                                    lambda e, c=c, a=a, half=half, csl=csl,
                                    bank=bank: e.matmul(
                                        ps[bank][:],
                                        CT[c][:, a * 128:(a + 1) * 128],
                                        x1c[csl][:, half * 512:
                                                 (half + 1) * 512],
                                        start=(c == 0), stop=(c == 31)),
                                    waits=w0,
                                    inc=(c == 31 or (ai == 1 and
                                                     half == 1)))
                        rx1.consumed(c + 32 * hp, tmm_l[1][1])
                    for ai in range(2):
                        a = hp * 2 + ai
                        t_xslot[a] = sca.emit(lambda e, a=a, ai=ai: (
                            e.activation(out=xslot[a][:, 0:512],
                                         in_=ps[ai * 2][:],
                                         func=AF.Identity),
                            e.activation(out=xslot[a][:, 512:1024],
                                         in_=ps[ai * 2 + 1][:],
                                         func=AF.Identity))[-1],
                            waits=[tmm_l[ai][0], tmm_l[ai][1]], inc=True)
                        ps_free[ai * 2] = t_xslot[a]
                        ps_free[ai * 2 + 1] = t_xslot[a]

            P.scope = f"b{b}F2a"
            P.barrier()
            if True:
                bump_f2 = Bump(bump_f.off)
                f2sb = bump_f2.maker()
                hf = [f2sb(f"hf{i}", (128, C), F32) for i in range(2)]
                fjunk = f2sb("fjunk", (128, C), F32)
                flnout = [f2sb(f"flnout{i}", (128, C), F32)
                          for i in range(2)]
                htb = f2sb("htb", (128, CCN * 512), BF16)
                s2 = [f2sb(f"s2_{a}", (128, C), BF16) for a in range(4)]

                bump_f2a = Bump(bump_f2.off)
                f2asb = bump_f2a.maker()
                qkT = f2asb("qkT", (128, 16 * 512), BF16)
                vnat = [f2asb(f"vnat{a}", (128, C), BF16)
                        for a in range(4)]
                expd = [f2asb(f"expd{i}", (128, 512), BF16)
                        for i in range(4)]
                esbf = [f2asb(f"esbf{i}", (128, 512), F32)
                        for i in range(2)]
                rcrep = [f2asb(f"rcrep{i}", (128, 512), BF16)
                         for i in range(4)]
                ot2 = f2asb("ot2", (128, CCN * 512), BF16)
                wvh = [f2asb(f"wvh{i}", (128, CCN * 512), BF16)
                       for i in range(2)]
                wqk = [f2asb(f"wqk{i}", (128, C), BF16) for i in range(2)]


                t_hf_cons = [None] * 2
                t_hTe = None
                for a in range(4):
                    rcol = lncol[:, a * 4:a * 4 + 1]
                    lnv = lncol[:, a * 4 + 1:a * 4 + 2]
                    t_sq = vec.emit(lambda e, a=a: e.scalar_tensor_tensor(
                        out=fjunk[:], in0=xslot[a][:], scalar=1.0,
                        in1=xslot[a][:], op0=ALU.mult, op1=ALU.mult,
                        accum_out=lncol[:, a * 4 + 2:a * 4 + 3]),
                        waits=[t_xslot[a]], inc=True, w="hsq")
                    sca.emit(lambda e, a=a, lnv=lnv: e.activation(
                        out=lnv, in_=lncol[:, a * 4 + 2:a * 4 + 3],
                        func=AF.Ln, scale=1.0 / C, bias=epsrms[:]),
                        waits=[t_sq])
                    t_r = sca.emit(lambda e, rcol=rcol, lnv=lnv:
                                   e.activation(
                        out=rcol, in_=lnv, func=AF.Exp, scale=-0.5),
                        inc=True)
                    t_h = vec.emit(lambda e, a=a, rcol=rcol:
                                   e.scalar_tensor_tensor(
                        out=hf[a % 2][:], in0=xslot[a][:], scalar=rcol,
                        op0=ALU.mult, in1=g1rep[:], op1=ALU.mult),
                        waits=[t_r, t_hf_cons[a % 2]], inc=True,
                        w=f"hf{a % 2}")
                    t_tr = None
                    for cc in range(CCN):
                        bank = 6 + (cc % 2)
                        t_tr = ten.emit(lambda e, a=a, cc=cc, bank=bank:
                                        e.transpose(
                            ps[bank][0:128, 0:128],
                            hf[a % 2][:, cc * 128:(cc + 1) * 128],
                            identf[:]),
                            waits=[t_h, ps_free[bank]], inc=True)
                        t_hTe = sca.emit(lambda e, a=a, cc=cc, bank=bank:
                                         e.activation(
                            out=htb[:, cc * 512 + a * 128:
                                    cc * 512 + (a + 1) * 128],
                            in_=ps[bank][0:128, 0:128], func=AF.Identity),
                            waits=[t_tr], inc=True)
                        ps_free[bank] = t_hTe
                    t_hf_cons[a % 2] = t_tr
                t_hT_done = t_hTe

                t_qk = [None] * 16
                for mc in range(16):
                    wsl = mc % 2
                    t_w = rwq.load(gps, mc, lambda e, mc=mc, wsl=wsl:
                                   e.dma_start(
                        out=wqk[wsl][:]
                        .rearrange("p (a f) -> p a f", a=CCN),
                        in_=dt["wqkv_t"][mc]))
                    bank = mc % 2
                    pairs = [(wqk[wsl][:, cc * 128:(cc + 1) * 128],
                              htb[:, cc * 512:(cc + 1) * 512])
                             for cc in range(CCN)]
                    tmm = mm_group(ps[bank][:], pairs, [t_w, t_hT_done],
                                   bank)
                    rwq.consumed(mc, tmm)
                    t_qk[mc] = sca.emit(lambda e, mc=mc, bank=bank:
                                        e.activation(
                        out=qkT[:, mc * 512:(mc + 1) * 512],
                        in_=ps[bank][:], func=AF.Identity),
                        waits=[tmm], inc=True)
                    ps_free[bank] = t_qk[mc]

                t_vn = [None] * 4
                t_vn_mm = None
                for mh in range(2):
                    t_wv = rwv.load(gps, mh, lambda e, mh=mh: e.dma_start(
                        out=wvh[mh % 2][:]
                        .rearrange("p (a f) -> p a f", a=CCN),
                        in_=dt["wv2"][:, :, mh * 512:(mh + 1) * 512]
                        .rearrange("p a f -> p a f")))
                    for a in range(4):
                        bank = 2 + a % 2
                        pairs = [(htb[:, cc * 512 + a * 128:
                                      cc * 512 + (a + 1) * 128],
                                  wvh[mh % 2][:, cc * 512:(cc + 1) * 512])
                                 for cc in range(CCN)]
                        t_vn_mm = mm_group(ps[bank][:], pairs,
                                           [t_wv, t_hT_done], bank)
                        t_vn[a] = sca.emit(lambda e, a=a, mh=mh, bank=bank:
                                           e.activation(
                            out=vnat[a][:, mh * 512:(mh + 1) * 512],
                            in_=ps[bank][:], func=AF.Identity),
                            waits=[t_vn_mm], inc=True)
                        ps_free[bank] = t_vn[a]
                    rwv.consumed(mh, t_vn_mm)

                t_expd_cons = [None] * 4
                t_esbf_cons = [None] * 2
                t_rcflat_cons = None
                t_ot2e = []
                for g in range(2):
                    t_ote_g = []
                    t_rc_g = []
                    t_cs_last = None
                    for hh in range(8):
                        h = g * 8 + hh
                        qmc, qoff = h // 2, 64 * (h % 2)
                        kmc = 8 + h // 2
                        t_ex = [None] * 4
                        for tkc in range(4):
                            bank = 4 + (tkc % 2)
                            pairs = [(qkT[qoff:qoff + 64,
                                          kmc * 512 + tkc * 128:
                                          kmc * 512 + (tkc + 1) * 128],
                                      qkT[qoff:qoff + 64,
                                          qmc * 512:(qmc + 1) * 512])]
                            tmm = mm_group(ps[bank][:], pairs,
                                           [t_qk[qmc], t_qk[kmc]], bank)
                            t_e = sca.emit(lambda e, tkc=tkc, bank=bank:
                                           e.activation(
                                out=esbf[tkc % 2][:], in_=ps[bank][:],
                                func=AF.Exp, scale=0.125),
                                waits=[tmm, t_esbf_cons[tkc % 2]],
                                inc=True)
                            ps_free[bank] = t_e
                            t_ex[tkc] = vec.emit(lambda e, tkc=tkc:
                                                 e.tensor_mul(
                                expd[tkc][:], esbf[tkc % 2][:],
                                minn[tkc][:]),
                                waits=[t_e, t_minn[tkc],
                                       t_expd_cons[tkc]],
                                inc=True, w=f"ex{tkc}")
                            t_esbf_cons[tkc % 2] = t_ex[tkc]
                        for tqs in range(4):
                            pairs = [(expd[tkc][:,
                                               tqs * 128:(tqs + 1) * 128],
                                      ones_c_b[:]) for tkc in range(4)]
                            t_cs_last = mm_group(
                                ps[6][:, 4 * hh + tqs:4 * hh + tqs + 1],
                                pairs, t_ex + [t_c[3]], 6,
                                skip_check=True)
                        t_rc = vec.emit(lambda e, hh=hh: e.reciprocal(
                            rcall[:, 4 * hh:4 * hh + 4],
                            ps[6][:, 4 * hh:4 * hh + 4]),
                            waits=[t_cs_last], inc=True, w=f"rc{hh}")
                        t_rc_g.append(t_rc)
                        bank = hh // 2
                        poff = 64 * (hh % 2)
                        pairs = [(vnat[tkc][:, h * 64:h * 64 + 64],
                                  expd[tkc][:]) for tkc in range(4)]
                        tmm = mm_group(ps[bank][poff:poff + 64, :], pairs,
                                       list(t_vn) + t_ex,
                                       bank if hh % 2 == 0 else None)
                        t_ote_g.append(tmm)
                        for tkc in range(4):
                            t_expd_cons[tkc] = tmm
                    t1 = vec.emit(lambda e: e.tensor_copy(cmp32[:],
                                                          rcall[:]),
                                  waits=t_rc_g, inc=True, w="rcc",
                                  r=tuple(f"rc{i}" for i in range(8)))
                    t2 = ten.emit(lambda e: e.transpose(
                        ps[6][0:32, 0:128], cmp32[:], identf[:]),
                        waits=[t1, ps_free[6]], inc=True)
                    t3 = vec.emit(lambda e: e.tensor_copy(
                        scr32[:], ps[6][0:32, 0:128]), waits=[t2],
                        inc=True, w="rcd")
                    ps_free[6] = t3
                    for pp in range(4):
                        mc = g * 4 + pp
                        t4 = syn.emit(lambda e, pp=pp: e.dma_start(
                            out=rcflat[0:1, :]
                            .rearrange("o (c p) -> o c p", c=8),
                            in_=scr32[8 * pp:8 * pp + 8, :]),
                            waits=[t3, t_rcflat_cons], inc=True, val=16)
                        t_m1 = ten.emit(lambda e: e.matmul(
                            ps[6][0:64, :], ones_r_f[0:1, 0:64],
                            rcflat[0:1, 0:512],
                            start=True, stop=True, skip_group_check=True),
                            waits=[t4, ps_free[6]], inc=True)
                        t_m2 = ten.emit(lambda e: e.matmul(
                            ps[6][64:128, :], ones_r_f[0:1, 0:64],
                            rcflat[0:1, 512:1024],
                            start=True, stop=True, skip_group_check=True),
                            inc=True)
                        t_rcflat_cons = t_m2
                        t_rr = vec.emit(lambda e, pp=pp: e.tensor_copy(
                            rcrep[pp][:], ps[6][:]), waits=[t_m2],
                            inc=True, w=f"rcr{pp}")
                        ps_free[6] = t_rr
                        t_oe = vec.emit(lambda e, mc=mc, pp=pp:
                                        e.tensor_mul(
                            ot2[:, mc * 512:(mc + 1) * 512],
                            ps[pp][:], rcrep[pp][:]),
                            waits=[t_ote_g[2 * pp], t_ote_g[2 * pp + 1],
                                   t_rr],
                            inc=True, w=f"oe{mc}", r=(f"rcr{pp}",))
                        ps_free[pp] = t_oe
                        t_ot2e.append(t_oe)

                t_s2 = [None] * 4
                for half in range(2):
                    t_wo = rwv.load(gps, 2 + half, lambda e, half=half:
                                    e.dma_start(
                        out=wvh[half % 2][:]
                        .rearrange("p (a f) -> p a f", a=CCN),
                        in_=dt["wout_t"][:, :, half * 512:(half + 1) * 512]
                        .rearrange("p a f -> p a f")))
                    for a in range(4):
                        bank = (a * 2 + half) % 4
                        pairs = [(ot2[:, mc * 512 + a * 128:
                                      mc * 512 + (a + 1) * 128],
                                  wvh[half % 2][:,
                                                mc * 512:(mc + 1) * 512])
                                 for mc in range(CCN)]
                        tmm = mm_group(ps[bank][:], pairs,
                                       [t_wo, t_ot2e[-1]], bank)
                        t_s2h = vec.emit(lambda e, a=a, half=half,
                                         bank=bank:
                                         e.scalar_tensor_tensor(
                            out=s2[a][:, half * 512:(half + 1) * 512],
                            in0=ps[bank][:], scalar=1.0,
                            in1=xslot[a][:, half * 512:(half + 1) * 512],
                            op0=ALU.mult, op1=ALU.add),
                            waits=[tmm], inc=True, w=f"s2_{a}{half}")
                        ps_free[bank] = t_s2h
                        t_s2[a] = t_s2h
                    rwv.consumed(2 + half, tmm)

                P.scope = f"b{b}F2b"
                P.barrier()
                f2bsb = Bump(bump_f2.off).maker()
                CA = [f2bsb(f"CA{a}", (128, T), BF16) for a in range(4)]
                fT = [f2bsb(f"fT{m}", (128, 512), BF16)
                      for m in range(MCF)]
                wfa = [f2bsb(f"wfa{i}", (128, C), BF16) for i in range(2)]
                wfb = [f2bsb(f"wfb{i}", (128, C), BF16) for i in range(2)]
                w3s = [f2bsb(f"w3s{i}", (128, C), BF16) for i in range(2)]
                silt = [f2bsb(f"silt{i}", (128, 512), BF16)
                        for i in range(2)]
                selout = [f2bsb(f"selout{i}", (128, C), F32)
                          for i in range(2)]
                delta = [f2bsb(f"delta{a}", (128, C), BF16)
                         for a in range(4)]
                sdt = [f2bsb(f"sdt{i}", (128, C), F32) for i in range(2)]
                sxt = [f2bsb(f"sxt{i}", (128, C), BF16) for i in range(2)]
                uh = [f2bsb(f"uh{i}", (128, TBS), F32) for i in range(2)]

                t_CA = build_CA(CA, [t_slot], f2bsb)

                t_h2Te = None
                for a in range(4):
                    rcol = lncol[:, a * 4 + 3:a * 4 + 4]
                    lnv = lncol[:, a * 4 + 2:a * 4 + 3]
                    t_sq = vec.emit(lambda e, a=a: e.scalar_tensor_tensor(
                        out=fjunk[:], in0=s2[a][:], scalar=1.0,
                        in1=s2[a][:], op0=ALU.mult, op1=ALU.mult,
                        accum_out=lncol[:, a * 4 + 1:a * 4 + 2]),
                        waits=[t_s2[a]], inc=True, w="h2sq")
                    sca.emit(lambda e, a=a, lnv=lnv: e.activation(
                        out=lnv, in_=lncol[:, a * 4 + 1:a * 4 + 2],
                        func=AF.Ln, scale=1.0 / C, bias=epsrms[:]),
                        waits=[t_sq])
                    t_r = sca.emit(lambda e, rcol=rcol, lnv=lnv:
                                   e.activation(
                        out=rcol, in_=lnv, func=AF.Exp, scale=-0.5),
                        inc=True)
                    t_h2 = vec.emit(lambda e, a=a, rcol=rcol:
                                    e.scalar_tensor_tensor(
                        out=hf[a % 2][:], in0=s2[a][:], scalar=rcol,
                        op0=ALU.mult, in1=g2rep[:], op1=ALU.mult),
                        waits=[t_r, t_hf_cons[a % 2]], inc=True,
                        w=f"hf{a % 2}")
                    t_tr = None
                    for cc in range(CCN):
                        bank = 6 + (cc % 2)
                        t_tr = ten.emit(lambda e, a=a, cc=cc, bank=bank:
                                        e.transpose(
                            ps[bank][0:128, 0:128],
                            hf[a % 2][:, cc * 128:(cc + 1) * 128],
                            identf[:]),
                            waits=[t_h2, ps_free[bank], t_vn_mm],
                            inc=True)
                        t_h2Te = sca.emit(lambda e, a=a, cc=cc, bank=bank:
                                          e.activation(
                            out=htb[:, cc * 512 + a * 128:
                                    cc * 512 + (a + 1) * 128],
                            in_=ps[bank][0:128, 0:128],
                            func=AF.Identity),
                            waits=[t_tr], inc=True)
                        ps_free[bank] = t_h2Te
                    t_hf_cons[a % 2] = t_tr
                t_h2T_done = t_h2Te

                t_fT = [None] * MCF
                t_sil = None
                for mc in range(MCF):
                    wsl = mc % 2
                    t_w1 = rwq.load(gps, 16 + mc, lambda e, mc=mc,
                                    wsl=wsl: e.dma_start(
                        out=wfa[wsl][:]
                        .rearrange("p (a f) -> p a f", a=CCN),
                        in_=dt["w1_t"][mc]))
                    t_w2 = rwb.load(gps, mc, lambda e, mc=mc, wsl=wsl:
                                    e.dma_start(
                        out=wfb[wsl][:]
                        .rearrange("p (a f) -> p a f", a=CCN),
                        in_=dt["w2_t"][mc]))
                    pairs1 = [(wfa[wsl][:, cc * 128:(cc + 1) * 128],
                               htb[:, cc * 512:(cc + 1) * 512])
                              for cc in range(CCN)]
                    tmm1 = mm_group(ps[4][:], pairs1, [t_w1, t_h2T_done],
                                    4)
                    pairs2 = [(wfb[wsl][:, cc * 128:(cc + 1) * 128],
                               htb[:, cc * 512:(cc + 1) * 512])
                              for cc in range(CCN)]
                    tmm2 = mm_group(ps[5][:], pairs2, [t_w2], 5)
                    rwq.consumed(16 + mc, tmm1)
                    rwb.consumed(mc, tmm2)
                    t_sil = sca.emit(lambda e, mc=mc: e.activation(
                        out=silt[mc % 2][:], in_=ps[4][:], func=AF.Silu),
                        waits=[tmm1, t_fT[mc - 2] if mc >= 2 else None],
                        inc=True)
                    ps_free[4] = t_sil
                    t_fT[mc] = vec.emit(lambda e, mc=mc: e.tensor_mul(
                        fT[mc][:], silt[mc % 2][:], ps[5][:]),
                        waits=[tmm2, t_sil], inc=True, w=f"fT{mc}")
                    ps_free[5] = t_fT[mc]

                t_xffmm = None
                for mc in range(MCF):
                    t_w3 = rw3b.load(gps, mc, lambda e, mc=mc: e.dma_start(
                        out=w3s[mc % 2][:], in_=dt["w3_t"][mc]))
                    for a in range(4):
                        for half in range(2):
                            bank = a * 2 + half
                            w0 = []
                            if mc == 0:
                                w0 = [t_w3, t_fT[mc], ps_free[bank]]
                            elif a == 0 and half == 0:
                                w0 = [t_w3, t_fT[mc]]
                            t_xffmm = ten.emit(lambda e, mc=mc, a=a,
                                               half=half, bank=bank:
                                               e.matmul(
                                ps[bank][:],
                                fT[mc][:, a * 128:(a + 1) * 128],
                                w3s[mc % 2][:,
                                            half * 512:(half + 1) * 512],
                                start=(mc == 0), stop=(mc == MCF - 1)),
                                waits=w0, inc=(a == 3 and half == 1))
                    rw3b.consumed(mc, t_xffmm)

                t_delta = [None] * 4
                t_selout_cons = [None] * 2
                t_lnf_prev = [None] * 2
                for a in range(4):
                    t_preh = [None, None]
                    for half in range(2):
                        bank = a * 2 + half
                        vec.emit(lambda e, a=a, half=half, bank=bank:
                                 e.scalar_tensor_tensor(
                            out=fjunk[:, half * 512:(half + 1) * 512],
                            in0=ps[bank][:], scalar=1.0,
                            in1=s2[a][:, half * 512:(half + 1) * 512],
                            op0=ALU.mult, op1=ALU.add),
                            waits=[t_xffmm, t_selout_cons[a % 2]],
                            w=f"prh{half}")
                        t_preh[half] = vec.emit(
                            lambda e, a=a, half=half:
                            e.scalar_tensor_tensor(
                                out=selout[a % 2][:,
                                                  half * 512:
                                                  (half + 1) * 512],
                                in0=xslot[a][:,
                                             half * 512:(half + 1) * 512],
                                scalar=1.0,
                                in1=fjunk[:, half * 512:(half + 1) * 512],
                                op0=ALU.mult, op1=ALU.add),
                            inc=True, w=f"pre{a}{half}",
                            r=(f"prh{half}",))
                        ps_free[bank] = t_preh[half]
                    vec.emit(lambda e, a=a: e.bn_stats(
                        stats[:, 0:6], selout[a % 2][:, 0:512]),
                        waits=t_preh, w="fbnsa",
                        r=(f"pre{a}0", f"pre{a}1"))
                    vec.emit(lambda e, a=a: e.bn_stats(
                        stats[:, 6:12], selout[a % 2][:, 512:1024]),
                        w="fbnsb")
                    t_fbn = vec.emit(lambda e: e.bn_aggr(
                        mvt[:, 0:2], stats[:, 0:12]),
                        inc=True, w="fmv", r=("fbnsa", "fbnsb"))
                    rcol = lncol[:, a * 4:a * 4 + 1]
                    nmr = lncol[:, a * 4 + 1:a * 4 + 2]
                    lnv = lncol[:, a * 4 + 2:a * 4 + 3]
                    sca.emit(lambda e, lnv=lnv: e.activation(
                        out=lnv, in_=mvt[:, 1:2], func=AF.Ln,
                        bias=epsln[:]), waits=[t_fbn])
                    t_r = sca.emit(lambda e, rcol=rcol, lnv=lnv:
                                   e.activation(
                        out=rcol, in_=lnv, func=AF.Exp, scale=-0.5),
                        inc=True)
                    vec.emit(lambda e, nmr=nmr, rcol=rcol: e.tensor_mul(
                        nmr, mvt[:, 0:1], rcol),
                        waits=[t_r], w="fnm")
                    t_nmr = vec.emit(lambda e, nmr=nmr:
                                     e.tensor_scalar_mul(
                        nmr, nmr, -1.0), inc=True, w="fnm2", r=("fnm",))
                    t_lnf = sca.emit(lambda e, a=a, rcol=rcol, nmr=nmr:
                                     e.activation(
                        out=flnout[a % 2][:], in_=selout[a % 2][:],
                        func=AF.Identity, scale=rcol, bias=nmr),
                        waits=[t_nmr, t_lnf_prev[a % 2]], inc=True)
                    t_selout_cons[a % 2] = t_lnf
                    t_delta[a] = vec.emit(lambda e, a=a:
                                          e.scalar_tensor_tensor(
                        out=delta[a][:], in0=xslot[a][:],
                        scalar=nsscol[:, a:a + 1], op0=ALU.mult,
                        in1=flnout[a % 2][:], op1=ALU.add),
                        waits=[t_lnf, t_nss], inc=True, w=f"dl{a}")
                    t_lnf_prev[a % 2] = t_delta[a]
                    if dbg:
                        t_dd = syn.emit(lambda e, a=a, b=b: e.dma_start(
                            out=dt["dbg_selout"][b,
                                                 a * 128:(a + 1) * 128,
                                                 :],
                            in_=flnout[a % 2][:]), waits=[t_lnf],
                            inc=True, val=16)
                        t_lnf_prev[a % 2] = t_dd
                        t_ds = gps.emit(lambda e, a=a, b=b: e.dma_start(
                            out=dt["dbg_s2"][b, a * 128:(a + 1) * 128,
                                             :],
                            in_=s2[a][:]), waits=[t_s2[a]], inc=True,
                            val=16)
                        t_xf = vec.emit(lambda e, a=a: e.tensor_copy(
                            fjunk[:], xslot[a][:]),
                            waits=[t_xslot[a], t_preh[1]],
                            inc=True, w="dbgx", r=("prh1",))
                        t_dx = syn.emit(lambda e, a=a, b=b: e.dma_start(
                            out=dt["dbg_xslot"][b,
                                                a * 128:(a + 1) * 128,
                                                :],
                            in_=fjunk[:]), waits=[t_xf], inc=True,
                            val=16)

                t_sd_cons = [None, None]
                for c in range(32):
                    t_sx = rsx.load(sca, c + 32 * b,
                                    lambda e, c=c, b=b: e.dma_start(
                        out=sxt[c % 2][:],
                        in_=dt["x1_s"][b, c * 128:(c + 1) * 128, :]))
                    t_ev = None
                    for half in range(2):
                        bank = half
                        pairs = [(CA[a][:, c * 128:(c + 1) * 128],
                                  delta[a][:, half * 512:(half + 1) * 512])
                                 for a in range(4)]
                        tmm = mm_group(ps[bank][:], pairs,
                                       t_delta + t_CA +
                                       [t_sd_cons[c % 2]], bank)
                        vec.emit(lambda e, c=c, half=half:
                                 e.tensor_single_scalar(
                            uh[half][:],
                            sxt[c % 2][:, half * 512:(half + 1) * 512],
                            smat[:, c:c + 1], ALU.mult),
                            waits=[t_sx], w=f"uh{half}")
                        t_ev = vec.emit(lambda e, c=c, half=half,
                                        bank=bank:
                                        e.scalar_tensor_tensor(
                            out=sdt[c % 2][:, half * 512:(half + 1) * 512],
                            in0=ps[bank][:], scalar=1.0,
                            in1=uh[half][:], op0=ALU.mult, op1=ALU.add),
                            waits=[tmm], inc=True,
                            w=f"sd{c % 2}{half}", r=(f"uh{half}",))
                        ps_free[bank] = t_ev
                    rsx.consumed(c + 32 * b, t_ev)
                    t_st = syn.emit(lambda e, c=c, b=b: e.dma_start(
                        out=dt["out"][b, c * 128:(c + 1) * 128, :],
                        in_=sdt[c % 2][:]),
                        waits=[t_ev], inc=True, val=16,
                        sem=scat_sems[c % 2])
                    t_sd_cons[c % 2] = t_st

                P.barrier()

    block = ctx.enter_context(nc.Block())
    P.play(block)
    return nc, ctx



_CACHE = {}


def _prep_host(inputs):
    x = np.asarray(inputs["L0_attn_x"], dtype=np.float32)

    perm = np.empty(C, dtype=np.int64)
    perm[:512] = np.arange(512) * 2
    perm[512:] = np.arange(512) * 2 + 1

    def tile_w(w):
        return np.ascontiguousarray(w.reshape(CCN, 128, -1).transpose(1, 0, 2))

    wk = tile_w(np.asarray(inputs["L0_attn_Wk"], np.float32)[:, perm])
    wq = tile_w(np.asarray(inputs["L0_attn_Wq"], np.float32)[:, perm])
    wv = tile_w(np.asarray(inputs["L0_attn_Wv"], np.float32))
    wp = tile_w(np.asarray(inputs["L0_attn_Wp"], np.float32))
    bkcv = np.ascontiguousarray(
        np.asarray(inputs["L0_attn_bk"], np.float32)[perm].reshape(CCN, 128).T)
    bqcv = np.ascontiguousarray(
        np.asarray(inputs["L0_attn_bq"], np.float32)[perm].reshape(CCN, 128).T)

    wqkv = np.asarray(inputs["L0_ffn_W_qkv"], np.float32)
    wqkvT = wqkv.T
    wqkv_t = np.ascontiguousarray(
        wqkvT[:, :2048].reshape(CCN, 128, 16, 128).transpose(2, 1, 0, 3))
    wv2 = np.ascontiguousarray(
        wqkvT[:, 2048:].reshape(CCN, 128, C).transpose(1, 0, 2))
    wout_t = np.ascontiguousarray(
        np.asarray(inputs["L0_ffn_W_out"], np.float32).T
        .reshape(CCN, 128, C).transpose(1, 0, 2))

    w1p = np.zeros((DFFP, C), np.float32)
    w1p[:DFF] = np.asarray(inputs["L0_ffn_W1"], np.float32)
    w2p = np.zeros((DFFP, C), np.float32)
    w2p[:DFF] = np.asarray(inputs["L0_ffn_W2"], np.float32)
    w3p = np.zeros((C, DFFP), np.float32)
    w3p[:, :DFF] = np.asarray(inputs["L0_ffn_W3"], np.float32)
    w1_t = np.ascontiguousarray(
        w1p.reshape(MCF, 128, CCN, 128).transpose(0, 3, 2, 1))
    w2_t = np.ascontiguousarray(
        w2p.reshape(MCF, 128, CCN, 128).transpose(0, 3, 2, 1))
    w3_t = np.ascontiguousarray(w3p.T.reshape(MCF, 128, C))

    jj = np.arange(512, dtype=np.float64)
    tt = np.arange(T, dtype=np.float64)
    ang = np.outer(jj, (T - 1) - tt)
    cosT = np.cos(ang).astype(ml_dtypes.bfloat16)
    sinT = np.sin(ang).astype(ml_dtypes.bfloat16)

    def tile_cs(a):
        return np.ascontiguousarray(
            a.reshape(4, 128, NTB, TBS).transpose(2, 1, 0, 3))

    pidx = np.arange(128)[:, None]
    fidx = np.arange(TBS)[None, :]
    masks_np = np.stack([np.where(128 * a + pidx <= fidx, 0.0, NEG)
                         for a in range(4)], axis=1).astype(np.float32)
    tri_np = np.stack([(fidx < 128 * a + pidx).astype(np.float32)
                       for a in range(4)], axis=1).astype(np.float32)
    slotiota_np = np.broadcast_to(
        np.arange(512, dtype=np.float32)[None, :], (128, 512)).copy()
    iotacols_np = (128.0 * np.arange(4)[None, :] + pidx).astype(np.float32)

    shared = dict(
        wk=wk, wq=wq, wv=wv, wp=wp, bkc=bkcv, bqc=bqcv,
        bv_row=np.asarray(inputs["L0_attn_bv"], np.float32).reshape(1, C),
        bp_row=np.asarray(inputs["L0_attn_bp"], np.float32).reshape(1, C),
        wqkv_t=wqkv_t, wv2=wv2, wout_t=wout_t,
        w1_t=w1_t, w2_t=w2_t, w3_t=w3_t,
        g1_row=np.asarray(inputs["L0_ffn_g1"], np.float32).reshape(1, C),
        g2_row=np.asarray(inputs["L0_ffn_g2"], np.float32).reshape(1, C),
        wr_row=np.asarray(inputs["L0_ffn_w_router"],
                          np.float32).reshape(1, C),
        cos_t=tile_cs(cosT), sin_t=tile_cs(sinT),
        masks=masks_np, tri=tri_np,
        slotiota=slotiota_np, iotacols=iotacols_np,
        identf=np.eye(128, dtype=np.float32),
        ones_r_b=np.ones((1, 128), ml_dtypes.bfloat16),
        ones_r_f=np.ones((1, 128), np.float32),
        ones_c_b=np.ones((128, 1), ml_dtypes.bfloat16),
        ones_sq_f=np.ones((128, 128), np.float32),
    )

    in_maps = []
    for k in range(NCORES):
        xb = np.ascontiguousarray(x[BPC * k: BPC * (k + 1)])
        xtr = np.ascontiguousarray(xb.transpose(0, 2, 1))
        m = dict(shared)
        m["x_nat"] = xb
        m["x_tr"] = xtr
        in_maps.append(m)
    return in_maps


def kernel(**inputs):
    if "prog" not in _CACHE:
        _CACHE["prog"] = build_program(dbg=False)
    nc, _ctx = _CACHE["prog"]
    in_maps = _prep_host(inputs)
    res = run_bass_kernel_spmd(nc, in_maps, core_ids=list(range(NCORES)))
    out = np.concatenate([r["out"] for r in res.results], axis=0)
    return out.reshape(16, T, C)
